# version 02
# baseline (speedup 1.0000x reference)
import numpy as np
import ml_dtypes
import concourse.bass as bass
import concourse.mybir as mybir
from concourse.bass_utils import run_bass_kernel_spmd

F32 = mybir.dt.float32
BF16 = mybir.dt.bfloat16
U8 = mybir.dt.uint8
ALU = mybir.AluOpType
AF = mybir.ActivationFunctionType
BF16NP = ml_dtypes.bfloat16

SEQ = 4096
DM = 1024
SLAB = 512
NSLAB = SEQ // SLAB
NH_A = 16
HD = 64
DIN = 2048
NH_B = 32
DS = 64
CONVD = DIN + 2 * DS
DINP = 2 * DIN + 2 * DS + NH_B
EPS_LN = 1e-5
EPS_RMS = 1e-5

ENGINES = ("sync", "gpsimd", "scalar", "vector", "tensor")


class Ctr:
    def __init__(self, h):
        self.h = h
        self.n = 0

    def inc(self, k):
        self.n += k
        return self.n


class Tape:

    def __init__(self, nc):
        self.nc = nc
        self.streams = {e: [] for e in ENGINES}
        self.mute = False

    def emit(self, eng, fn, inc=None, k=1):
        if self.mute:
            return inc.n if inc is not None else None
        if inc is not None:
            t = inc.inc(k)
            self.streams[eng].append(lambda e, fn=fn, h=inc.h, k=k: fn(e).then_inc(h, k))
            return t
        self.streams[eng].append(fn)
        return None

    def wait(self, eng, ctr, val):
        if self.mute:
            return
        if val is None or val <= 0:
            return
        self.streams[eng].append(lambda e, h=ctr.h, v=val: e.wait_ge(h, v))

    def run_block(self, name=None):
        nc = self.nc
        with nc.Block(name) as blk:
            for ename in ENGINES:
                fns = self.streams[ename]
                if not fns:
                    continue
                dec = getattr(blk, ename)

                def body(eng, fns=fns):
                    for f in fns:
                        f(eng)

                dec(body)


class Sems:
    def __init__(self, nc, ctx):
        self.d_sync = Ctr(ctx.enter_context(nc.semaphore("d_sync")))
        self.d_gp = Ctr(ctx.enter_context(nc.semaphore("d_gp")))
        self.d_sca = Ctr(ctx.enter_context(nc.semaphore("d_sca")))
        self.c_pe = Ctr(ctx.enter_context(nc.semaphore("c_pe")))
        self.c_ve = Ctr(ctx.enter_context(nc.semaphore("c_ve")))
        self.c_act = Ctr(ctx.enter_context(nc.semaphore("c_act")))
        self.c_gp = Ctr(ctx.enter_context(nc.semaphore("c_gp")))

    def dma_ctr(self, eng):
        return {"sync": self.d_sync, "gpsimd": self.d_gp, "scalar": self.d_sca}[eng]


def dma(tape, eng, out_ap, in_ap, sems):
    ctr = sems.dma_ctr(eng)
    return tape.emit(eng, lambda e, o=out_ap, i=in_ap: e.dma_start(o, i), ctr, 16)


def _lp(nc, fn):
    with nc.allow_low_precision(reason="bf16 reciprocal feeds bf16 matmul broadcast"):
        return fn()


def carve(arena, off_bytes, shape, dtype):
    nel = int(np.prod(shape))
    nb = nel * mybir.dt.size(dtype)
    ap = arena[:, off_bytes:off_bytes + nb].bitcast(dtype)
    if len(shape) > 1:
        dims = " ".join(f"d{i}" for i in range(len(shape)))
        ap = ap.rearrange(f"p ({dims}) -> p {dims}",
                          **{f"d{i}": shape[i] for i in range(len(shape))})
    return ap



def ln_emit(tape, sems, src3, xn3, ln, nch=8, cols=SLAB, xn_off=0):
    sq, xc = ln["sq_bf"], ln["xc_bf"]
    t_sq = t_xc = None
    for c in range(nch):
        t_sq = tape.emit("scalar", lambda e, c=c: e.activation(
            sq[:, c, 0:cols], src3[:, c, 0:cols], AF.Square), sems.c_act)
        t_xc = tape.emit("vector", lambda e, c=c: e.tensor_copy(
            xc[:, c, 0:cols], src3[:, c, 0:cols]), sems.c_ve)
    tape.wait("tensor", sems.c_act, t_sq)
    tape.wait("tensor", sems.c_ve, t_xc)
    t_mm = None
    for c in range(nch):
        tape.emit("tensor", lambda e, c=c: e.matmul(
            ln["ps_s"][:, 0:cols], ln["ones_col"], xc[:, c, 0:cols],
            start=(c == 0), stop=(c == nch - 1)), None)
        t_mm = tape.emit("tensor", lambda e, c=c: e.matmul(
            ln["ps_q"][:, 0:cols], ln["ones_col"], sq[:, c, 0:cols],
            start=(c == 0), stop=(c == nch - 1)), sems.c_pe)
    inv = 1.0 / (nch * 128)
    tape.wait("vector", sems.c_pe, t_mm)
    tape.emit("vector", lambda e: e.tensor_scalar(
        ln["m_row"][:, 0:cols], ln["ps_s"][:, 0:cols], inv, None, ALU.mult), None)
    tape.emit("vector", lambda e: e.tensor_scalar(
        ln["a_row"][:, 0:cols], ln["ps_q"][:, 0:cols], inv, EPS_LN, ALU.mult, ALU.add), None)
    tape.emit("vector", lambda e: e.tensor_tensor(
        ln["msq"][:, 0:cols], ln["m_row"][:, 0:cols], ln["m_row"][:, 0:cols], ALU.mult), None)
    t_v = tape.emit("vector", lambda e: e.tensor_tensor(
        ln["v_row"][:, 0:cols], ln["a_row"][:, 0:cols], ln["msq"][:, 0:cols],
        ALU.subtract), sems.c_ve)
    tape.wait("scalar", sems.c_ve, t_v)
    tape.emit("scalar", lambda e: e.activation(
        ln["lnr"][:, 0:cols], ln["v_row"][:, 0:cols], AF.Ln), None)
    t_r = tape.emit("scalar", lambda e: e.activation(
        ln["rstd_row"][:, 0:cols], ln["lnr"][:, 0:cols], AF.Exp, scale=-0.5), sems.c_act)
    tape.wait("vector", sems.c_act, t_r)
    tape.emit("vector", lambda e: e.tensor_copy(
        ln["m_bf"][:, 0:cols], ln["m_row"][:, 0:cols]), None)
    t_cast = tape.emit("vector", lambda e: e.tensor_copy(
        ln["rstd_bf"][:, 0:cols], ln["rstd_row"][:, 0:cols]), sems.c_ve)
    tape.wait("tensor", sems.c_ve, t_cast)
    tape.emit("tensor", lambda e: e.matmul(
        ln["ps_mb"][:, 0:cols], ln["ones_row"], ln["m_bf"][:, 0:cols],
        start=True, stop=True), None)
    t_b = tape.emit("tensor", lambda e: e.matmul(
        ln["ps_rb"][:, 0:cols], ln["ones_row"], ln["rstd_bf"][:, 0:cols],
        start=True, stop=True), sems.c_pe)
    tape.wait("vector", sems.c_pe, t_b)
    t_ve = None
    for c in range(nch):
        tape.emit("vector", lambda e, c=c: e.tensor_tensor(
            ln["tmp32"][:, 0:cols], src3[:, c, 0:cols], ln["ps_mb"][:, 0:cols],
            ALU.subtract), None)
        t_ve = tape.emit("vector", lambda e, c=c: e.tensor_tensor(
            xn3[:, c, xn_off:xn_off + cols], ln["tmp32"][:, 0:cols],
            ln["ps_rb"][:, 0:cols], ALU.mult), sems.c_ve)
    return {"ve": t_ve, "sq": t_sq}


def make_ln_tiles(nc, ctx, pfx, nch=8, cols=SLAB):
    sq = ctx.enter_context(nc.sbuf_tensor(pfx + "_sq", [128, nch, cols], BF16))
    xc = ctx.enter_context(nc.sbuf_tensor(pfx + "_xc", [128, nch, cols], BF16))
    rows = ctx.enter_context(nc.sbuf_tensor(pfx + "_rows", [1, 6, cols], F32))
    rbf = ctx.enter_context(nc.sbuf_tensor(pfx + "_rbf", [1, 2, cols], BF16))
    tmp32 = ctx.enter_context(nc.sbuf_tensor(pfx + "_tmp32", [128, cols], F32))
    oc = ctx.enter_context(nc.sbuf_tensor(pfx + "_onc", [128, 1], BF16))
    orow = ctx.enter_context(nc.sbuf_tensor(pfx + "_onr", [1, 128], BF16))
    ps_s = ctx.enter_context(nc.psum_tensor(pfx + "_pss", [1, cols], F32))
    ps_q = ctx.enter_context(nc.psum_tensor(pfx + "_psq", [1, cols], F32))
    ps_mb = ctx.enter_context(nc.psum_tensor(pfx + "_psmb", [128, cols], F32))
    ps_rb = ctx.enter_context(nc.psum_tensor(pfx + "_psrb", [128, cols], F32))
    return dict(sq_bf=sq, xc_bf=xc,
                m_row=rows[:, 0, :], a_row=rows[:, 1, :], msq=rows[:, 2, :],
                v_row=rows[:, 3, :], lnr=rows[:, 4, :], rstd_row=rows[:, 5, :],
                m_bf=rbf[:, 0, :], rstd_bf=rbf[:, 1, :], tmp32=tmp32,
                ones_col=oc[:, 0:1], ones_row=orow[0:1, :],
                ps_s=ps_s, ps_q=ps_q, ps_mb=ps_mb, ps_rb=ps_rb,
                ones_col_t=oc, ones_row_t=orow)


def load_ln_consts(tape, sems, ln, dram):
    t = dma(tape, "sync", ln["ones_col_t"][:, :], dram["ones_col"].ap(), sems)
    t = dma(tape, "sync", ln["ones_row_t"][:, :], dram["ones_row"].ap(), sems)
    return t



def phase_a1(nc, sems, dram):
    tape = Tape(nc)
    from contextlib import ExitStack
    with ExitStack() as ctx:
        xslab = ctx.enter_context(nc.sbuf_tensor("a1_xslab", [128, 8, SLAB], F32))
        xn = ctx.enter_context(nc.sbuf_tensor("a1_xn", [128, 8, SLAB], BF16))
        wbuf = ctx.enter_context(nc.sbuf_tensor("a1_wbuf", [128, 2, 8, 2560], BF16))
        bA = ctx.enter_context(nc.sbuf_tensor("a1_bA", [128, 40], F32))
        stage = ctx.enter_context(nc.sbuf_tensor("a1_stage", [128, 2, SLAB], BF16))
        ln = make_ln_tiles(nc, ctx, "a1")
        ps_p = [ctx.enter_context(nc.psum_tensor("a1_pp0", [128, SLAB], F32)),
                ctx.enter_context(nc.psum_tensor("a1_pp1", [128, SLAB], F32))]

        t_c = load_ln_consts(tape, sems, ln, dram)
        t_c = dma(tape, "sync", bA[:, :], dram["bA"].ap(), sems)

        spill_dst = []
        for j in range(40):
            if j < 8:
                spill_dst.append((dram["scr_vfm"], j * 128))
            elif j < 16:
                spill_dst.append((dram["scr_wq"], (j - 8) * 128))
            elif j < 24:
                spill_dst.append((dram["scr_wk"], (j - 16) * 128))
            elif j < 32:
                spill_dst.append((dram["scr_gq"], (j - 24) * 128))
            else:
                spill_dst.append((dram["scr_gk"], (j - 32) * 128))

        xT_r = dram["xT"].ap().rearrange("(c p) t -> p c t", p=128)
        wA_r = dram["wA"].ap().rearrange("(c p) j -> p c j", p=128)

        t_w = [None, None]
        for hlf in range(2):
            t_w[hlf] = dma(tape, "gpsimd", wbuf[:, hlf, :, :],
                           wA_r[:, :, hlf * 2560:(hlf + 1) * 2560], sems)
        t_spill = {}
        t_evac = {}
        prev_ln = None
        t_half_mm = [None, None]
        for s in range(NSLAB):
            if prev_ln is not None:
                tape.wait("sync", sems.c_ve, prev_ln["ve"])
                tape.wait("sync", sems.c_act, prev_ln["sq"])
            t_x = dma(tape, "sync", xslab[:, :, :], xT_r[:, :, s * SLAB:(s + 1) * SLAB], sems)
            tape.wait("scalar", sems.d_sync, t_x)
            tape.wait("vector", sems.d_sync, t_x)
            prev_ln = ln_emit(tape, sems, xslab, xn, ln)
            for j in range(40):
                hlf, jq = j // 20, j % 20
                ps = ps_p[j % 2]
                tape.wait("tensor", sems.d_gp, t_w[hlf])
                tape.wait("tensor", sems.c_ve, prev_ln["ve"])
                if (s, j) >= (0, 2):
                    pj = (s, j - 2) if j >= 2 else (s - 1, 38 + j)
                    tape.wait("tensor", sems.c_act, t_evac[pj])
                t_mm = None
                for c in range(8):
                    t_mm = tape.emit("tensor", lambda e, c=c, jq=jq, hlf=hlf, ps=ps: e.matmul(
                        ps[:, :], wbuf[:, hlf, c, jq * 128:(jq + 1) * 128],
                        xn[:, c, :], start=(c == 0), stop=(c == 7)),
                        sems.c_pe if c == 7 else None)
                t_half_mm[hlf] = t_mm
                tape.wait("scalar", sems.c_pe, t_mm)
                tape.wait("scalar", sems.d_sync, t_c)
                pj = None
                if j >= 2:
                    pj = (s, j - 2)
                elif s > 0:
                    pj = (s - 1, 38 + j)
                if pj is not None and pj in t_spill:
                    tape.wait("scalar", sems.d_sync, t_spill[pj])
                t_evac[(s, j)] = tape.emit("scalar", lambda e, j=j, ps=ps: e.activation(
                    stage[:, j % 2, :], ps[:, :], AF.Identity, bias=bA[:, j:j + 1]), sems.c_act)
                scr, row = spill_dst[j]
                tape.wait("sync", sems.c_act, t_evac[(s, j)])
                t_spill[(s, j)] = dma(
                    tape, "sync", scr.ap()[row:row + 128, s * SLAB:(s + 1) * SLAB],
                    stage[:, j % 2, :], sems)
        tape.wait("sync", sems.d_sync, sems.d_sync.n)
        tape.run_block("a1")



A2_PARTS = "all"


def phase_a2(nc, sems, dram):
    parts = A2_PARTS
    tape = Tape(nc)
    from contextlib import ExitStack
    with ExitStack() as ctx:
        vt = ctx.enter_context(nc.sbuf_tensor("a2_vt", [128, 32, NH_A * 65], BF16))
        att = ctx.enter_context(nc.sbuf_tensor("a2_att", [128, 8, SEQ], BF16))
        arena = ctx.enter_context(nc.sbuf_tensor("a2_arena", [128, 36864], U8))
        vch = carve(arena, 0, [8, 128], BF16)
        qp = carve(arena, 0, [SEQ], BF16)
        kp = carve(arena, 8192, [SEQ], BF16)
        wqp = carve(arena, 16384, [SEQ], BF16)
        wkp = carve(arena, 24576, [SEQ], BF16)
        woa = carve(arena, 0, [8, 1024], BF16)
        xres = carve(arena, 16384, [8, SLAB], F32)
        ostage = carve(arena, 32768, [2, SLAB], F32)

        ptile = ctx.enter_context(nc.sbuf_tensor("a2_p", [128, 2, 2, 1024], BF16))
        wout = ctx.enter_context(nc.sbuf_tensor("a2_wout", [64, 2, SEQ], BF16))
        gb = ctx.enter_context(nc.sbuf_tensor("a2_gb", [128, 32], F32))
        wb = ctx.enter_context(nc.sbuf_tensor("a2_wb", [128, 1], F32))
        rden = ctx.enter_context(nc.sbuf_tensor("a2_rden", [65, 2048], BF16))
        rd0 = ctx.enter_context(nc.sbuf_tensor("a2_rd0", [1, 2, 1024], BF16))
        pv32 = ctx.enter_context(nc.sbuf_tensor("a2_pv32", [64, 1024], F32))
        obf = ctx.enter_context(nc.sbuf_tensor("a2_obf", [64, 1024], BF16))
        ident = ctx.enter_context(nc.sbuf_tensor("a2_id", [128, 128], BF16))
        onesb = ctx.enter_context(nc.sbuf_tensor("a2_onesb", [128, 64], BF16))
        onr = ctx.enter_context(nc.sbuf_tensor("a2_onr", [1, 128], BF16))
        boa = ctx.enter_context(nc.sbuf_tensor("a2_boa", [128, 8], F32))
        psA = ctx.enter_context(nc.psum_tensor("a2_psA", [128, 1024], F32))
        psB = ctx.enter_context(nc.psum_tensor("a2_psB", [128, 1024], F32))
        ps_v = ctx.enter_context(nc.psum_tensor("a2_ps_v", [65, 1024], F32))
        ps_w = ctx.enter_context(nc.psum_tensor("a2_ps_w", [128, 512], F32))
        ps_vw = ctx.enter_context(nc.psum_tensor("a2_ps_vw", [65, 512], F32))

        maskw = ctx.enter_context(nc.sbuf_tensor("a2_mw", [128, 128], BF16))
        t_c = dma(tape, "sync", gb[:, :], dram["gbias"].ap(), sems)
        t_c = dma(tape, "sync", wb[:, :], dram["wbias"].ap(), sems)
        t_c = dma(tape, "sync", maskw[:, :], dram["maskw"].ap(), sems)
        t_c = dma(tape, "sync", ident[:, :], dram["ident"].ap(), sems)
        t_fill1 = tape.emit("gpsimd", lambda e: e.memset(onesb[:, :], 1.0), sems.c_gp)
        t_c = dma(tape, "sync", onr[:, :], dram["ones_row"].ap(), sems)

        t_fill = tape.emit("gpsimd", lambda e: e.memset(vt[:, :, :], 1.0), sems.c_gp)
        vfm_r = dram["scr_vfm"].ap().rearrange("(c p) t -> p c t", p=128)
        t_copy_bank = {0: None, 1: None}
        t_tr_last = None
        for tc in range(32):
            if t_tr_last is not None:
                tape.wait("sync", sems.c_pe, t_tr_last)
            t_vl = dma(tape, "sync", vch[:, :, :], vfm_r[:, :, tc * 128:(tc + 1) * 128], sems)
            tape.wait("tensor", sems.d_sync, t_vl)
            if tc == 0:
                tape.wait("tensor", sems.d_sync, t_c)
                tape.wait("tensor", sems.c_gp, t_fill)
            for fc in range(8):
                b = fc % 2
                psb = (psA if b == 0 else psB).bitcast(BF16)
                tape.wait("tensor", sems.c_ve, t_copy_bank[b])
                t_tr_last = tape.emit("tensor", lambda e, fc=fc, psb=psb: e.transpose(
                    psb[:, 0:128], vch[:, fc, :], ident[:, :]), sems.c_pe)
                tape.wait("vector", sems.c_pe, t_tr_last)
                h0, h1 = 2 * fc, 2 * fc + 1
                tape.emit("vector", lambda e, psb=psb, tc=tc, h0=h0: e.tensor_copy(
                    vt[:, tc, h0 * 65:h0 * 65 + 64], psb[:, 0:64]), None)
                t_copy_bank[b] = tape.emit("vector", lambda e, psb=psb, tc=tc, h1=h1: e.tensor_copy(
                    vt[:, tc, h1 * 65:h1 * 65 + 64], psb[:, 64:128]), sems.c_ve)
        t_vt_done = t_copy_bank[1]

        gq, gk = dram["scr_gq"].ap(), dram["scr_gk"].ap()
        wq_, wk_ = dram["scr_wq"].ap(), dram["scr_wk"].ap()
        t_sAfree = t_copy_bank[0]
        t_sBfree = t_copy_bank[1]
        t_vfree_act = None
        t_vfree_ve = None
        t_att_dma = None
        t_obf_dma = None
        t_exp = {}
        t_pv32_rd = None
        t_parity_pe = [t_tr_last, t_tr_last]
        t_ld = [None, None]
        for h in range(NH_A):
            fc = h // 2
            r0 = (h % 2) * 64
            rs = slice(r0, r0 + 64)
            tape.wait("sync", sems.c_pe, t_parity_pe[h % 2])
            dma(tape, "sync", qp[rs, :], gq[h * 64:h * 64 + 64, :], sems)
            dma(tape, "sync", kp[rs, :], gk[h * 64:h * 64 + 64, :], sems)
            dma(tape, "sync", wqp[rs, :], wq_[h * 64:h * 64 + 64, :], sems)
            t_ld[h % 2] = dma(tape, "sync", wkp[rs, :], wk_[h * 64:h * 64 + 64, :], sems)
            if h == 0:
                tape.wait("tensor", sems.c_gp, t_fill1)
                tape.wait("tensor", sems.c_ve, t_vt_done)

            tape.wait("tensor", sems.d_sync, t_ld[h % 2])
            t_wexp = None
            t_wpv = None
            t_wmask = None
            t_wout = None
            t_wpvc = None
            for wg in range(8):
                qw0 = wg * 512
                if t_wexp is not None:
                    tape.wait("tensor", sems.c_act, t_wexp)
                t_qkw = None
                for i in range(4):
                    d = wg * 4 + i
                    t_qkw = tape.emit("tensor", lambda e, d=d, i=i, rs=rs: e.matmul(
                        ps_w[:, i * 128:(i + 1) * 128],
                        wkp[rs, d * 128:(d + 1) * 128],
                        wqp[rs, d * 128:(d + 1) * 128],
                        start=(i == 0), stop=(i == 3)),
                        sems.c_pe if i == 3 else None)
                tape.wait("scalar", sems.c_pe, t_qkw)
                if t_wmask is not None:
                    tape.wait("scalar", sems.c_ve, t_wmask)
                t_wexp = tape.emit("scalar", lambda e: e.activation(
                    ptile[:, 1, 1, 0:512], ps_w[:, 0:512], AF.Exp,
                    bias=wb[:, 0:1], scale=0.125), sems.c_act)
                tape.wait("vector", sems.c_act, t_wexp)
                if t_wpv is not None:
                    tape.wait("vector", sems.c_pe, t_wpv)
                mwb = maskw[:, None, :].broadcast_to([128, 4, 128])
                t_wmask = tape.emit("vector", lambda e, mwb=mwb: e.tensor_tensor(
                    ptile[:, 1, 1, 0:512].rearrange("p (a b) -> p a b", a=4),
                    ptile[:, 1, 1, 0:512].rearrange("p (a b) -> p a b", a=4),
                    mwb, ALU.mult), sems.c_ve)
                tape.wait("tensor", sems.c_ve, t_wmask)
                if t_wpvc is not None:
                    tape.wait("tensor", sems.c_act, t_wpvc)
                    tape.wait("tensor", sems.c_ve, t_wrr)
                for i in range(4):
                    d = wg * 4 + i
                    t_wpv = tape.emit(
                        "tensor",
                        lambda e, d=d, h=h, i=i: e.matmul(
                            ps_vw[:, i * 128:i * 128 + 128],
                            vt[:, d, h * 65:h * 65 + 65],
                            ptile[:, 1, 1, i * 128:(i + 1) * 128],
                            start=(i == 0), stop=(i == 3)),
                        sems.c_pe if i == 3 else None)
                tape.wait("vector", sems.c_pe, t_wpv)
                t_wrr = tape.emit("vector", lambda e: _lp(nc, lambda: e.reciprocal(
                    rden[64:65, 0:512], ps_vw[64:65, 0:512])), sems.c_ve)
                tape.wait("sync", sems.c_ve, t_wrr)
                t_rdma = dma(tape, "sync", rd0[0:1, 0, 0:512], rden[64:65, 0:512], sems)
                tape.wait("tensor", sems.d_sync, t_rdma)
                if t_wout is not None:
                    tape.wait("tensor", sems.c_ve, t_wout)
                t_rbw = tape.emit("tensor", lambda e: e.matmul(
                    psB[0:64, 0:512], onr[0:1, 0:64], rd0[0:1, 0, 0:512],
                    start=True, stop=True), sems.c_pe)
                tape.wait("scalar", sems.c_pe, t_wpv)
                tape.wait("scalar", sems.c_ve, t_pv32_rd)
                t_wpvc = tape.emit("scalar", lambda e: e.activation(
                    pv32[:, 0:512], ps_vw[0:64, 0:512], AF.Copy), sems.c_act)
                tape.wait("vector", sems.c_act, t_wpvc)
                tape.wait("vector", sems.c_pe, t_rbw)
                t_wout = tape.emit("vector", lambda e, qw0=qw0: e.tensor_tensor(
                    wout[:, 0, qw0:qw0 + 512], pv32[:, 0:512], psB[0:64, 0:512],
                    ALU.mult), sems.c_ve)
                t_pv32_rd = t_wout
            t_wout_done = t_wout

            tape.wait("tensor", sems.c_ve, t_wout_done)
            tape.wait("tensor", sems.c_act, t_wpvc)
            if t_vfree_act is not None:
                tape.wait("tensor", sems.c_act, t_vfree_act)
                tape.wait("tensor", sems.c_ve, t_vfree_ve)
            for qb in range(4):
                q0 = qb * 1024
                for kc in range(32):
                    par = kc % 2
                    psS = psA if par == 0 else psB
                    if kc >= 2:
                        tape.wait("tensor", sems.c_act, t_exp.get(kc - 2))
                    t_qk = None
                    for nn in range(2):
                        t_qk = tape.emit("tensor", lambda e, kc=kc, q0=q0, nn=nn, rs=rs,
                                         psS=psS: e.matmul(
                            psS[:, nn * 512:(nn + 1) * 512],
                            kp[rs, kc * 128:(kc + 1) * 128],
                            qp[rs, q0 + nn * 512:q0 + (nn + 1) * 512],
                            start=True, stop=True),
                            sems.c_pe if nn == 1 else None)
                    tape.wait("scalar", sems.c_pe, t_qk)
                    t_exp[kc] = tape.emit("scalar", lambda e, kc=kc, par=par, psS=psS:
                                          e.activation(
                        ptile[:, 0, par, :], psS[:, :], AF.Exp, bias=gb[:, kc:kc + 1]),
                        sems.c_act)
                    tape.wait("tensor", sems.c_act, t_exp[kc])
                    for nn in range(2):
                        t_pvk = tape.emit("tensor", lambda e, kc=kc, par=par, nn=nn, h=h:
                                          e.matmul(
                            ps_v[:, nn * 512:(nn + 1) * 512],
                            vt[:, kc, h * 65:h * 65 + 65],
                            ptile[:, 0, par, nn * 512:(nn + 1) * 512],
                            start=(kc == 0), stop=(kc == 31)),
                            sems.c_pe if nn == 1 else None)
                t_lpv = t_pvk
                tape.wait("vector", sems.c_pe, t_lpv)
                t_rr = tape.emit("vector", lambda e: _lp(nc, lambda: e.reciprocal(
                    rden[64:65, 1024:2048], ps_v[64:65, :])), sems.c_ve)
                tape.wait("sync", sems.c_ve, t_rr)
                t_rdma = dma(tape, "sync", rd0[0:1, 1, :], rden[64:65, 1024:2048], sems)
                tape.wait("tensor", sems.d_sync, t_rdma)
                tape.wait("tensor", sems.c_act, t_exp[30])
                tape.wait("tensor", sems.c_act, t_exp[31])
                t_rb = None
                for nn in range(2):
                    psS = psA if nn == 0 else psB
                    t_rb = tape.emit("tensor", lambda e, nn=nn, psS=psS: e.matmul(
                        psS[0:64, 0:512], onr[0:1, 0:64],
                        rd0[0:1, 1, nn * 512:(nn + 1) * 512], start=True, stop=True),
                        sems.c_pe)
                tape.wait("scalar", sems.c_pe, t_lpv)
                tape.wait("scalar", sems.c_ve, t_pv32_rd)
                if t_obf_dma is not None:
                    tape.wait("scalar", sems.d_sync, t_obf_dma)
                t_pvc = tape.emit("scalar", lambda e: e.activation(
                    pv32[:, :], ps_v[0:64, :], AF.Copy), sems.c_act)
                tape.wait("vector", sems.c_act, t_pvc)
                tape.wait("vector", sems.c_pe, t_rb)
                if t_obf_dma is not None:
                    tape.wait("vector", sems.d_sync, t_obf_dma)
                tape.emit("vector", lambda e: e.tensor_tensor(
                    obf[:, 0:512], pv32[:, 0:512], psA[0:64, 0:512], ALU.mult), None)
                t_gout = tape.emit("vector", lambda e: e.tensor_tensor(
                    obf[:, 512:1024], pv32[:, 512:1024], psB[0:64, 0:512], ALU.mult),
                    sems.c_ve)
                t_pv32_rd = t_gout
                if h % 2 == 0:
                    t_aw = tape.emit("vector", lambda e, fc=fc, q0=q0: e.tensor_tensor(
                        att[0:64, fc, q0:q0 + 1024], obf[:, :],
                        wout[:, 0, q0:q0 + 1024], ALU.add), sems.c_ve)
                    t_obf_last_ve = t_aw
                else:
                    t_aw = tape.emit("vector", lambda e, q0=q0: e.tensor_tensor(
                        obf[:, :], obf[:, :], wout[:, 0, q0:q0 + 1024], ALU.add),
                        sems.c_ve)
                    tape.wait("sync", sems.c_ve, t_aw)
                    t_att_dma = dma(
                        tape, "sync", att[64:128, fc, q0:q0 + 1024], obf[:, :], sems)
                    t_obf_dma = t_att_dma
                    t_obf_last_ve = t_aw
                t_vfree_act = t_pvc
                t_vfree_ve = t_rr
                tape.wait("tensor", sems.c_ve, t_gout)
            t_parity_pe[h % 2] = t_pvk
            t_sfree_ve = t_obf_last_ve
        t_last_pair_pe = t_pvk

        if parts != "all":
            tape.wait("sync", sems.d_sync, sems.d_sync.n)
            tape.run_block("a2")
            return
        t_att_done = t_sfree_ve
        tape.wait("gpsimd", sems.c_pe, t_last_pair_pe)
        t_woa = dma(tape, "gpsimd", woa[:, :, :],
                    dram["w_out_a"].ap().rearrange("(c p) j -> p c j", p=128), sems)
        t_boa = dma(tape, "sync", boa[:, :], dram["b_out_a"].ap(), sems)
        xT_r = dram["xT"].ap().rearrange("(c p) t -> p c t", p=128)
        t_oev = {}
        t_ost = {}
        t_radd = {}
        for s in range(NSLAB):
            if s > 0:
                tape.wait("sync", sems.c_ve, t_radd[(s - 1, 7)])
            else:
                tape.wait("sync", sems.c_pe, t_last_pair_pe)
                tape.wait("sync", sems.c_ve, t_att_done)
            t_xr = dma(tape, "sync", xres[:, :, :], xT_r[:, :, s * SLAB:(s + 1) * SLAB], sems)
            for dmc in range(8):
                ps = psA if dmc % 2 == 0 else psB
                pj = (s, dmc - 2) if dmc >= 2 else ((s - 1, 6 + dmc) if s > 0 else None)
                tape.wait("tensor", sems.d_gp, t_woa)
                if s == 0 and dmc == 0:
                    tape.wait("tensor", sems.c_ve, t_att_done)
                    tape.wait("tensor", sems.c_act, t_vfree_act)
                    tape.wait("tensor", sems.d_sync, t_att_dma)
                if pj is not None:
                    tape.wait("tensor", sems.c_ve, t_oev[pj])
                t_mm = None
                for c in range(8):
                    t_mm = tape.emit("tensor", lambda e, c=c, dmc=dmc, s=s, ps=ps: e.matmul(
                        ps[:, 0:512], woa[:, c, dmc * 128:(dmc + 1) * 128],
                        att[:, c, s * SLAB:(s + 1) * SLAB], start=(c == 0), stop=(c == 7)),
                        sems.c_pe if c == 7 else None)
                tape.wait("vector", sems.c_pe, t_mm)
                tape.wait("vector", sems.d_sync, t_xr)
                tape.wait("vector", sems.d_sync, t_boa)
                if pj is not None and pj in t_ost:
                    tape.wait("vector", sems.d_sync, t_ost[pj])
                t_radd[(s, dmc)] = tape.emit(
                    "vector", lambda e, ps=ps, dmc=dmc: e.scalar_tensor_tensor(
                        ostage[:, dmc % 2, :], ps[:, 0:512], boa[:, dmc:dmc + 1],
                        xres[:, dmc, :], ALU.add, ALU.add), sems.c_ve)
                t_oev[(s, dmc)] = t_radd[(s, dmc)]
                tape.wait("sync", sems.c_ve, t_radd[(s, dmc)])
                t_ost[(s, dmc)] = dma(
                    tape, "sync",
                    dram["scr_R"].ap()[dmc * 128:(dmc + 1) * 128, s * SLAB:(s + 1) * SLAB],
                    ostage[:, dmc % 2, :], sems)
        tape.wait("sync", sems.d_sync, sems.d_sync.n)
        tape.run_block("a2")



def phase_ffn(nc, sems, dram, src, dst, pfx):
    tape = Tape(nc)
    from contextlib import ExitStack
    with ExitStack() as ctx:
        Rs = ctx.enter_context(nc.sbuf_tensor(pfx + "_R", [128, 8, SLAB], F32))
        xn = ctx.enter_context(nc.sbuf_tensor(pfx + "_xn", [128, 8, SLAB + 1], BF16))
        xr = ctx.enter_context(nc.sbuf_tensor(pfx + "_xr", [128, 8, SLAB], BF16))
        xk = ctx.enter_context(nc.sbuf_tensor(pfx + "_xk", [128, 8, SLAB], BF16))
        mixt = ctx.enter_context(nc.sbuf_tensor(pfx + "_mt", [128, SLAB], BF16))
        wF = ctx.enter_context(nc.sbuf_tensor(pfx + "_wF", [128, 8, 3072], BF16))
        muF = ctx.enter_context(nc.sbuf_tensor(pfx + "_mu", [128, 16], F32))
        omF = ctx.enter_context(nc.sbuf_tensor(pfx + "_om", [128, 16], F32))
        r32 = ctx.enter_context(nc.sbuf_tensor(pfx + "_r32", [128, 8, SLAB], F32))
        ka = ctx.enter_context(nc.sbuf_tensor(pfx + "_ka", [128, 8, SLAB], BF16))
        km = ctx.enter_context(nc.sbuf_tensor(pfx + "_km", [128, 2, SLAB], F32))
        vst = ctx.enter_context(nc.sbuf_tensor(pfx + "_vst", [128, 2, SLAB], F32))
        ost = ctx.enter_context(nc.sbuf_tensor(pfx + "_ost", [128, 2, SLAB], F32))
        ln = make_ln_tiles(nc, ctx, pfx)
        ps_p = [ctx.enter_context(nc.psum_tensor(pfx + "_pp0", [128, SLAB], F32)),
                ctx.enter_context(nc.psum_tensor(pfx + "_pp1", [128, SLAB], F32))]

        t_c = load_ln_consts(tape, sems, ln, dram)
        t_mu = dma(tape, "sync", muF[:, :], dram["muF"].ap(), sems)
        t_wf = dma(tape, "gpsimd", wF[:, :, :],
                   dram["wF"].ap().rearrange("(c p) j -> p c j", p=128), sems)
        tape.wait("vector", sems.d_sync, t_mu)
        t_om = tape.emit("vector", lambda e: e.tensor_scalar(
            omF[:, :], muF[:, :], -1.0, 1.0, ALU.mult, ALU.add), sems.c_ve)
        tape.emit("vector", lambda e: e.memset(xn[:, :, 0:1], 0.0), None)

        src_r = src.ap().rearrange("(c p) t -> p c t", p=128)
        dst_r = dst.ap().rearrange("(c p) t -> p c t", p=128)
        prev_ln = None
        t_mix_prev = None
        t_rkv_mm = None
        t_vadd = {}
        t_store = {}
        psg = [(None, None), (None, None)]
        for s in range(NSLAB):
            if prev_ln is not None:
                tape.wait("sync", sems.c_ve, prev_ln["ve"])
                tape.wait("sync", sems.c_act, prev_ln["sq"])
                tape.wait("sync", sems.c_ve, t_vadd[(s - 1, 7)])
            t_R = dma(tape, "sync", Rs[:, :, :], src_r[:, :, s * SLAB:(s + 1) * SLAB], sems)
            tape.wait("scalar", sems.d_sync, t_R)
            tape.wait("vector", sems.d_sync, t_R)
            if s > 0:
                tape.emit("vector", lambda e: e.tensor_copy(
                    xn[:, :, 0:1], xn[:, :, SLAB:SLAB + 1]), None)
            prev_ln = ln_emit(tape, sems, Rs, xn, ln, xn_off=1)
            tape.wait("vector", sems.c_pe, t_rkv_mm)
            t_mix = None
            for c in range(8):
                tape.emit("vector", lambda e, c=c: e.tensor_scalar(
                    mixt[:, :], xn[:, c, 1:SLAB + 1], muF[:, c:c + 1], None, ALU.mult), None)
                tape.emit("vector", lambda e, c=c: e.scalar_tensor_tensor(
                    xr[:, c, :], xn[:, c, 0:SLAB], omF[:, c:c + 1], mixt[:, :],
                    ALU.mult, ALU.add), None)
                tape.emit("vector", lambda e, c=c: e.tensor_scalar(
                    mixt[:, :], xn[:, c, 1:SLAB + 1], muF[:, 8 + c:9 + c], None, ALU.mult), None)
                t_mix = tape.emit("vector", lambda e, c=c: e.scalar_tensor_tensor(
                    xk[:, c, :], xn[:, c, 0:SLAB], omF[:, 8 + c:9 + c], mixt[:, :],
                    ALU.mult, ALU.add), sems.c_ve)
            tape.wait("tensor", sems.d_gp, t_wf)
            tape.wait("tensor", sems.c_ve, t_mix)
            for dj in range(8):
                ps = ps_p[dj % 2]
                g = psg[dj % 2]
                tape.wait("tensor", g[0], g[1]) if g[0] else None
                t_mm = None
                for c in range(8):
                    t_mm = tape.emit("tensor", lambda e, c=c, dj=dj, ps=ps: e.matmul(
                        ps[:, :], wF[:, c, dj * 128:(dj + 1) * 128], xr[:, c, :],
                        start=(c == 0), stop=(c == 7)), sems.c_pe if c == 7 else None)
                tape.wait("scalar", sems.c_pe, t_mm)
                t_ev = tape.emit("scalar", lambda e, dj=dj, ps=ps: e.activation(
                    r32[:, dj, :], ps[:, :], AF.Sigmoid), sems.c_act)
                psg[dj % 2] = (sems.c_act, t_ev)
            for dj in range(8):
                ps = ps_p[dj % 2]
                g = psg[dj % 2]
                tape.wait("tensor", g[0], g[1]) if g[0] else None
                t_mm = None
                for c in range(8):
                    t_mm = tape.emit("tensor", lambda e, c=c, dj=dj, ps=ps: e.matmul(
                        ps[:, :], wF[:, c, 1024 + dj * 128:1024 + (dj + 1) * 128], xk[:, c, :],
                        start=(c == 0), stop=(c == 7)), sems.c_pe if c == 7 else None)
                tape.wait("vector", sems.c_pe, t_mm)
                if dj >= 2:
                    tape.wait("vector", sems.c_act, t_kmr)
                t_km = tape.emit("vector", lambda e, dj=dj, ps=ps: e.tensor_scalar(
                    km[:, dj % 2, :], ps[:, :], 0.0, None, ALU.max), sems.c_ve)
                psg[dj % 2] = (sems.c_ve, t_km)
                tape.wait("scalar", sems.c_ve, t_km)
                t_kmr = tape.emit("scalar", lambda e, dj=dj: e.activation(
                    ka[:, dj, :], km[:, dj % 2, :], AF.Square), sems.c_act)
            t_ka_done = t_kmr
            tape.wait("tensor", sems.c_act, t_ka_done)
            for dj in range(8):
                ps = ps_p[dj % 2]
                g = psg[dj % 2]
                tape.wait("tensor", g[0], g[1]) if g[0] else None
                t_mm = None
                for c in range(8):
                    t_mm = tape.emit("tensor", lambda e, c=c, dj=dj, ps=ps: e.matmul(
                        ps[:, :], wF[:, c, 2048 + dj * 128:2048 + (dj + 1) * 128], ka[:, c, :],
                        start=(c == 0), stop=(c == 7)), sems.c_pe if c == 7 else None)
                t_rkv_mm = t_mm
                tape.wait("vector", sems.c_pe, t_mm)
                pj = (s, dj - 2) if dj >= 2 else ((s - 1, 6 + dj) if s > 0 else None)
                if pj is not None and pj in t_store:
                    tape.wait("vector", sems.d_sync, t_store[pj])
                tape.emit("vector", lambda e, dj=dj, ps=ps: e.tensor_tensor(
                    vst[:, dj % 2, :], r32[:, dj, :], ps[:, :], ALU.mult), None)
                psg[dj % 2] = (sems.c_ve, None)
                t_vadd[(s, dj)] = tape.emit("vector", lambda e, dj=dj: e.tensor_tensor(
                    ost[:, dj % 2, :], vst[:, dj % 2, :], Rs[:, dj, :], ALU.add), sems.c_ve)
                psg[dj % 2] = (sems.c_ve, t_vadd[(s, dj)])
                tape.wait("sync", sems.c_ve, t_vadd[(s, dj)])
                t_store[(s, dj)] = dma(
                    tape, "sync", dst_r[:, dj, s * SLAB:(s + 1) * SLAB],
                    ost[:, dj % 2, :], sems)
        tape.wait("sync", sems.d_sync, sems.d_sync.n)
        tape.run_block(pfx)



def phase_c1(nc, sems, dram):
    tape = Tape(nc)
    from contextlib import ExitStack
    with ExitStack() as ctx:
        Rs = ctx.enter_context(nc.sbuf_tensor("c1_R", [128, 8, SLAB], F32))
        u = ctx.enter_context(nc.sbuf_tensor("c1_u", [128, 8, SLAB], BF16))
        winT = ctx.enter_context(nc.sbuf_tensor("c1_w", [128, 8, DINP], BF16))
        xbc = ctx.enter_context(nc.sbuf_tensor("c1_xbc", [128, 17, SLAB + 3], F32))
        zst = ctx.enter_context(nc.sbuf_tensor("c1_zst", [128, 2, SLAB], BF16))
        cva = ctx.enter_context(nc.sbuf_tensor("c1_cva", [128, 2, SLAB], F32))
        cvb = ctx.enter_context(nc.sbuf_tensor("c1_cvb", [128, 2, SLAB], F32))
        xsil = ctx.enter_context(nc.sbuf_tensor("c1_xsil", [128, 2, SLAB], BF16))
        cw = ctx.enter_context(nc.sbuf_tensor("c1_cw", [128, 68], F32))
        cb = ctx.enter_context(nc.sbuf_tensor("c1_cb", [128, 17], F32))
        dtb = ctx.enter_context(nc.sbuf_tensor("c1_dtb", [32, 1], F32))
        alog = ctx.enter_context(nc.sbuf_tensor("c1_alog", [32, 1], F32))
        acol = ctx.enter_context(nc.sbuf_tensor("c1_acol", [32, 1], F32))
        drows = ctx.enter_context(nc.sbuf_tensor("c1_dr", [32, 8, SLAB], F32))
        dbf = ctx.enter_context(nc.sbuf_tensor("c1_dbf", [32, 3, SLAB], BF16))
        ahi32 = ctx.enter_context(nc.sbuf_tensor("c1_ahi32", [32, SLAB], F32))
        ln = make_ln_tiles(nc, ctx, "c1")
        ps_p = [ctx.enter_context(nc.psum_tensor("c1_pp0", [128, SLAB], F32)),
                ctx.enter_context(nc.psum_tensor("c1_pp1", [128, SLAB], F32))]

        t_c = load_ln_consts(tape, sems, ln, dram)
        t_c = dma(tape, "sync", cw[:, :], dram["convw"].ap(), sems)
        t_c = dma(tape, "sync", cb[:, :], dram["convb"].ap(), sems)
        t_c = dma(tape, "sync", dtb[:, :], dram["dtb"].ap(), sems)
        t_al = dma(tape, "sync", alog[:, :], dram["alog"].ap(), sems)
        t_wi = dma(tape, "gpsimd", winT[:, :, :],
                   dram["w_in"].ap().rearrange("(c p) j -> p c j", p=128), sems)
        tape.wait("scalar", sems.d_sync, t_al)
        t_ea = tape.emit("scalar", lambda e: e.activation(
            acol[:, :], alog[:, :], AF.Exp), sems.c_act)

        t_tail = tape.emit("vector", lambda e: e.memset(xbc[:, :, 0:3], 0.0), sems.c_ve)
        t_tap_gp = None

        src_r = dram["scr_R"].ap().rearrange("(c p) t -> p c t", p=128)
        prev_ln = None
        psg = [(None, None), (None, None)]
        t_zsp = {}
        t_conv_ve = None
        t_conv_gp = None
        t_xbc_ev = {}
        t_sil_sp = {}
        t_dspill_prev = None
        for s in range(NSLAB):
            if prev_ln is not None:
                tape.wait("sync", sems.c_ve, prev_ln["ve"])
                tape.wait("sync", sems.c_act, prev_ln["sq"])
            t_R = dma(tape, "sync", Rs[:, :, :], src_r[:, :, s * SLAB:(s + 1) * SLAB], sems)
            tape.wait("scalar", sems.d_sync, t_R)
            tape.wait("vector", sems.d_sync, t_R)
            prev_ln = ln_emit(tape, sems, Rs, u, ln)
            tape.wait("vector", sems.c_gp, t_tap_gp)
            if s > 0:
                t_tail = tape.emit("vector", lambda e: e.tensor_copy(
                    xbc[:, :, 0:3], xbc[:, :, SLAB:SLAB + 3]), sems.c_ve)
            tape.wait("tensor", sems.d_gp, t_wi)
            tape.wait("tensor", sems.c_ve, prev_ln["ve"])
            for j in range(34):
                ps = ps_p[j % 2]
                g = psg[j % 2]
                tape.wait("tensor", g[0], g[1]) if g[0] else None
                mwid = 128 if j < 33 else 32
                t_mm = None
                for c in range(8):
                    t_mm = tape.emit("tensor", lambda e, c=c, j=j, ps=ps, mwid=mwid: e.matmul(
                        ps[0:mwid, :], winT[:, c, j * 128:j * 128 + mwid], u[:, c, :],
                        start=(c == 0), stop=(c == 7)), sems.c_pe if c == 7 else None)
                if j < 16:
                    tape.wait("scalar", sems.c_pe, t_mm)
                    pj = (s, j - 2) if j >= 2 else ((s - 1, 14 + j) if s > 0 else None)
                    if pj is not None and pj in t_zsp:
                        tape.wait("scalar", sems.d_sync, t_zsp[pj])
                    t_ev = tape.emit("scalar", lambda e, j=j, ps=ps: e.activation(
                        zst[:, j % 2, :], ps[:, :], AF.Copy), sems.c_act)
                    psg[j % 2] = (sems.c_act, t_ev)
                    tape.wait("sync", sems.c_act, t_ev)
                    t_zsp[(s, j)] = dma(
                        tape, "sync",
                        dram["scr_z"].ap()[j * 128:(j + 1) * 128, s * SLAB:(s + 1) * SLAB],
                        zst[:, j % 2, :], sems)
                elif j < 33:
                    fc = j - 16
                    tape.wait("scalar", sems.c_pe, t_mm)
                    if fc == 0:
                        tape.wait("scalar", sems.c_ve, t_tail)
                    t_ev = tape.emit("scalar", lambda e, fc=fc, ps=ps: e.activation(
                        xbc[:, fc, 3:SLAB + 3], ps[:, :], AF.Copy), sems.c_act)
                    psg[j % 2] = (sems.c_act, t_ev)
                    t_xbc_ev[fc] = t_ev
                else:
                    tape.wait("scalar", sems.c_pe, t_mm)
                    tape.wait("scalar", sems.d_sync, t_c)
                    t_dtp = tape.emit("scalar", lambda e, ps=ps: e.activation(
                        drows[:, 0, :], ps[0:32, :], AF.Identity, bias=dtb[:, 0:1]), sems.c_act)
                    psg[j % 2] = (sems.c_act, t_dtp)
                    tape.wait("vector", sems.c_act, t_dtp)
                    t_ve1 = tape.emit("vector", lambda e: e.tensor_scalar(
                        drows[:, 4, :], drows[:, 0, :], 0.0, None, ALU.max), sems.c_ve)
                    tape.emit("scalar", lambda e: e.activation(
                        drows[:, 1, :], drows[:, 0, :], AF.Abs), None)
                    tape.wait("scalar", sems.c_ve, t_ve1)
                    tape.emit("scalar", lambda e: e.activation(
                        drows[:, 2, :], drows[:, 1, :], AF.Exp, scale=-1.0), None)
                    t_a1 = tape.emit("scalar", lambda e: e.activation(
                        drows[:, 3, :], drows[:, 2, :], AF.Ln, bias=1.0), sems.c_act)
                    tape.wait("vector", sems.c_act, t_a1)
                    t_dt = tape.emit("vector", lambda e: e.tensor_tensor(
                        drows[:, 5, :], drows[:, 4, :], drows[:, 3, :], ALU.add), sems.c_ve)
                    tape.wait("vector", sems.c_act, t_ea)
                    t_ad = tape.emit("vector", lambda e: e.tensor_scalar(
                        drows[:, 7, :], drows[:, 5, :], acol[:, 0:1], -1.0,
                        ALU.mult, ALU.mult), sems.c_ve)
                    t_sc = None
                    for cc in range(8):
                        t_sc = tape.emit("vector", lambda e, cc=cc: e.tensor_tensor_scan(
                            drows[:, 6, cc * 64:(cc + 1) * 64],
                            drows[:, 7, cc * 64:(cc + 1) * 64],
                            drows[:, 7, cc * 64:(cc + 1) * 64], 0.0,
                            ALU.add, ALU.bypass), sems.c_ve)
                    tape.emit("vector", lambda e: e.tensor_scalar(
                        drows[:, 1, :], drows[:, 5, :], 1e-30, None, ALU.max), None)
                    tape.emit("vector", lambda e: e.tensor_copy(
                        dbf[:, 0, :], drows[:, 6, :]), None)
                    t_hi = tape.emit("vector", lambda e: e.tensor_copy(
                        ahi32[:, :], dbf[:, 0, :]), sems.c_ve)
                    t_lo = tape.emit("vector", lambda e: e.tensor_tensor(
                        dbf[:, 1, :], drows[:, 6, :], ahi32[:, :], ALU.subtract), sems.c_ve)
                    tape.wait("scalar", sems.c_ve, t_lo)
                    t_lnd = tape.emit("scalar", lambda e: e.activation(
                        dbf[:, 2, :], drows[:, 1, :], AF.Ln), sems.c_act)
                    if t_dspill_prev is not None:
                        tape.wait("sync", sems.d_sync, t_dspill_prev)
                    tape.wait("sync", sems.c_ve, t_lo)
                    tape.wait("sync", sems.c_act, t_lnd)
                    sl = slice(s * SLAB, (s + 1) * SLAB)
                    dma(tape, "sync", dram["scr_dt"].ap()[:, sl], drows[:, 5, :], sems)
                    dma(tape, "sync", dram["scr_acum"].ap()[:, sl], drows[:, 6, :], sems)
                    dma(tape, "sync", dram["scr_ahi"].ap()[:, sl], dbf[:, 0, :], sems)
                    dma(tape, "sync", dram["scr_alo"].ap()[:, sl], dbf[:, 1, :], sems)
                    t_dspill_prev = dma(tape, "sync", dram["scr_lnd"].ap()[:, sl],
                                        dbf[:, 2, :], sems)
            tape.wait("vector", sems.d_sync, t_c)
            tape.wait("gpsimd", sems.d_sync, t_c)
            t_sil = {}
            for fc in range(17):
                eng = "vector"
                cv = cva if fc % 2 == 0 else cvb
                cctr = sems.c_ve
                pp = 0
                tape.wait(eng, sems.c_act, t_xbc_ev[fc])
                if fc >= 2:
                    tape.wait(eng, sems.c_act, t_sil.get(fc - 2))
                tape.emit(eng, lambda e, fc=fc, cv=cv, pp=pp: e.tensor_scalar(
                    cv[:, pp, :], xbc[:, fc, 0:SLAB], cw[:, 4 * fc:4 * fc + 1], None,
                    ALU.mult), None)
                t_cv = None
                for tap in (1, 2, 3):
                    t_cv = tape.emit(eng, lambda e, fc=fc, cv=cv, pp=pp, tap=tap:
                                     e.scalar_tensor_tensor(
                                         cv[:, pp, :], xbc[:, fc, tap:SLAB + tap],
                                         cw[:, 4 * fc + tap:4 * fc + tap + 1], cv[:, pp, :],
                                         ALU.mult, ALU.add),
                                     cctr if tap == 3 else None)

                tape.wait("scalar", cctr, t_cv)
                pj = fc - 2
                if pj >= 0 and pj in t_sil_sp:
                    tape.wait("scalar", sems.d_sync, t_sil_sp[pj])
                t_sil[fc] = tape.emit("scalar", lambda e, fc=fc, cv=cv, pp=pp: e.activation(
                    xsil[:, fc % 2, :], cv[:, pp, :], AF.Silu, bias=cb[:, fc:fc + 1]),
                    sems.c_act)
                tape.wait("sync", sems.c_act, t_sil[fc])
                if fc < 16:
                    dst = dram["scr_xs"].ap()[fc * 128:(fc + 1) * 128, s * SLAB:(s + 1) * SLAB]
                else:
                    dst = dram["scr_bc"].ap()[:, s * SLAB:(s + 1) * SLAB]
                t_sil_sp[fc] = dma(tape, "sync", dst, xsil[:, fc % 2, :], sems)
        tape.wait("sync", sems.d_sync, sems.d_sync.n)
        tape.run_block("c1")




C2_LEVEL = 99


def phase_c2(nc, sems, dram):
    tape = Tape(nc)

    def sect(lv):
        tape.mute = lv > C2_LEVEL
    from contextlib import ExitStack
    with ExitStack() as ctx:
        Rs = ctx.enter_context(nc.sbuf_tensor("c2_R", [128, 8, SLAB], F32))
        z16 = ctx.enter_context(nc.sbuf_tensor("c2_z", [128, 16, SLAB], BF16))
        xf16 = ctx.enter_context(nc.sbuf_tensor("c2_xf", [128, 16, SLAB], BF16))
        bfm0 = ctx.enter_context(nc.sbuf_tensor("c2_bf0", [64, SLAB], BF16))
        cfm0 = ctx.enter_context(nc.sbuf_tensor("c2_cf0", [64, SLAB], BF16))
        xtc = [ctx.enter_context(nc.sbuf_tensor(f"c2_xt{i}", [64, 16, 128], BF16))
               for i in range(2)]
        dts = ctx.enter_context(nc.sbuf_tensor("c2_dt", [32, SLAB], F32))
        ac32 = ctx.enter_context(nc.sbuf_tensor("c2_ac", [32, SLAB], F32))
        ahi = ctx.enter_context(nc.sbuf_tensor("c2_ahi", [32, SLAB], BF16))
        alo = ctx.enter_context(nc.sbuf_tensor("c2_alo", [32, SLAB], BF16))
        lnd = ctx.enter_context(nc.sbuf_tensor("c2_lnd", [32, SLAB], BF16))
        afh = ctx.enter_context(nc.sbuf_tensor("c2_afh", [1, 2, 2048], BF16))
        afl = ctx.enter_context(nc.sbuf_tensor("c2_afl", [1, 2, 2048], BF16))
        wob = ctx.enter_context(nc.sbuf_tensor("c2_wob", [128, 16, 1024], BF16))
        S = ctx.enter_context(nc.sbuf_tensor("c2_S", [64, 32, 64], F32))
        Sbf = ctx.enter_context(nc.sbuf_tensor("c2_Sbf", [64, 32, 64], BF16))
        eE = ctx.enter_context(nc.sbuf_tensor("c2_eE", [64, 1024], BF16))
        sdo = ctx.enter_context(nc.sbuf_tensor("c2_sdo", [64, 1024], BF16))
        Pss = ctx.enter_context(nc.sbuf_tensor("c2_P", [64, 1024], BF16))
        xw = ctx.enter_context(nc.sbuf_tensor("c2_xw", [64, 1024], BF16))
        t2 = ctx.enter_context(nc.sbuf_tensor("c2_t2", [64, 1024], F32))
        t3 = t2
        ybf = ctx.enter_context(nc.sbuf_tensor("c2_ybf", [64, 2, 1024], BF16))
        btok = ctx.enter_context(nc.sbuf_tensor("c2_btok", [64, 64], BF16))
        wdt = ctx.enter_context(nc.sbuf_tensor("c2_wd", [32, 2, 64], F32))
        wdbf = ctx.enter_context(nc.sbuf_tensor("c2_wdbf", [32, 64], BF16))
        ebc64 = ctx.enter_context(nc.sbuf_tensor("c2_ebc", [32, 64], BF16))
        eb32 = ctx.enter_context(nc.sbuf_tensor("c2_eb32", [32, 1], F32))
        ysl = ctx.enter_context(nc.sbuf_tensor("c2_ysl", [128, 16, SLAB], F32))
        y3 = z16
        sqst = ctx.enter_context(nc.sbuf_tensor("c2_sqst", [128, 2, SLAB], BF16))
        szst = ctx.enter_context(nc.sbuf_tensor("c2_szst", [128, 2, SLAB], BF16))
        rrow = ctx.enter_context(nc.sbuf_tensor("c2_rrow", [1, 2, SLAB], F32))
        rbf = ctx.enter_context(nc.sbuf_tensor("c2_rbf", [1, SLAB], BF16))
        ident = ctx.enter_context(nc.sbuf_tensor("c2_id", [128, 128], BF16))
        onr = ctx.enter_context(nc.sbuf_tensor("c2_onr", [1, 128], BF16))
        onc = ctx.enter_context(nc.sbuf_tensor("c2_onc", [128, 1], BF16))
        Ind = ctx.enter_context(nc.sbuf_tensor("c2_ind", [32, 2048], BF16))
        IndN = ctx.enter_context(nc.sbuf_tensor("c2_indn", [32, 2048], BF16))
        maskc = ctx.enter_context(nc.sbuf_tensor("c2_mask", [64, 2048], BF16))
        Drep = ctx.enter_context(nc.sbuf_tensor("c2_drep", [128, 16], F32))
        nrmw = ctx.enter_context(nc.sbuf_tensor("c2_nw", [128, 16], F32))
        ostage = ctx.enter_context(nc.sbuf_tensor("c2_ostg", [128, 2, SLAB], F32))
        PS = [ctx.enter_context(nc.psum_tensor(f"c2_ps{i}", [128, 1024], F32))
              for i in range(4)]
        psYT = PS[2].bitcast(BF16)

        t_c = dma(tape, "sync", ident[:, :], dram["ident"].ap(), sems)
        t_c = dma(tape, "sync", onr[:, :], dram["ones_row"].ap(), sems)
        t_c = dma(tape, "sync", onc[:, :], dram["ones_col"].ap(), sems)
        t_c = dma(tape, "sync", Ind[:, :], dram["Ind"].ap(), sems)
        t_c = dma(tape, "sync", IndN[:, :], dram["IndN"].ap(), sems)
        t_c = dma(tape, "sync", maskc[:, :], dram["maskc"].ap(), sems)
        t_c = dma(tape, "sync", Drep[:, :], dram["Drep"].ap(), sems)
        t_c = dma(tape, "sync", nrmw[:, :], dram["normw"].ap(), sems)
        t_wo = dma(tape, "gpsimd", wob[:, :, :],
                   dram["w_out_b"].ap().rearrange("(c p) j -> p c j", p=128), sems)
        tape.emit("vector", lambda e: e.memset(S[:, :, :], 0.0), None)
        t_S0 = tape.emit("vector", lambda e: e.memset(Sbf[:, :, :], 0.0), sems.c_ve)

        src_r = dram["scr_R"].ap().rearrange("(c p) t -> p c t", p=128)
        z_r = dram["scr_z"].ap().rearrange("(c p) t -> p c t", p=128)
        xs_r = dram["scr_xs"].ap().rearrange("(c p) t -> p c t", p=128)

        t_sfree = {i: (None, None) for i in range(4)}

        def psum_wait(eng, idx):
            ctr, val = t_sfree[idx]
            if ctr is not None:
                tape.wait(eng, ctr, val)

        t_xtc_free = [(None, None), (None, None)]
        t_radd = {}
        t_ost = {}
        t_y3_mm = None
        t_sq15 = None
        t_Snew = t_S0
        for s in range(NSLAB):
            sect(0)
            sl = slice(s * SLAB, (s + 1) * SLAB)
            if s > 0:
                tape.wait("sync", sems.c_ve, t_radd.get((s - 1, 7)))
                tape.wait("sync", sems.c_pe, t_y3_mm)
                tape.wait("sync", sems.c_act, t_sq15)
            t_z = dma(tape, "sync", z16[:, :, :], z_r[:, :, sl], sems)
            t_xf = dma(tape, "sync", xf16[:, :, :], xs_r[:, :, sl], sems)
            t_b0 = dma(tape, "sync", bfm0[:, :], dram["scr_bc"].ap()[0:64, sl], sems)
            t_c0 = dma(tape, "sync", cfm0[:, :], dram["scr_bc"].ap()[64:128, sl], sems)
            t_dt = dma(tape, "sync", dts[:, :], dram["scr_dt"].ap()[:, sl], sems)
            t_ac = dma(tape, "sync", ac32[:, :], dram["scr_acum"].ap()[:, sl], sems)
            t_hi = dma(tape, "sync", ahi[:, :], dram["scr_ahi"].ap()[:, sl], sems)
            t_lo = dma(tape, "sync", alo[:, :], dram["scr_alo"].ap()[:, sl], sems)
            t_ln = dma(tape, "sync", lnd[:, :], dram["scr_lnd"].ap()[:, sl], sems)
            t_R = dma(tape, "sync", Rs[:, :, :], src_r[:, :, sl], sems)
            tape.wait("tensor", sems.d_sync, t_ln)
            tape.wait("tensor", sems.d_sync, t_xf)
            if s == 0:
                tape.wait("tensor", sems.d_gp, t_wo)

            t_af_mm = {0: None, 1: None}
            for c in range(8):
                sect(0)
                par = c % 2
                c64 = slice(c * 64, (c + 1) * 64)
                if t_af_mm[par] is not None:
                    tape.wait("sync", sems.c_pe, t_af_mm[par])
                csl = slice(s * SLAB + c * 64, s * SLAB + (c + 1) * 64)
                dma(tape, "sync", afh[:, par, :].rearrange("p (h l) -> p h l", h=32),
                    dram["scr_ahi"].ap()[:, csl].unsqueeze(0), sems)
                t_af = dma(tape, "sync", afl[:, par, :].rearrange("p (h l) -> p h l", h=32),
                           dram["scr_alo"].ap()[:, csl].unsqueeze(0), sems)
                sect(1)
                psum_wait("tensor", 2)
                g0, g1 = t_xtc_free[par]
                if g0 is not None:
                    tape.wait("tensor", g0, g1)
                t_xtr = {}
                for grp in range(2):
                    if grp == 1:
                        tape.wait("tensor", sems.c_ve, t_xtr[0])
                    for i in range(8):
                        fcc = grp * 8 + i
                        t_tr = tape.emit("tensor", lambda e, fcc=fcc, i=i, c64=c64: e.transpose(
                            psYT[0:64, i * 128:(i + 1) * 128], xf16[:, fcc, c64],
                            ident[:, :]), sems.c_pe if i == 7 else None)
                    tape.wait("vector", sems.c_pe, t_tr)
                    t_xtr[grp] = tape.emit("vector", lambda e, grp=grp, par=par: e.tensor_copy(
                        xtc[par][:, grp * 8:(grp + 1) * 8, :],
                        psYT[0:64, 0:1024].rearrange("p (a b) -> p a b", a=8)), sems.c_ve)
                    t_sfree[2] = (sems.c_ve, t_xtr[grp])
                t_xt_done = t_xtr[1]
                sect(2)
                psum_wait("tensor", 0)
                tape.wait("tensor", sems.d_sync, t_b0)
                psEb = PS[0].bitcast(BF16)
                t_bt = tape.emit("tensor", lambda e, c64=c64, psEb=psEb: e.transpose(
                    psEb[0:64, 0:64], bfm0[:, c64], ident[0:64, 0:64]), sems.c_pe)
                tape.wait("vector", sems.c_pe, t_bt)
                t_btc = tape.emit("vector", lambda e, psEb=psEb: e.tensor_copy(
                    btok[:, :], psEb[0:64, 0:64]), sems.c_ve)
                t_sfree[0] = (sems.c_ve, t_btc)
                sect(2)
                tape.wait("vector", sems.d_sync, t_ac)
                tape.wait("vector", sems.d_sync, t_dt)
                t_d1 = tape.emit("vector", lambda e, c=c: e.tensor_scalar(
                    wdt[:, 0, :], ac32[:, c * 64:(c + 1) * 64],
                    ac32[:, c * 64 + 63:c * 64 + 64], None, ALU.subtract), sems.c_ve)
                tape.wait("scalar", sems.c_ve, t_d1)
                t_ew = tape.emit("scalar", lambda e: e.activation(
                    wdt[:, 1, :], wdt[:, 0, :], AF.Exp, scale=-1.0), sems.c_act)
                tape.wait("vector", sems.c_act, t_ew)
                t_wd = tape.emit("vector", lambda e, c=c: e.tensor_tensor(
                    wdbf[:, :], wdt[:, 1, :], dts[:, c * 64:(c + 1) * 64], ALU.mult),
                    sems.c_ve)
                sect(2)
                tape.wait("scalar", sems.d_sync, t_ac)
                t_eb = tape.emit("scalar", lambda e, c=c: e.activation(
                    eb32[:, :], ac32[:, c * 64 + 63:c * 64 + 64], AF.Exp), sems.c_act)
                tape.wait("vector", sems.c_act, t_eb)
                t_ebc = tape.emit("vector", lambda e: e.tensor_copy(
                    ebc64[:, :], eb32[:, 0:1].broadcast_to([32, 64])), sems.c_ve)
                sect(3)
                psum_wait("tensor", 1)
                tape.wait("tensor", sems.d_sync, t_c0)
                t_G = None
                for half in range(2):
                    crep = cfm0[:, c64][:, None, :].broadcast_to([64, 8, 64])
                    t_G = tape.emit("tensor", lambda e, crep=crep, c64=c64, half=half: e.matmul(
                        PS[1][0:64, half * 512:(half + 1) * 512], bfm0[:, c64], crep,
                        start=True, stop=True), sems.c_pe)

                for hg in range(2):
                    hoff0 = hg * 1024
                    sect(3)
                    psum_wait("tensor", 0)
                    tape.wait("tensor", sems.d_sync, t_ln)
                    t_E = None
                    for half in range(2):
                        hs = slice(half * 512, (half + 1) * 512)
                        hoff = hoff0 + half * 512
                        for oi, (lt, rt) in enumerate(((ahi, IndN), (alo, IndN), (lnd, Ind))):
                            tape.emit("tensor", lambda e, lt=lt, rt=rt, c=c, hs=hs, hoff=hoff,
                                      oi=oi: e.matmul(
                                PS[0][0:64, hs], lt[:, c * 64:(c + 1) * 64],
                                rt[:, hoff:hoff + 512],
                                start=(oi == 0), stop=False), None)
                        tape.emit("tensor", lambda e, hs=hs, hoff=hoff: e.matmul(
                            PS[0][0:64, hs], ident[0:64, 0:64], maskc[:, hoff:hoff + 512],
                            start=False, stop=False), None)
                        if half == 0 and hoff0 == 0:
                            tape.wait("tensor", sems.d_sync, t_af)
                        tape.emit("tensor", lambda e, hs=hs, par=par, hoff=hoff: e.matmul(
                            PS[0][0:64, hs], onr[0:1, 0:64],
                            afh[0:1, par, hoff:hoff + 512],
                            start=False, stop=False), None)
                        t_E = tape.emit("tensor", lambda e, hs=hs, par=par, hoff=hoff: e.matmul(
                            PS[0][0:64, hs], onr[0:1, 0:64],
                            afl[0:1, par, hoff:hoff + 512],
                            start=False, stop=True), sems.c_pe)
                        t_af_mm[par] = t_E
                    tape.wait("scalar", sems.c_pe, t_E)
                    t_eE = tape.emit("scalar", lambda e: e.activation(
                        eE[:, :], PS[0][0:64, :], AF.Exp), sems.c_act)
                    t_sfree[0] = (sems.c_act, t_eE)
                    sect(3)
                    psum_wait("tensor", 0)
                    t_D = None
                    for half in range(2):
                        hs = slice(half * 512, (half + 1) * 512)
                        hoff = hoff0 + half * 512
                        tape.emit("tensor", lambda e, c=c, hs=hs, hoff=hoff: e.matmul(
                            PS[0][0:64, hs], ahi[:, c * 64:(c + 1) * 64],
                            Ind[:, hoff:hoff + 512], start=True, stop=False), None)
                        t_D = tape.emit("tensor", lambda e, c=c, hs=hs, hoff=hoff: e.matmul(
                            PS[0][0:64, hs], alo[:, c * 64:(c + 1) * 64],
                            Ind[:, hoff:hoff + 512], start=False, stop=True), sems.c_pe)
                    tape.wait("scalar", sems.c_pe, t_D)
                    t_sdo = tape.emit("scalar", lambda e: e.activation(
                        sdo[:, :], PS[0][0:64, :], AF.Exp), sems.c_act)
                    t_sfree[0] = (sems.c_act, t_sdo)
                    sect(3)
                    tape.wait("vector", sems.c_pe, t_G)
                    tape.wait("vector", sems.c_act, t_eE)
                    t_P = tape.emit("vector", lambda e: e.tensor_tensor(
                        Pss[:, :], PS[1][0:64, 0:1024], eE[:, :], ALU.mult), sems.c_ve)
                    if hg == 1:
                        t_sfree[1] = (sems.c_ve, t_P)
                    sect(4)
                    psum_wait("tensor", 2)
                    tape.wait("tensor", sems.c_ve, t_P)
                    tape.wait("tensor", sems.c_ve, t_xt_done)
                    t_yd = None
                    for i in range(16):
                        h = hg * 16 + i
                        hfc, hcol = h // 2, (h % 2) * 64
                        t_yd = tape.emit("tensor", lambda e, i=i, hfc=hfc, hcol=hcol,
                                         par=par: e.matmul(
                            PS[2][0:64, i * 64:(i + 1) * 64], Pss[:, i * 64:(i + 1) * 64],
                            xtc[par][:, hfc, hcol:hcol + 64],
                            start=(i in (0, 8)), stop=(i == 15)),
                            sems.c_pe if i == 15 else None)
                    sect(4)
                    psum_wait("tensor", 3)
                    t_yo = None
                    for i in range(16):
                        h = hg * 16 + i
                        t_yo = tape.emit("tensor", lambda e, i=i, h=h, c64=c64: e.matmul(
                            PS[3][0:64, i * 64:(i + 1) * 64], cfm0[:, c64],
                            Sbf[:, h, :], start=(i in (0, 8)), stop=(i == 15)),
                            sems.c_pe if i == 15 else None)
                    tape.wait("vector", sems.c_pe, t_yo)
                    tape.wait("vector", sems.c_act, t_sdo)
                    t_t2 = tape.emit("vector", lambda e: e.tensor_tensor(
                        t2[:, :], PS[3][0:64, :], sdo[:, :], ALU.mult), sems.c_ve)
                    t_sfree[3] = (sems.c_ve, t_t2)
                    tape.wait("vector", sems.c_pe, t_yd)
                    t_y = tape.emit("vector", lambda e, hg=hg: e.tensor_tensor(
                        ybf[:, hg, :], PS[2][0:64, :], t2[:, :], ALU.add), sems.c_ve)
                    t_sfree[2] = (sems.c_ve, t_y)
                    sect(5)
                    psum_wait("tensor", 0)
                    tape.wait("tensor", sems.c_ve, t_wd)
                    t_wb = None
                    for half in range(2):
                        hs = slice(half * 512, (half + 1) * 512)
                        hoff = hoff0 + half * 512
                        t_wb = tape.emit("tensor", lambda e, hs=hs, hoff=hoff: e.matmul(
                            PS[0][0:64, hs], wdbf[:, :], Ind[:, hoff:hoff + 512],
                            start=True, stop=True), sems.c_pe)
                    tape.wait("vector", sems.c_pe, t_wb)
                    xts = xtc[par][:, hg * 8:(hg + 1) * 8, :].rearrange("p a b -> p (a b)")
                    t_xw = tape.emit("vector", lambda e, xts=xts: e.tensor_tensor(
                        xw[:, :], PS[0][0:64, :], xts, ALU.mult), sems.c_ve)
                    t_sfree[0] = (sems.c_ve, t_xw)
                    sect(5)
                    psum_wait("tensor", 0)
                    tape.wait("tensor", sems.c_ve, t_xw)
                    tape.wait("tensor", sems.c_ve, t_btc)
                    t_st = None
                    for half in range(2):
                        hs = slice(half * 512, (half + 1) * 512)
                        t_st = tape.emit("tensor", lambda e, hs=hs: e.matmul(
                            PS[0][0:64, hs], btok[:, :], xw[:, hs],
                            start=True, stop=True), sems.c_pe)
                    sect(5)
                    psum_wait("tensor", 3)
                    tape.wait("tensor", sems.c_ve, t_ebc)
                    t_eB = None
                    for half in range(2):
                        hs = slice(half * 512, (half + 1) * 512)
                        hoff = hoff0 + half * 512
                        t_eB = tape.emit("tensor", lambda e, hs=hs, hoff=hoff: e.matmul(
                            PS[3][0:64, hs], ebc64[:, :], Ind[:, hoff:hoff + 512],
                            start=True, stop=True), sems.c_pe)
                    sect(5)
                    tape.wait("vector", sems.c_pe, t_eB)
                    Sv = S[:, hg * 16:(hg + 1) * 16, :].rearrange("p a b -> p (a b)")
                    t_t3 = tape.emit("vector", lambda e, Sv=Sv: e.tensor_tensor(
                        t3[:, :], Sv, PS[3][0:64, :], ALU.mult), sems.c_ve)
                    t_sfree[3] = (sems.c_ve, t_t3)
                    tape.wait("vector", sems.c_pe, t_st)
                    t_Snew = tape.emit("vector", lambda e, Sv=Sv: e.tensor_tensor(
                        Sv, t3[:, :], PS[0][0:64, :], ALU.add), sems.c_ve)
                    t_sfree[0] = (sems.c_ve, t_Snew)
                    Sbv = Sbf[:, hg * 16:(hg + 1) * 16, :].rearrange("p a b -> p (a b)")
                    t_Snew = tape.emit("vector", lambda e, Sv=Sv, Sbv=Sbv: e.tensor_copy(
                        Sbv, Sv), sems.c_ve)
                t_xtc_free[par] = (sems.c_ve, t_Snew)

                sect(6)
                for hg in range(2):
                    psum_wait("tensor", 2)
                    t_tr = None
                    for i in range(8):
                        t_tr = tape.emit("tensor", lambda e, hg=hg, i=i: e.transpose(
                            psYT[:, i * 64:(i + 1) * 64], ybf[0:64, hg, i * 128:(i + 1) * 128],
                            ident[0:64, 0:64]), sems.c_pe if i == 7 else None)
                    tape.wait("vector", sems.c_pe, t_tr)
                    t_yev = tape.emit("vector", lambda e, hg=hg, c=c: e.tensor_copy(
                        ysl[:, hg * 8:(hg + 1) * 8, c * 64:(c + 1) * 64],
                        psYT[:, 0:512].rearrange("p (a b) -> p a b", a=8)), sems.c_ve)
                    t_sfree[2] = (sems.c_ve, t_yev)

            sect(7)
            tape.wait("vector", sems.d_sync, t_xf)
            tape.wait("vector", sems.d_sync, t_c)
            for fcc in range(16):
                tape.emit("vector", lambda e, fcc=fcc: e.scalar_tensor_tensor(
                    ysl[:, fcc, :], xf16[:, fcc, :], Drep[:, fcc:fcc + 1], ysl[:, fcc, :],
                    ALU.mult, ALU.add), None)
            tape.wait("scalar", sems.d_sync, t_z)
            t_g2 = {}
            for fcc in range(16):
                if fcc >= 2:
                    tape.wait("scalar", sems.c_ve, t_g2[fcc - 2])
                t_sza = tape.emit("scalar", lambda e, fcc=fcc: e.activation(
                    szst[:, fcc % 2, :], z16[:, fcc, :], AF.Silu), sems.c_act)
                tape.wait("vector", sems.c_act, t_sza)
                t_g2[fcc] = tape.emit("vector", lambda e, fcc=fcc: e.tensor_tensor(
                    ysl[:, fcc, :], ysl[:, fcc, :], szst[:, fcc % 2, :], ALU.mult),
                    sems.c_ve)
            psum_wait("tensor", 1)
            t_rmm = None
            t_sq15 = None
            for fcc in range(16):
                tape.wait("scalar", sems.c_ve, t_g2[fcc])
                if fcc >= 2:
                    tape.wait("scalar", sems.c_pe, t_rmm)
                t_sq15 = tape.emit("scalar", lambda e, fcc=fcc: e.activation(
                    sqst[:, fcc % 2, :], ysl[:, fcc, :], AF.Square), sems.c_act)
                tape.wait("tensor", sems.c_act, t_sq15)
                t_rmm = tape.emit("tensor", lambda e, fcc=fcc: e.matmul(
                    PS[1][0:1, 0:512], onc[:, 0:1], sqst[:, fcc % 2, :],
                    start=(fcc == 0), stop=(fcc == 15)), sems.c_pe)
            tape.wait("vector", sems.c_pe, t_rmm)
            t_r1 = tape.emit("vector", lambda e: e.tensor_scalar(
                rrow[:, 0, :], PS[1][0:1, 0:512], 1.0 / DIN, EPS_RMS, ALU.mult, ALU.add),
                sems.c_ve)
            tape.wait("scalar", sems.c_ve, t_r1)
            tape.emit("scalar", lambda e: e.activation(
                rrow[:, 1, :], rrow[:, 0, :], AF.Ln), None)
            t_rst = tape.emit("scalar", lambda e: e.activation(
                rrow[:, 0, :], rrow[:, 1, :], AF.Exp, scale=-0.5), sems.c_act)
            tape.wait("vector", sems.c_act, t_rst)
            t_rbfc = tape.emit("vector", lambda e: e.tensor_copy(rbf[:, :], rrow[:, 0, :]),
                               sems.c_ve)
            tape.wait("tensor", sems.c_ve, t_rbfc)
            t_rb = tape.emit("tensor", lambda e: e.matmul(
                PS[1][:, 0:512], onr[0:1, :], rbf[:, :], start=True, stop=True), sems.c_pe)
            tape.wait("vector", sems.c_pe, t_rb)
            t_y3 = None
            for fcc in range(16):
                t_y3 = tape.emit("vector", lambda e, fcc=fcc: e.scalar_tensor_tensor(
                    y3[:, fcc, :], ysl[:, fcc, :], nrmw[:, fcc:fcc + 1], PS[1][:, 0:512],
                    ALU.mult, ALU.mult), sems.c_ve)
            t_sfree[1] = (sems.c_ve, t_y3)
            for dmc in range(8):
                psi = 2 if dmc % 2 == 0 else 3
                psum_wait("tensor", psi)
                tape.wait("tensor", sems.c_ve, t_y3)
                t_mm = None
                for fcc in range(16):
                    t_mm = tape.emit("tensor", lambda e, fcc=fcc, dmc=dmc, psi=psi: e.matmul(
                        PS[psi][:, 0:512], wob[:, fcc, dmc * 128:(dmc + 1) * 128],
                        y3[:, fcc, :], start=(fcc == 0), stop=(fcc == 15)),
                        sems.c_pe if fcc == 15 else None)
                t_y3_mm = t_mm
                tape.wait("vector", sems.c_pe, t_mm)
                tape.wait("vector", sems.d_sync, t_R)
                pj = (s, dmc - 2) if dmc >= 2 else ((s - 1, 6 + dmc) if s > 0 else None)
                if pj is not None and pj in t_ost:
                    tape.wait("vector", sems.d_sync, t_ost[pj])
                t_radd[(s, dmc)] = tape.emit("vector", lambda e, dmc=dmc, psi=psi: e.tensor_tensor(
                    ostage[:, dmc % 2, :], PS[psi][:, 0:512], Rs[:, dmc, :], ALU.add),
                    sems.c_ve)
                t_sfree[psi] = (sems.c_ve, t_radd[(s, dmc)])
                tape.wait("sync", sems.c_ve, t_radd[(s, dmc)])
                t_ost[(s, dmc)] = dma(
                    tape, "sync",
                    dram["scr_R"].ap()[dmc * 128:(dmc + 1) * 128, sl],
                    ostage[:, dmc % 2, :], sems)
        sect(0)
        tape.wait("sync", sems.d_sync, sems.d_sync.n)
        tape.run_block("c2")



def phase_finish(nc, sems, dram):
    tape = Tape(nc)
    t = dma(tape, "sync", dram["out"].ap(), dram["scr_R"].ap(), sems)
    tape.wait("sync", sems.d_sync, t)
    tape.run_block("fin")


def build_nc(n_sub=4):
    from contextlib import ExitStack
    nc = bass.Bass("TRN2", target_bir_lowering=False)
    dram = {}
    dram["xT"] = nc.dram_tensor("xT", [DM, SEQ], F32, kind="ExternalInput")
    dram["wA"] = nc.dram_tensor("wA", [DM, 5120], F32, kind="ExternalInput")
    dram["bA"] = nc.dram_tensor("bA", [128, 40], F32, kind="ExternalInput")
    dram["w_out_a"] = nc.dram_tensor("w_out_a", [DM, DM], F32, kind="ExternalInput")
    dram["b_out_a"] = nc.dram_tensor("b_out_a", [128, 8], F32, kind="ExternalInput")
    dram["gbias"] = nc.dram_tensor("gbias", [128, 32], F32, kind="ExternalInput")
    dram["wbias"] = nc.dram_tensor("wbias", [128, 1], F32, kind="ExternalInput")
    dram["wF"] = nc.dram_tensor("wF", [DM, 3072], F32, kind="ExternalInput")
    dram["muF"] = nc.dram_tensor("muF", [128, 16], F32, kind="ExternalInput")
    dram["w_in"] = nc.dram_tensor("w_in", [DM, DINP], F32, kind="ExternalInput")
    dram["w_out_b"] = nc.dram_tensor("w_out_b", [DIN, DM], F32, kind="ExternalInput")
    dram["convw"] = nc.dram_tensor("convw", [128, 68], F32, kind="ExternalInput")
    dram["convb"] = nc.dram_tensor("convb", [128, 17], F32, kind="ExternalInput")
    dram["dtb"] = nc.dram_tensor("dtb", [32, 1], F32, kind="ExternalInput")
    dram["alog"] = nc.dram_tensor("alog", [32, 1], F32, kind="ExternalInput")
    dram["Drep"] = nc.dram_tensor("Drep", [128, 16], F32, kind="ExternalInput")
    dram["normw"] = nc.dram_tensor("normw", [128, 16], F32, kind="ExternalInput")
    dram["ident"] = nc.dram_tensor("ident", [128, 128], BF16, kind="ExternalInput")
    dram["ones_col"] = nc.dram_tensor("ones_col", [128, 1], BF16, kind="ExternalInput")
    dram["ones_row"] = nc.dram_tensor("ones_row", [1, 128], BF16, kind="ExternalInput")
    dram["Ind"] = nc.dram_tensor("Ind", [32, 2048], BF16, kind="ExternalInput")
    dram["IndN"] = nc.dram_tensor("IndN", [32, 2048], BF16, kind="ExternalInput")
    dram["maskc"] = nc.dram_tensor("maskc", [64, 2048], BF16, kind="ExternalInput")
    dram["maskw"] = nc.dram_tensor("maskw", [128, 128], BF16, kind="ExternalInput")
    dram["out"] = nc.dram_tensor("out", [DM, SEQ], F32, kind="ExternalOutput")
    for nm, shp, dt in [
        ("scr_R", [DM, SEQ], F32), ("scr_vfm", [DM, SEQ], BF16),
        ("scr_wq", [DM, SEQ], BF16), ("scr_wk", [DM, SEQ], BF16),
        ("scr_gq", [DM, SEQ], BF16), ("scr_gk", [DM, SEQ], BF16),
        ("scr_z", [DIN, SEQ], BF16), ("scr_xs", [DIN, SEQ], BF16),
        ("scr_bc", [128, SEQ], BF16), ("scr_dt", [32, SEQ], F32),
        ("scr_acum", [32, SEQ], F32), ("scr_ahi", [32, SEQ], BF16),
        ("scr_alo", [32, SEQ], BF16), ("scr_lnd", [32, SEQ], BF16),
    ]:
        dram[nm] = nc.dram_tensor(nm, shp, dt, kind="Internal")

    with ExitStack() as ctx:
        sems = Sems(nc, ctx)
        phase_a1(nc, sems, dram)
        if n_sub >= 1:
            phase_a2(nc, sems, dram)
        if n_sub >= 2:
            phase_ffn(nc, sems, dram, dram["scr_R"], dram["scr_R"], "b")
        if n_sub >= 2.5:
            phase_c1(nc, sems, dram)
        if n_sub >= 3:
            phase_c2(nc, sems, dram)
        if n_sub >= 4:
            phase_ffn(nc, sems, dram, dram["scr_R"], dram["out"], "d")
        else:
            phase_finish(nc, sems, dram)
    return nc


def host_inputs(inputs, core):
    f32 = np.float32
    x = np.ascontiguousarray(inputs["L0_attn_a_x"][core].T).astype(f32, copy=False)
    d = {"xT": x}
    d["wA"] = np.ascontiguousarray(np.concatenate(
        [inputs["L0_attn_a_w_v"], inputs["L0_attn_a_w_wqk"], inputs["L0_attn_a_w_qk"]],
        axis=1)).astype(f32, copy=False)
    bcat = np.concatenate([inputs["L0_attn_a_b_v"], inputs["L0_attn_a_b_wqk"],
                           inputs["L0_attn_a_b_qk"]])
    d["bA"] = np.ascontiguousarray(bcat.reshape(40, 128).T).astype(f32, copy=False)
    d["w_out_a"] = np.ascontiguousarray(inputs["L0_attn_a_w_out"]).astype(f32, copy=False)
    d["b_out_a"] = np.ascontiguousarray(
        inputs["L0_attn_a_b_out"].reshape(8, 128).T).astype(f32, copy=False)
    pe = np.asarray(inputs["L0_attn_a_pos_embedding"])
    g = np.empty(SEQ, dtype=f32)
    g[0] = pe[4094]
    g[1:] = pe[::-1]
    d["gbias"] = np.ascontiguousarray(g.reshape(32, 128).T).astype(f32, copy=False)
    wpe = np.asarray(inputs["L0_attn_a_w_pos_embedding"])
    wb = wpe[63::-1]
    d["wbias"] = np.ascontiguousarray(np.tile(wb, 2).reshape(128, 1)).astype(f32, copy=False)
    d["wF"] = np.ascontiguousarray(np.concatenate(
        [inputs["L0_ffn_W_r"].T, inputs["L0_ffn_W_k"].T, inputs["L0_ffn_W_v"].T],
        axis=1)).astype(f32, copy=False)
    mu = np.concatenate([inputs["L0_ffn_mu_r"], inputs["L0_ffn_mu_k"]])
    d["muF"] = np.ascontiguousarray(mu.reshape(16, 128).T).astype(f32, copy=False)
    d["w_in"] = np.ascontiguousarray(inputs["L0_attn_b_W_in"].T).astype(f32, copy=False)
    d["w_out_b"] = np.ascontiguousarray(inputs["L0_attn_b_W_out"].T).astype(f32, copy=False)
    cwf = np.asarray(inputs["L0_attn_b_conv_w"]).reshape(CONVD, 4)
    d["convw"] = np.ascontiguousarray(
        cwf.reshape(17, 128, 4).transpose(1, 0, 2).reshape(128, 68)).astype(f32, copy=False)
    d["convb"] = np.ascontiguousarray(
        np.asarray(inputs["L0_attn_b_conv_b"]).reshape(17, 128).T).astype(f32, copy=False)
    d["dtb"] = np.ascontiguousarray(
        np.asarray(inputs["L0_attn_b_dt_bias"]).reshape(32, 1)).astype(f32, copy=False)
    d["alog"] = np.ascontiguousarray(
        np.asarray(inputs["L0_attn_b_A_log"]).reshape(32, 1)).astype(f32, copy=False)
    d["Drep"] = np.ascontiguousarray(np.repeat(
        np.asarray(inputs["L0_attn_b_D"]), 64).reshape(16, 128).T).astype(f32, copy=False)
    d["normw"] = np.ascontiguousarray(
        np.asarray(inputs["L0_attn_b_norm_w"]).reshape(16, 128).T).astype(f32, copy=False)
    d["ident"] = np.eye(128, dtype=BF16NP)
    d["ones_col"] = np.ones((128, 1), dtype=BF16NP)
    d["ones_row"] = np.ones((1, 128), dtype=BF16NP)
    ind = np.repeat(np.eye(32, dtype=np.float32), 64, axis=1)
    d["Ind"] = ind.astype(BF16NP)
    d["IndN"] = (-ind).astype(BF16NP)
    ll = np.arange(64)
    mask = np.where(ll[None, :] < np.arange(64)[:, None], np.float32(-1e30), np.float32(0.0))
    d["maskc"] = np.ascontiguousarray(np.tile(mask, (1, 32))).astype(BF16NP)
    mw = np.zeros((128, 128), dtype=np.float32)
    mw[0:64, 0:64] = 1.0
    mw[64:128, 64:128] = 1.0
    d["maskw"] = mw.astype(BF16NP)
    return d


_NC_CACHE = {}


def kernel(_n_sub=4, **inputs):
    if _n_sub not in _NC_CACHE:
        _NC_CACHE[_n_sub] = build_nc(_n_sub)
    nc = _NC_CACHE[_n_sub]
    in_maps = [host_inputs(inputs, core) for core in range(8)]
    res = run_bass_kernel_spmd(nc, in_maps, core_ids=list(range(8)))
    out = np.stack([np.asarray(r["out"]).T for r in res.results], axis=0)
    return np.ascontiguousarray(out.astype(np.float32))


# version 03
# speedup vs baseline: 1.1996x; 1.1996x over previous
import numpy as np
import ml_dtypes
import concourse.bass as bass
import concourse.mybir as mybir
from concourse.bass_utils import run_bass_kernel_spmd

F32 = mybir.dt.float32
BF16 = mybir.dt.bfloat16
U8 = mybir.dt.uint8
ALU = mybir.AluOpType
AF = mybir.ActivationFunctionType
BF16NP = ml_dtypes.bfloat16

SEQ = 4096
DM = 1024
SLAB = 512
NSLAB = SEQ // SLAB
NH_A = 16
HD = 64
DIN = 2048
NH_B = 32
DS = 64
CONVD = DIN + 2 * DS
DINP = 2 * DIN + 2 * DS + NH_B
EPS_LN = 1e-5
EPS_RMS = 1e-5

ENGINES = ("sync", "gpsimd", "scalar", "vector", "tensor")


class Ctr:
    def __init__(self, h):
        self.h = h
        self.n = 0

    def inc(self, k):
        self.n += k
        return self.n


class Tape:

    def __init__(self, nc):
        self.nc = nc
        self.streams = {e: [] for e in ENGINES}
        self.mute = False

    def emit(self, eng, fn, inc=None, k=1):
        if self.mute:
            return inc.n if inc is not None else None
        if inc is not None:
            t = inc.inc(k)
            self.streams[eng].append(lambda e, fn=fn, h=inc.h, k=k: fn(e).then_inc(h, k))
            return t
        self.streams[eng].append(fn)
        return None

    def wait(self, eng, ctr, val):
        if self.mute:
            return
        if val is None or val <= 0:
            return
        self.streams[eng].append(lambda e, h=ctr.h, v=val: e.wait_ge(h, v))

    def run_block(self, name=None):
        nc = self.nc
        with nc.Block(name) as blk:
            for ename in ENGINES:
                fns = self.streams[ename]
                if not fns:
                    continue
                dec = getattr(blk, ename)

                def body(eng, fns=fns):
                    for f in fns:
                        f(eng)

                dec(body)


class Sems:
    def __init__(self, nc, ctx):
        self.d_sync = Ctr(ctx.enter_context(nc.semaphore("d_sync")))
        self.d_gp = Ctr(ctx.enter_context(nc.semaphore("d_gp")))
        self.d_sca = Ctr(ctx.enter_context(nc.semaphore("d_sca")))
        self.c_pe = Ctr(ctx.enter_context(nc.semaphore("c_pe")))
        self.c_ve = Ctr(ctx.enter_context(nc.semaphore("c_ve")))
        self.c_act = Ctr(ctx.enter_context(nc.semaphore("c_act")))
        self.c_gp = Ctr(ctx.enter_context(nc.semaphore("c_gp")))

    def dma_ctr(self, eng):
        return {"sync": self.d_sync, "gpsimd": self.d_gp, "scalar": self.d_sca}[eng]


def dma(tape, eng, out_ap, in_ap, sems):
    ctr = sems.dma_ctr(eng)
    return tape.emit(eng, lambda e, o=out_ap, i=in_ap: e.dma_start(o, i), ctr, 16)


def _lp(nc, fn):
    with nc.allow_low_precision(reason="bf16 reciprocal feeds bf16 matmul broadcast"):
        return fn()


def carve(arena, off_bytes, shape, dtype):
    nel = int(np.prod(shape))
    nb = nel * mybir.dt.size(dtype)
    ap = arena[:, off_bytes:off_bytes + nb].bitcast(dtype)
    if len(shape) > 1:
        dims = " ".join(f"d{i}" for i in range(len(shape)))
        ap = ap.rearrange(f"p ({dims}) -> p {dims}",
                          **{f"d{i}": shape[i] for i in range(len(shape))})
    return ap



def ln_emit(tape, sems, src3, xn3, ln, nch=8, cols=SLAB, xn_off=0):
    sq, xc = ln["sq_bf"], ln["xc_bf"]
    t_sq = t_xc = None
    for c in range(nch):
        t_sq = tape.emit("scalar", lambda e, c=c: e.activation(
            sq[:, c, 0:cols], src3[:, c, 0:cols], AF.Square), sems.c_act)
        t_xc = tape.emit("vector", lambda e, c=c: e.tensor_copy(
            xc[:, c, 0:cols], src3[:, c, 0:cols]), sems.c_ve)
    tape.wait("tensor", sems.c_act, t_sq)
    tape.wait("tensor", sems.c_ve, t_xc)
    t_mm = None
    for c in range(nch):
        tape.emit("tensor", lambda e, c=c: e.matmul(
            ln["ps_s"][:, 0:cols], ln["ones_col"], xc[:, c, 0:cols],
            start=(c == 0), stop=(c == nch - 1)), None)
        t_mm = tape.emit("tensor", lambda e, c=c: e.matmul(
            ln["ps_q"][:, 0:cols], ln["ones_col"], sq[:, c, 0:cols],
            start=(c == 0), stop=(c == nch - 1)), sems.c_pe)
    inv = 1.0 / (nch * 128)
    tape.wait("vector", sems.c_pe, t_mm)
    tape.emit("vector", lambda e: e.tensor_scalar(
        ln["m_row"][:, 0:cols], ln["ps_s"][:, 0:cols], inv, None, ALU.mult), None)
    tape.emit("vector", lambda e: e.tensor_scalar(
        ln["a_row"][:, 0:cols], ln["ps_q"][:, 0:cols], inv, EPS_LN, ALU.mult, ALU.add), None)
    tape.emit("vector", lambda e: e.tensor_tensor(
        ln["msq"][:, 0:cols], ln["m_row"][:, 0:cols], ln["m_row"][:, 0:cols], ALU.mult), None)
    t_v = tape.emit("vector", lambda e: e.tensor_tensor(
        ln["v_row"][:, 0:cols], ln["a_row"][:, 0:cols], ln["msq"][:, 0:cols],
        ALU.subtract), sems.c_ve)
    tape.wait("scalar", sems.c_ve, t_v)
    tape.emit("scalar", lambda e: e.activation(
        ln["lnr"][:, 0:cols], ln["v_row"][:, 0:cols], AF.Ln), None)
    t_r = tape.emit("scalar", lambda e: e.activation(
        ln["rstd_row"][:, 0:cols], ln["lnr"][:, 0:cols], AF.Exp, scale=-0.5), sems.c_act)
    tape.wait("vector", sems.c_act, t_r)
    tape.emit("vector", lambda e: e.tensor_copy(
        ln["m_bf"][:, 0:cols], ln["m_row"][:, 0:cols]), None)
    t_cast = tape.emit("vector", lambda e: e.tensor_copy(
        ln["rstd_bf"][:, 0:cols], ln["rstd_row"][:, 0:cols]), sems.c_ve)
    tape.wait("tensor", sems.c_ve, t_cast)
    tape.emit("tensor", lambda e: e.matmul(
        ln["ps_mb"][:, 0:cols], ln["ones_row"], ln["m_bf"][:, 0:cols],
        start=True, stop=True), None)
    t_b = tape.emit("tensor", lambda e: e.matmul(
        ln["ps_rb"][:, 0:cols], ln["ones_row"], ln["rstd_bf"][:, 0:cols],
        start=True, stop=True), sems.c_pe)
    tape.wait("vector", sems.c_pe, t_b)
    t_ve = None
    for c in range(nch):
        tape.emit("vector", lambda e, c=c: e.tensor_tensor(
            ln["tmp32"][:, 0:cols], src3[:, c, 0:cols], ln["ps_mb"][:, 0:cols],
            ALU.subtract), None)
        t_ve = tape.emit("vector", lambda e, c=c: e.tensor_tensor(
            xn3[:, c, xn_off:xn_off + cols], ln["tmp32"][:, 0:cols],
            ln["ps_rb"][:, 0:cols], ALU.mult), sems.c_ve)
    return {"ve": t_ve, "sq": t_sq}


def make_ln_tiles(nc, ctx, pfx, nch=8, cols=SLAB):
    sq = ctx.enter_context(nc.sbuf_tensor(pfx + "_sq", [128, nch, cols], BF16))
    xc = ctx.enter_context(nc.sbuf_tensor(pfx + "_xc", [128, nch, cols], BF16))
    rows = ctx.enter_context(nc.sbuf_tensor(pfx + "_rows", [1, 6, cols], F32))
    rbf = ctx.enter_context(nc.sbuf_tensor(pfx + "_rbf", [1, 2, cols], BF16))
    tmp32 = ctx.enter_context(nc.sbuf_tensor(pfx + "_tmp32", [128, cols], F32))
    oc = ctx.enter_context(nc.sbuf_tensor(pfx + "_onc", [128, 1], BF16))
    orow = ctx.enter_context(nc.sbuf_tensor(pfx + "_onr", [1, 128], BF16))
    ps_s = ctx.enter_context(nc.psum_tensor(pfx + "_pss", [1, cols], F32))
    ps_q = ctx.enter_context(nc.psum_tensor(pfx + "_psq", [1, cols], F32))
    ps_mb = ctx.enter_context(nc.psum_tensor(pfx + "_psmb", [128, cols], F32))
    ps_rb = ctx.enter_context(nc.psum_tensor(pfx + "_psrb", [128, cols], F32))
    return dict(sq_bf=sq, xc_bf=xc,
                m_row=rows[:, 0, :], a_row=rows[:, 1, :], msq=rows[:, 2, :],
                v_row=rows[:, 3, :], lnr=rows[:, 4, :], rstd_row=rows[:, 5, :],
                m_bf=rbf[:, 0, :], rstd_bf=rbf[:, 1, :], tmp32=tmp32,
                ones_col=oc[:, 0:1], ones_row=orow[0:1, :],
                ps_s=ps_s, ps_q=ps_q, ps_mb=ps_mb, ps_rb=ps_rb,
                ones_col_t=oc, ones_row_t=orow)


def load_ln_consts(tape, sems, ln, dram):
    t = dma(tape, "sync", ln["ones_col_t"][:, :], dram["ones_col"].ap(), sems)
    t = dma(tape, "sync", ln["ones_row_t"][:, :], dram["ones_row"].ap(), sems)
    return t



def phase_a1(nc, sems, dram):
    tape = Tape(nc)
    from contextlib import ExitStack
    with ExitStack() as ctx:
        xslab = ctx.enter_context(nc.sbuf_tensor("a1_xslab", [128, 8, SLAB], F32))
        xn = ctx.enter_context(nc.sbuf_tensor("a1_xn", [128, 8, SLAB], BF16))
        wbuf = ctx.enter_context(nc.sbuf_tensor("a1_wbuf", [128, 2, 8, 2560], BF16))
        bA = ctx.enter_context(nc.sbuf_tensor("a1_bA", [128, 40], F32))
        stage = ctx.enter_context(nc.sbuf_tensor("a1_stage", [128, 2, SLAB], BF16))
        ln = make_ln_tiles(nc, ctx, "a1")
        ps_p = [ctx.enter_context(nc.psum_tensor("a1_pp0", [128, SLAB], F32)),
                ctx.enter_context(nc.psum_tensor("a1_pp1", [128, SLAB], F32))]

        t_c = load_ln_consts(tape, sems, ln, dram)
        t_c = dma(tape, "sync", bA[:, :], dram["bA"].ap(), sems)

        spill_dst = []
        for j in range(40):
            if j < 8:
                spill_dst.append((dram["scr_vfm"], j * 128))
            elif j < 16:
                spill_dst.append((dram["scr_wq"], (j - 8) * 128))
            elif j < 24:
                spill_dst.append((dram["scr_wk"], (j - 16) * 128))
            elif j < 32:
                spill_dst.append((dram["scr_gq"], (j - 24) * 128))
            else:
                spill_dst.append((dram["scr_gk"], (j - 32) * 128))

        xT_r = dram["xT"].ap().rearrange("(c p) t -> p c t", p=128)
        wA_r = dram["wA"].ap().rearrange("(c p) j -> p c j", p=128)

        t_w = [None, None]
        for hlf in range(2):
            t_w[hlf] = dma(tape, "gpsimd", wbuf[:, hlf, :, :],
                           wA_r[:, :, hlf * 2560:(hlf + 1) * 2560], sems)
        t_spill = {}
        t_evac = {}
        prev_ln = None
        t_half_mm = [None, None]
        for s in range(NSLAB):
            if prev_ln is not None:
                tape.wait("sync", sems.c_ve, prev_ln["ve"])
                tape.wait("sync", sems.c_act, prev_ln["sq"])
            t_x = dma(tape, "sync", xslab[:, :, :], xT_r[:, :, s * SLAB:(s + 1) * SLAB], sems)
            tape.wait("scalar", sems.d_sync, t_x)
            tape.wait("vector", sems.d_sync, t_x)
            prev_ln = ln_emit(tape, sems, xslab, xn, ln)
            for j in range(40):
                hlf, jq = j // 20, j % 20
                ps = ps_p[j % 2]
                tape.wait("tensor", sems.d_gp, t_w[hlf])
                tape.wait("tensor", sems.c_ve, prev_ln["ve"])
                if (s, j) >= (0, 2):
                    pj = (s, j - 2) if j >= 2 else (s - 1, 38 + j)
                    tape.wait("tensor", sems.c_act, t_evac[pj])
                t_mm = None
                for c in range(8):
                    t_mm = tape.emit("tensor", lambda e, c=c, jq=jq, hlf=hlf, ps=ps: e.matmul(
                        ps[:, :], wbuf[:, hlf, c, jq * 128:(jq + 1) * 128],
                        xn[:, c, :], start=(c == 0), stop=(c == 7)),
                        sems.c_pe if c == 7 else None)
                t_half_mm[hlf] = t_mm
                tape.wait("scalar", sems.c_pe, t_mm)
                tape.wait("scalar", sems.d_sync, t_c)
                pj = None
                if j >= 2:
                    pj = (s, j - 2)
                elif s > 0:
                    pj = (s - 1, 38 + j)
                if pj is not None and pj in t_spill:
                    tape.wait("scalar", sems.d_sync, t_spill[pj])
                t_evac[(s, j)] = tape.emit("scalar", lambda e, j=j, ps=ps: e.activation(
                    stage[:, j % 2, :], ps[:, :], AF.Identity, bias=bA[:, j:j + 1]), sems.c_act)
                scr, row = spill_dst[j]
                tape.wait("sync", sems.c_act, t_evac[(s, j)])
                t_spill[(s, j)] = dma(
                    tape, "sync", scr.ap()[row:row + 128, s * SLAB:(s + 1) * SLAB],
                    stage[:, j % 2, :], sems)
        tape.wait("sync", sems.d_sync, sems.d_sync.n)
        tape.run_block("a1")



A2_PARTS = "all"


def phase_a2(nc, sems, dram):
    parts = A2_PARTS
    tape = Tape(nc)
    from contextlib import ExitStack
    with ExitStack() as ctx:
        vt = ctx.enter_context(nc.sbuf_tensor("a2_vt", [128, 32, NH_A * 65], BF16))
        att = ctx.enter_context(nc.sbuf_tensor("a2_att", [128, 8, SEQ], BF16))
        arena = ctx.enter_context(nc.sbuf_tensor("a2_arena", [128, 36864], U8))
        vch = carve(arena, 0, [8, 128], BF16)
        qp = carve(arena, 0, [SEQ], BF16)
        kp = carve(arena, 8192, [SEQ], BF16)
        wqp = carve(arena, 16384, [SEQ], BF16)
        wkp = carve(arena, 24576, [SEQ], BF16)
        woa = carve(arena, 0, [8, 1024], BF16)
        xres = carve(arena, 16384, [8, SLAB], F32)
        ostage = carve(arena, 32768, [2, SLAB], F32)

        ptile = ctx.enter_context(nc.sbuf_tensor("a2_p", [128, 2, 2, 1024], BF16))
        wout = ctx.enter_context(nc.sbuf_tensor("a2_wout", [64, 2, SEQ], BF16))
        gb = ctx.enter_context(nc.sbuf_tensor("a2_gb", [128, 32], F32))
        wb = ctx.enter_context(nc.sbuf_tensor("a2_wb", [128, 1], F32))
        rden = ctx.enter_context(nc.sbuf_tensor("a2_rden", [65, 2048], BF16))
        rd0 = ctx.enter_context(nc.sbuf_tensor("a2_rd0", [1, 2, 1024], BF16))
        pv32 = ctx.enter_context(nc.sbuf_tensor("a2_pv32", [64, 1024], F32))
        obf = ctx.enter_context(nc.sbuf_tensor("a2_obf", [64, 1024], BF16))
        ident = ctx.enter_context(nc.sbuf_tensor("a2_id", [128, 128], BF16))
        onesb = ctx.enter_context(nc.sbuf_tensor("a2_onesb", [128, 64], BF16))
        onr = ctx.enter_context(nc.sbuf_tensor("a2_onr", [1, 128], BF16))
        boa = ctx.enter_context(nc.sbuf_tensor("a2_boa", [128, 8], F32))
        psA = ctx.enter_context(nc.psum_tensor("a2_psA", [128, 1024], F32))
        psB = ctx.enter_context(nc.psum_tensor("a2_psB", [128, 1024], F32))
        ps_v = ctx.enter_context(nc.psum_tensor("a2_ps_v", [65, 1024], F32))
        ps_w = ctx.enter_context(nc.psum_tensor("a2_ps_w", [128, 512], F32))
        ps_vw = ctx.enter_context(nc.psum_tensor("a2_ps_vw", [65, 512], F32))

        maskw = ctx.enter_context(nc.sbuf_tensor("a2_mw", [128, 128], BF16))
        t_c = dma(tape, "sync", gb[:, :], dram["gbias"].ap(), sems)
        t_c = dma(tape, "sync", wb[:, :], dram["wbias"].ap(), sems)
        t_c = dma(tape, "sync", maskw[:, :], dram["maskw"].ap(), sems)
        t_c = dma(tape, "sync", ident[:, :], dram["ident"].ap(), sems)
        t_fill1 = tape.emit("gpsimd", lambda e: e.memset(onesb[:, :], 1.0), sems.c_gp)
        t_c = dma(tape, "sync", onr[:, :], dram["ones_row"].ap(), sems)

        t_fill = tape.emit("gpsimd", lambda e: e.memset(vt[:, :, :], 1.0), sems.c_gp)
        vfm_r = dram["scr_vfm"].ap().rearrange("(c p) t -> p c t", p=128)
        t_copy_bank = {0: None, 1: None}
        t_tr_last = None
        for tc in range(32):
            if t_tr_last is not None:
                tape.wait("sync", sems.c_pe, t_tr_last)
            t_vl = dma(tape, "sync", vch[:, :, :], vfm_r[:, :, tc * 128:(tc + 1) * 128], sems)
            tape.wait("tensor", sems.d_sync, t_vl)
            if tc == 0:
                tape.wait("tensor", sems.d_sync, t_c)
                tape.wait("tensor", sems.c_gp, t_fill)
            for fc in range(8):
                b = fc % 2
                psb = (psA if b == 0 else psB).bitcast(BF16)
                tape.wait("tensor", sems.c_ve, t_copy_bank[b])
                t_tr_last = tape.emit("tensor", lambda e, fc=fc, psb=psb: e.transpose(
                    psb[:, 0:128], vch[:, fc, :], ident[:, :]), sems.c_pe)
                tape.wait("vector", sems.c_pe, t_tr_last)
                h0, h1 = 2 * fc, 2 * fc + 1
                tape.emit("vector", lambda e, psb=psb, tc=tc, h0=h0: e.tensor_copy(
                    vt[:, tc, h0 * 65:h0 * 65 + 64], psb[:, 0:64]), None)
                t_copy_bank[b] = tape.emit("vector", lambda e, psb=psb, tc=tc, h1=h1: e.tensor_copy(
                    vt[:, tc, h1 * 65:h1 * 65 + 64], psb[:, 64:128]), sems.c_ve)
        t_vt_done = t_copy_bank[1]

        gq, gk = dram["scr_gq"].ap(), dram["scr_gk"].ap()
        wq_, wk_ = dram["scr_wq"].ap(), dram["scr_wk"].ap()
        t_sAfree = t_copy_bank[0]
        t_sBfree = t_copy_bank[1]
        t_vfree_act = None
        t_vfree_ve = None
        t_att_dma = None
        t_obf_dma = None
        t_exp = {}
        t_pv32_rd = None
        t_parity_pe = [t_tr_last, t_tr_last]
        t_ld = [None, None]
        for h in range(NH_A):
            fc = h // 2
            r0 = (h % 2) * 64
            rs = slice(r0, r0 + 64)
            tape.wait("sync", sems.c_pe, t_parity_pe[h % 2])
            dma(tape, "sync", qp[rs, :], gq[h * 64:h * 64 + 64, :], sems)
            dma(tape, "sync", kp[rs, :], gk[h * 64:h * 64 + 64, :], sems)
            dma(tape, "sync", wqp[rs, :], wq_[h * 64:h * 64 + 64, :], sems)
            t_ld[h % 2] = dma(tape, "sync", wkp[rs, :], wk_[h * 64:h * 64 + 64, :], sems)
            if h == 0:
                tape.wait("tensor", sems.c_gp, t_fill1)
                tape.wait("tensor", sems.c_ve, t_vt_done)

            tape.wait("tensor", sems.d_sync, t_ld[h % 2])
            t_wexp = None
            t_wpv = None
            t_wmask = None
            t_wout = None
            t_wpvc = None
            for wg in range(8):
                qw0 = wg * 512
                if t_wexp is not None:
                    tape.wait("tensor", sems.c_act, t_wexp)
                t_qkw = None
                for i in range(4):
                    d = wg * 4 + i
                    t_qkw = tape.emit("tensor", lambda e, d=d, i=i, rs=rs: e.matmul(
                        ps_w[:, i * 128:(i + 1) * 128],
                        wkp[rs, d * 128:(d + 1) * 128],
                        wqp[rs, d * 128:(d + 1) * 128],
                        start=(i == 0), stop=(i == 3)),
                        sems.c_pe if i == 3 else None)
                tape.wait("scalar", sems.c_pe, t_qkw)
                if t_wmask is not None:
                    tape.wait("scalar", sems.c_ve, t_wmask)
                t_wexp = tape.emit("scalar", lambda e: e.activation(
                    ptile[:, 1, 1, 0:512], ps_w[:, 0:512], AF.Exp,
                    bias=wb[:, 0:1], scale=0.125), sems.c_act)
                tape.wait("vector", sems.c_act, t_wexp)
                if t_wpv is not None:
                    tape.wait("vector", sems.c_pe, t_wpv)
                mwb = maskw[:, None, :].broadcast_to([128, 4, 128])
                t_wmask = tape.emit("vector", lambda e, mwb=mwb: e.tensor_tensor(
                    ptile[:, 1, 1, 0:512].rearrange("p (a b) -> p a b", a=4),
                    ptile[:, 1, 1, 0:512].rearrange("p (a b) -> p a b", a=4),
                    mwb, ALU.mult), sems.c_ve)
                tape.wait("tensor", sems.c_ve, t_wmask)
                if t_wpvc is not None:
                    tape.wait("tensor", sems.c_act, t_wpvc)
                    tape.wait("tensor", sems.c_ve, t_wrr)
                for i in range(4):
                    d = wg * 4 + i
                    t_wpv = tape.emit(
                        "tensor",
                        lambda e, d=d, h=h, i=i: e.matmul(
                            ps_vw[:, i * 128:i * 128 + 128],
                            vt[:, d, h * 65:h * 65 + 65],
                            ptile[:, 1, 1, i * 128:(i + 1) * 128],
                            start=(i == 0), stop=(i == 3)),
                        sems.c_pe if i == 3 else None)
                tape.wait("vector", sems.c_pe, t_wpv)
                t_wrr = tape.emit("vector", lambda e: _lp(nc, lambda: e.reciprocal(
                    rden[64:65, 0:512], ps_vw[64:65, 0:512])), sems.c_ve)
                tape.wait("sync", sems.c_ve, t_wrr)
                t_rdma = dma(tape, "sync", rd0[0:1, 0, 0:512], rden[64:65, 0:512], sems)
                tape.wait("tensor", sems.d_sync, t_rdma)
                if t_wout is not None:
                    tape.wait("tensor", sems.c_ve, t_wout)
                t_rbw = tape.emit("tensor", lambda e: e.matmul(
                    psB[0:64, 0:512], onr[0:1, 0:64], rd0[0:1, 0, 0:512],
                    start=True, stop=True), sems.c_pe)
                tape.wait("scalar", sems.c_pe, t_wpv)
                tape.wait("scalar", sems.c_ve, t_pv32_rd)
                t_wpvc = tape.emit("scalar", lambda e: e.activation(
                    pv32[:, 0:512], ps_vw[0:64, 0:512], AF.Copy), sems.c_act)
                tape.wait("vector", sems.c_act, t_wpvc)
                tape.wait("vector", sems.c_pe, t_rbw)
                t_wout = tape.emit("vector", lambda e, qw0=qw0: e.tensor_tensor(
                    wout[:, 0, qw0:qw0 + 512], pv32[:, 0:512], psB[0:64, 0:512],
                    ALU.mult), sems.c_ve)
                t_pv32_rd = t_wout
            t_wout_done = t_wout

            tape.wait("tensor", sems.c_ve, t_wout_done)
            tape.wait("tensor", sems.c_act, t_wpvc)
            if t_vfree_act is not None:
                tape.wait("tensor", sems.c_act, t_vfree_act)
                tape.wait("tensor", sems.c_ve, t_vfree_ve)
            for qb in range(4):
                q0 = qb * 1024
                t_qkd = {}
                for kc in range(33):
                    if kc < 32:
                        par = kc % 2
                        psS = psA if par == 0 else psB
                        if kc >= 2:
                            tape.wait("tensor", sems.c_act, t_exp.get(kc - 2))
                        t_qk = None
                        for nn in range(2):
                            t_qk = tape.emit("tensor", lambda e, kc=kc, q0=q0, nn=nn, rs=rs,
                                             psS=psS: e.matmul(
                                psS[:, nn * 512:(nn + 1) * 512],
                                kp[rs, kc * 128:(kc + 1) * 128],
                                qp[rs, q0 + nn * 512:q0 + (nn + 1) * 512],
                                start=True, stop=True),
                                sems.c_pe if nn == 1 else None)
                        t_qkd[kc] = t_qk
                    if kc >= 1:
                        kp_ = kc - 1
                        parp = kp_ % 2
                        tape.wait("tensor", sems.c_act, t_exp[kp_])
                        for nn in range(2):
                            t_pvk = tape.emit("tensor", lambda e, kp_=kp_, parp=parp, nn=nn,
                                              h=h: e.matmul(
                                ps_v[:, nn * 512:(nn + 1) * 512],
                                vt[:, kp_, h * 65:h * 65 + 65],
                                ptile[:, 0, parp, nn * 512:(nn + 1) * 512],
                                start=(kp_ == 0), stop=(kp_ == 31)),
                                sems.c_pe if nn == 1 else None)
                    if kc < 32:
                        tape.wait("scalar", sems.c_pe, t_qkd[kc])
                        t_exp[kc] = tape.emit("scalar", lambda e, kc=kc, par=kc % 2,
                                              psS=psA if kc % 2 == 0 else psB:
                                              e.activation(
                            ptile[:, 0, par, :], psS[:, :], AF.Exp, bias=gb[:, kc:kc + 1]),
                            sems.c_act)
                t_lpv = t_pvk
                tape.wait("vector", sems.c_pe, t_lpv)
                t_rr = tape.emit("vector", lambda e: _lp(nc, lambda: e.reciprocal(
                    rden[64:65, 1024:2048], ps_v[64:65, :])), sems.c_ve)
                tape.wait("sync", sems.c_ve, t_rr)
                t_rdma = dma(tape, "sync", rd0[0:1, 1, :], rden[64:65, 1024:2048], sems)
                tape.wait("tensor", sems.d_sync, t_rdma)
                tape.wait("tensor", sems.c_act, t_exp[30])
                tape.wait("tensor", sems.c_act, t_exp[31])
                t_rb = None
                for nn in range(2):
                    psS = psA if nn == 0 else psB
                    t_rb = tape.emit("tensor", lambda e, nn=nn, psS=psS: e.matmul(
                        psS[0:64, 0:512], onr[0:1, 0:64],
                        rd0[0:1, 1, nn * 512:(nn + 1) * 512], start=True, stop=True),
                        sems.c_pe)
                tape.wait("scalar", sems.c_pe, t_lpv)
                tape.wait("scalar", sems.c_ve, t_pv32_rd)
                if t_obf_dma is not None:
                    tape.wait("scalar", sems.d_sync, t_obf_dma)
                t_pvc = tape.emit("scalar", lambda e: e.activation(
                    pv32[:, :], ps_v[0:64, :], AF.Copy), sems.c_act)
                tape.wait("vector", sems.c_act, t_pvc)
                tape.wait("vector", sems.c_pe, t_rb)
                if t_obf_dma is not None:
                    tape.wait("vector", sems.d_sync, t_obf_dma)
                tape.emit("vector", lambda e: e.tensor_tensor(
                    obf[:, 0:512], pv32[:, 0:512], psA[0:64, 0:512], ALU.mult), None)
                t_gout = tape.emit("vector", lambda e: e.tensor_tensor(
                    obf[:, 512:1024], pv32[:, 512:1024], psB[0:64, 0:512], ALU.mult),
                    sems.c_ve)
                t_pv32_rd = t_gout
                if h % 2 == 0:
                    t_aw = tape.emit("vector", lambda e, fc=fc, q0=q0: e.tensor_tensor(
                        att[0:64, fc, q0:q0 + 1024], obf[:, :],
                        wout[:, 0, q0:q0 + 1024], ALU.add), sems.c_ve)
                    t_obf_last_ve = t_aw
                else:
                    t_aw = tape.emit("vector", lambda e, q0=q0: e.tensor_tensor(
                        obf[:, :], obf[:, :], wout[:, 0, q0:q0 + 1024], ALU.add),
                        sems.c_ve)
                    tape.wait("sync", sems.c_ve, t_aw)
                    t_att_dma = dma(
                        tape, "sync", att[64:128, fc, q0:q0 + 1024], obf[:, :], sems)
                    t_obf_dma = t_att_dma
                    t_obf_last_ve = t_aw
                t_vfree_act = t_pvc
                t_vfree_ve = t_rr
                tape.wait("tensor", sems.c_ve, t_gout)
            t_parity_pe[h % 2] = t_pvk
            t_sfree_ve = t_obf_last_ve
        t_last_pair_pe = t_pvk

        if parts != "all":
            tape.wait("sync", sems.d_sync, sems.d_sync.n)
            tape.run_block("a2")
            return
        t_att_done = t_sfree_ve
        tape.wait("gpsimd", sems.c_pe, t_last_pair_pe)
        t_woa = dma(tape, "gpsimd", woa[:, :, :],
                    dram["w_out_a"].ap().rearrange("(c p) j -> p c j", p=128), sems)
        t_boa = dma(tape, "sync", boa[:, :], dram["b_out_a"].ap(), sems)
        xT_r = dram["xT"].ap().rearrange("(c p) t -> p c t", p=128)
        t_oev = {}
        t_ost = {}
        t_radd = {}
        for s in range(NSLAB):
            if s > 0:
                tape.wait("sync", sems.c_ve, t_radd[(s - 1, 7)])
            else:
                tape.wait("sync", sems.c_pe, t_last_pair_pe)
                tape.wait("sync", sems.c_ve, t_att_done)
            t_xr = dma(tape, "sync", xres[:, :, :], xT_r[:, :, s * SLAB:(s + 1) * SLAB], sems)
            for dmc in range(8):
                ps = psA if dmc % 2 == 0 else psB
                pj = (s, dmc - 2) if dmc >= 2 else ((s - 1, 6 + dmc) if s > 0 else None)
                tape.wait("tensor", sems.d_gp, t_woa)
                if s == 0 and dmc == 0:
                    tape.wait("tensor", sems.c_ve, t_att_done)
                    tape.wait("tensor", sems.c_act, t_vfree_act)
                    tape.wait("tensor", sems.d_sync, t_att_dma)
                if pj is not None:
                    tape.wait("tensor", sems.c_ve, t_oev[pj])
                t_mm = None
                for c in range(8):
                    t_mm = tape.emit("tensor", lambda e, c=c, dmc=dmc, s=s, ps=ps: e.matmul(
                        ps[:, 0:512], woa[:, c, dmc * 128:(dmc + 1) * 128],
                        att[:, c, s * SLAB:(s + 1) * SLAB], start=(c == 0), stop=(c == 7)),
                        sems.c_pe if c == 7 else None)
                tape.wait("vector", sems.c_pe, t_mm)
                tape.wait("vector", sems.d_sync, t_xr)
                tape.wait("vector", sems.d_sync, t_boa)
                if pj is not None and pj in t_ost:
                    tape.wait("vector", sems.d_sync, t_ost[pj])
                t_radd[(s, dmc)] = tape.emit(
                    "vector", lambda e, ps=ps, dmc=dmc: e.scalar_tensor_tensor(
                        ostage[:, dmc % 2, :], ps[:, 0:512], boa[:, dmc:dmc + 1],
                        xres[:, dmc, :], ALU.add, ALU.add), sems.c_ve)
                t_oev[(s, dmc)] = t_radd[(s, dmc)]
                tape.wait("sync", sems.c_ve, t_radd[(s, dmc)])
                t_ost[(s, dmc)] = dma(
                    tape, "sync",
                    dram["scr_R"].ap()[dmc * 128:(dmc + 1) * 128, s * SLAB:(s + 1) * SLAB],
                    ostage[:, dmc % 2, :], sems)
        tape.wait("sync", sems.d_sync, sems.d_sync.n)
        tape.run_block("a2")



def phase_ffn(nc, sems, dram, src, dst, pfx):
    tape = Tape(nc)
    from contextlib import ExitStack
    with ExitStack() as ctx:
        Rs = ctx.enter_context(nc.sbuf_tensor(pfx + "_R", [128, 8, SLAB], F32))
        xn = ctx.enter_context(nc.sbuf_tensor(pfx + "_xn", [128, 8, SLAB + 1], BF16))
        xr = ctx.enter_context(nc.sbuf_tensor(pfx + "_xr", [128, 8, SLAB], BF16))
        xk = ctx.enter_context(nc.sbuf_tensor(pfx + "_xk", [128, 8, SLAB], BF16))
        mixt = ctx.enter_context(nc.sbuf_tensor(pfx + "_mt", [128, SLAB], BF16))
        wF = ctx.enter_context(nc.sbuf_tensor(pfx + "_wF", [128, 8, 3072], BF16))
        muF = ctx.enter_context(nc.sbuf_tensor(pfx + "_mu", [128, 16], F32))
        omF = ctx.enter_context(nc.sbuf_tensor(pfx + "_om", [128, 16], F32))
        r32 = ctx.enter_context(nc.sbuf_tensor(pfx + "_r32", [128, 8, SLAB], F32))
        ka = ctx.enter_context(nc.sbuf_tensor(pfx + "_ka", [128, 8, SLAB], BF16))
        km = ctx.enter_context(nc.sbuf_tensor(pfx + "_km", [128, 2, SLAB], F32))
        vst = ctx.enter_context(nc.sbuf_tensor(pfx + "_vst", [128, 2, SLAB], F32))
        ost = ctx.enter_context(nc.sbuf_tensor(pfx + "_ost", [128, 2, SLAB], F32))
        ln = make_ln_tiles(nc, ctx, pfx)
        ps_p = [ctx.enter_context(nc.psum_tensor(pfx + "_pp0", [128, SLAB], F32)),
                ctx.enter_context(nc.psum_tensor(pfx + "_pp1", [128, SLAB], F32))]

        t_c = load_ln_consts(tape, sems, ln, dram)
        t_mu = dma(tape, "sync", muF[:, :], dram["muF"].ap(), sems)
        t_wf = dma(tape, "gpsimd", wF[:, :, :],
                   dram["wF"].ap().rearrange("(c p) j -> p c j", p=128), sems)
        tape.wait("vector", sems.d_sync, t_mu)
        t_om = tape.emit("vector", lambda e: e.tensor_scalar(
            omF[:, :], muF[:, :], -1.0, 1.0, ALU.mult, ALU.add), sems.c_ve)
        tape.emit("vector", lambda e: e.memset(xn[:, :, 0:1], 0.0), None)

        src_r = src.ap().rearrange("(c p) t -> p c t", p=128)
        dst_r = dst.ap().rearrange("(c p) t -> p c t", p=128)
        prev_ln = None
        t_mix_prev = None
        t_rkv_mm = None
        t_vadd = {}
        t_store = {}
        psg = [(None, None), (None, None)]
        for s in range(NSLAB):
            if prev_ln is not None:
                tape.wait("sync", sems.c_ve, prev_ln["ve"])
                tape.wait("sync", sems.c_act, prev_ln["sq"])
                tape.wait("sync", sems.c_ve, t_vadd[(s - 1, 7)])
            t_R = dma(tape, "sync", Rs[:, :, :], src_r[:, :, s * SLAB:(s + 1) * SLAB], sems)
            tape.wait("scalar", sems.d_sync, t_R)
            tape.wait("vector", sems.d_sync, t_R)
            if s > 0:
                tape.emit("vector", lambda e: e.tensor_copy(
                    xn[:, :, 0:1], xn[:, :, SLAB:SLAB + 1]), None)
            prev_ln = ln_emit(tape, sems, Rs, xn, ln, xn_off=1)
            tape.wait("vector", sems.c_pe, t_rkv_mm)
            t_mix = None
            for c in range(8):
                tape.emit("vector", lambda e, c=c: e.tensor_scalar(
                    mixt[:, :], xn[:, c, 1:SLAB + 1], muF[:, c:c + 1], None, ALU.mult), None)
                tape.emit("vector", lambda e, c=c: e.scalar_tensor_tensor(
                    xr[:, c, :], xn[:, c, 0:SLAB], omF[:, c:c + 1], mixt[:, :],
                    ALU.mult, ALU.add), None)
                tape.emit("vector", lambda e, c=c: e.tensor_scalar(
                    mixt[:, :], xn[:, c, 1:SLAB + 1], muF[:, 8 + c:9 + c], None, ALU.mult), None)
                t_mix = tape.emit("vector", lambda e, c=c: e.scalar_tensor_tensor(
                    xk[:, c, :], xn[:, c, 0:SLAB], omF[:, 8 + c:9 + c], mixt[:, :],
                    ALU.mult, ALU.add), sems.c_ve)
            tape.wait("tensor", sems.d_gp, t_wf)
            tape.wait("tensor", sems.c_ve, t_mix)
            for dj in range(8):
                ps = ps_p[dj % 2]
                g = psg[dj % 2]
                tape.wait("tensor", g[0], g[1]) if g[0] else None
                t_mm = None
                for c in range(8):
                    t_mm = tape.emit("tensor", lambda e, c=c, dj=dj, ps=ps: e.matmul(
                        ps[:, :], wF[:, c, dj * 128:(dj + 1) * 128], xr[:, c, :],
                        start=(c == 0), stop=(c == 7)), sems.c_pe if c == 7 else None)
                tape.wait("scalar", sems.c_pe, t_mm)
                t_ev = tape.emit("scalar", lambda e, dj=dj, ps=ps: e.activation(
                    r32[:, dj, :], ps[:, :], AF.Sigmoid), sems.c_act)
                psg[dj % 2] = (sems.c_act, t_ev)
            for dj in range(8):
                ps = ps_p[dj % 2]
                g = psg[dj % 2]
                tape.wait("tensor", g[0], g[1]) if g[0] else None
                t_mm = None
                for c in range(8):
                    t_mm = tape.emit("tensor", lambda e, c=c, dj=dj, ps=ps: e.matmul(
                        ps[:, :], wF[:, c, 1024 + dj * 128:1024 + (dj + 1) * 128], xk[:, c, :],
                        start=(c == 0), stop=(c == 7)), sems.c_pe if c == 7 else None)
                tape.wait("vector", sems.c_pe, t_mm)
                if dj >= 2:
                    tape.wait("vector", sems.c_act, t_kmr)
                t_km = tape.emit("vector", lambda e, dj=dj, ps=ps: e.tensor_scalar(
                    km[:, dj % 2, :], ps[:, :], 0.0, None, ALU.max), sems.c_ve)
                psg[dj % 2] = (sems.c_ve, t_km)
                tape.wait("scalar", sems.c_ve, t_km)
                t_kmr = tape.emit("scalar", lambda e, dj=dj: e.activation(
                    ka[:, dj, :], km[:, dj % 2, :], AF.Square), sems.c_act)
            t_ka_done = t_kmr
            tape.wait("tensor", sems.c_act, t_ka_done)
            for dj in range(8):
                ps = ps_p[dj % 2]
                g = psg[dj % 2]
                tape.wait("tensor", g[0], g[1]) if g[0] else None
                t_mm = None
                for c in range(8):
                    t_mm = tape.emit("tensor", lambda e, c=c, dj=dj, ps=ps: e.matmul(
                        ps[:, :], wF[:, c, 2048 + dj * 128:2048 + (dj + 1) * 128], ka[:, c, :],
                        start=(c == 0), stop=(c == 7)), sems.c_pe if c == 7 else None)
                t_rkv_mm = t_mm
                tape.wait("vector", sems.c_pe, t_mm)
                pj = (s, dj - 2) if dj >= 2 else ((s - 1, 6 + dj) if s > 0 else None)
                if pj is not None and pj in t_store:
                    tape.wait("vector", sems.d_sync, t_store[pj])
                tape.emit("vector", lambda e, dj=dj, ps=ps: e.tensor_tensor(
                    vst[:, dj % 2, :], r32[:, dj, :], ps[:, :], ALU.mult), None)
                psg[dj % 2] = (sems.c_ve, None)
                t_vadd[(s, dj)] = tape.emit("vector", lambda e, dj=dj: e.tensor_tensor(
                    ost[:, dj % 2, :], vst[:, dj % 2, :], Rs[:, dj, :], ALU.add), sems.c_ve)
                psg[dj % 2] = (sems.c_ve, t_vadd[(s, dj)])
                tape.wait("sync", sems.c_ve, t_vadd[(s, dj)])
                t_store[(s, dj)] = dma(
                    tape, "sync", dst_r[:, dj, s * SLAB:(s + 1) * SLAB],
                    ost[:, dj % 2, :], sems)
        tape.wait("sync", sems.d_sync, sems.d_sync.n)
        tape.run_block(pfx)



def phase_c1(nc, sems, dram):
    tape = Tape(nc)
    from contextlib import ExitStack
    with ExitStack() as ctx:
        Rs = ctx.enter_context(nc.sbuf_tensor("c1_R", [128, 8, SLAB], F32))
        u = ctx.enter_context(nc.sbuf_tensor("c1_u", [128, 8, SLAB], BF16))
        winT = ctx.enter_context(nc.sbuf_tensor("c1_w", [128, 8, DINP], BF16))
        xbc = ctx.enter_context(nc.sbuf_tensor("c1_xbc", [128, 17, SLAB + 3], F32))
        zst = ctx.enter_context(nc.sbuf_tensor("c1_zst", [128, 2, SLAB], BF16))
        cva = ctx.enter_context(nc.sbuf_tensor("c1_cva", [128, 2, SLAB], F32))
        cvb = ctx.enter_context(nc.sbuf_tensor("c1_cvb", [128, 2, SLAB], F32))
        xsil = ctx.enter_context(nc.sbuf_tensor("c1_xsil", [128, 2, SLAB], BF16))
        cw = ctx.enter_context(nc.sbuf_tensor("c1_cw", [128, 68], F32))
        cb = ctx.enter_context(nc.sbuf_tensor("c1_cb", [128, 17], F32))
        dtb = ctx.enter_context(nc.sbuf_tensor("c1_dtb", [32, 1], F32))
        alog = ctx.enter_context(nc.sbuf_tensor("c1_alog", [32, 1], F32))
        acol = ctx.enter_context(nc.sbuf_tensor("c1_acol", [32, 1], F32))
        drows = ctx.enter_context(nc.sbuf_tensor("c1_dr", [32, 8, SLAB], F32))
        dbf = ctx.enter_context(nc.sbuf_tensor("c1_dbf", [32, 3, SLAB], BF16))
        ahi32 = ctx.enter_context(nc.sbuf_tensor("c1_ahi32", [32, SLAB], F32))
        ln = make_ln_tiles(nc, ctx, "c1")
        ps_p = [ctx.enter_context(nc.psum_tensor("c1_pp0", [128, SLAB], F32)),
                ctx.enter_context(nc.psum_tensor("c1_pp1", [128, SLAB], F32))]

        t_c = load_ln_consts(tape, sems, ln, dram)
        t_c = dma(tape, "sync", cw[:, :], dram["convw"].ap(), sems)
        t_c = dma(tape, "sync", cb[:, :], dram["convb"].ap(), sems)
        t_c = dma(tape, "sync", dtb[:, :], dram["dtb"].ap(), sems)
        t_al = dma(tape, "sync", alog[:, :], dram["alog"].ap(), sems)
        t_wi = dma(tape, "gpsimd", winT[:, :, :],
                   dram["w_in"].ap().rearrange("(c p) j -> p c j", p=128), sems)
        tape.wait("scalar", sems.d_sync, t_al)
        t_ea = tape.emit("scalar", lambda e: e.activation(
            acol[:, :], alog[:, :], AF.Exp), sems.c_act)

        t_tail = tape.emit("vector", lambda e: e.memset(xbc[:, :, 0:3], 0.0), sems.c_ve)
        t_tap_gp = None

        src_r = dram["scr_R"].ap().rearrange("(c p) t -> p c t", p=128)
        prev_ln = None
        psg = [(None, None), (None, None)]
        t_zsp = {}
        t_conv_ve = None
        t_conv_gp = None
        t_xbc_ev = {}
        t_sil_sp = {}
        t_dspill_prev = None
        for s in range(NSLAB):
            if prev_ln is not None:
                tape.wait("sync", sems.c_ve, prev_ln["ve"])
                tape.wait("sync", sems.c_act, prev_ln["sq"])
            t_R = dma(tape, "sync", Rs[:, :, :], src_r[:, :, s * SLAB:(s + 1) * SLAB], sems)
            tape.wait("scalar", sems.d_sync, t_R)
            tape.wait("vector", sems.d_sync, t_R)
            prev_ln = ln_emit(tape, sems, Rs, u, ln)
            tape.wait("vector", sems.c_gp, t_tap_gp)
            if s > 0:
                t_tail = tape.emit("vector", lambda e: e.tensor_copy(
                    xbc[:, :, 0:3], xbc[:, :, SLAB:SLAB + 3]), sems.c_ve)
            tape.wait("tensor", sems.d_gp, t_wi)
            tape.wait("tensor", sems.c_ve, prev_ln["ve"])
            for j in range(34):
                ps = ps_p[j % 2]
                g = psg[j % 2]
                tape.wait("tensor", g[0], g[1]) if g[0] else None
                mwid = 128 if j < 33 else 32
                t_mm = None
                for c in range(8):
                    t_mm = tape.emit("tensor", lambda e, c=c, j=j, ps=ps, mwid=mwid: e.matmul(
                        ps[0:mwid, :], winT[:, c, j * 128:j * 128 + mwid], u[:, c, :],
                        start=(c == 0), stop=(c == 7)), sems.c_pe if c == 7 else None)
                if j < 16:
                    tape.wait("scalar", sems.c_pe, t_mm)
                    pj = (s, j - 2) if j >= 2 else ((s - 1, 14 + j) if s > 0 else None)
                    if pj is not None and pj in t_zsp:
                        tape.wait("scalar", sems.d_sync, t_zsp[pj])
                    t_ev = tape.emit("scalar", lambda e, j=j, ps=ps: e.activation(
                        zst[:, j % 2, :], ps[:, :], AF.Copy), sems.c_act)
                    psg[j % 2] = (sems.c_act, t_ev)
                    tape.wait("sync", sems.c_act, t_ev)
                    t_zsp[(s, j)] = dma(
                        tape, "sync",
                        dram["scr_z"].ap()[j * 128:(j + 1) * 128, s * SLAB:(s + 1) * SLAB],
                        zst[:, j % 2, :], sems)
                elif j < 33:
                    fc = j - 16
                    tape.wait("scalar", sems.c_pe, t_mm)
                    if fc == 0:
                        tape.wait("scalar", sems.c_ve, t_tail)
                    t_ev = tape.emit("scalar", lambda e, fc=fc, ps=ps: e.activation(
                        xbc[:, fc, 3:SLAB + 3], ps[:, :], AF.Copy), sems.c_act)
                    psg[j % 2] = (sems.c_act, t_ev)
                    t_xbc_ev[fc] = t_ev
                else:
                    tape.wait("scalar", sems.c_pe, t_mm)
                    tape.wait("scalar", sems.d_sync, t_c)
                    t_dtp = tape.emit("scalar", lambda e, ps=ps: e.activation(
                        drows[:, 0, :], ps[0:32, :], AF.Identity, bias=dtb[:, 0:1]), sems.c_act)
                    psg[j % 2] = (sems.c_act, t_dtp)
                    tape.wait("vector", sems.c_act, t_dtp)
                    t_ve1 = tape.emit("vector", lambda e: e.tensor_scalar(
                        drows[:, 4, :], drows[:, 0, :], 0.0, None, ALU.max), sems.c_ve)
                    tape.emit("scalar", lambda e: e.activation(
                        drows[:, 1, :], drows[:, 0, :], AF.Abs), None)
                    tape.wait("scalar", sems.c_ve, t_ve1)
                    tape.emit("scalar", lambda e: e.activation(
                        drows[:, 2, :], drows[:, 1, :], AF.Exp, scale=-1.0), None)
                    t_a1 = tape.emit("scalar", lambda e: e.activation(
                        drows[:, 3, :], drows[:, 2, :], AF.Ln, bias=1.0), sems.c_act)
                    tape.wait("vector", sems.c_act, t_a1)
                    t_dt = tape.emit("vector", lambda e: e.tensor_tensor(
                        drows[:, 5, :], drows[:, 4, :], drows[:, 3, :], ALU.add), sems.c_ve)
                    tape.wait("vector", sems.c_act, t_ea)
                    t_ad = tape.emit("vector", lambda e: e.tensor_scalar(
                        drows[:, 7, :], drows[:, 5, :], acol[:, 0:1], -1.0,
                        ALU.mult, ALU.mult), sems.c_ve)
                    t_sc = None
                    for cc in range(8):
                        t_sc = tape.emit("vector", lambda e, cc=cc: e.tensor_tensor_scan(
                            drows[:, 6, cc * 64:(cc + 1) * 64],
                            drows[:, 7, cc * 64:(cc + 1) * 64],
                            drows[:, 7, cc * 64:(cc + 1) * 64], 0.0,
                            ALU.add, ALU.bypass), sems.c_ve)
                    tape.emit("vector", lambda e: e.tensor_scalar(
                        drows[:, 1, :], drows[:, 5, :], 1e-30, None, ALU.max), None)
                    tape.emit("vector", lambda e: e.tensor_copy(
                        dbf[:, 0, :], drows[:, 6, :]), None)
                    t_hi = tape.emit("vector", lambda e: e.tensor_copy(
                        ahi32[:, :], dbf[:, 0, :]), sems.c_ve)
                    t_lo = tape.emit("vector", lambda e: e.tensor_tensor(
                        dbf[:, 1, :], drows[:, 6, :], ahi32[:, :], ALU.subtract), sems.c_ve)
                    tape.wait("scalar", sems.c_ve, t_lo)
                    t_lnd = tape.emit("scalar", lambda e: e.activation(
                        dbf[:, 2, :], drows[:, 1, :], AF.Ln), sems.c_act)
                    if t_dspill_prev is not None:
                        tape.wait("sync", sems.d_sync, t_dspill_prev)
                    tape.wait("sync", sems.c_ve, t_lo)
                    tape.wait("sync", sems.c_act, t_lnd)
                    sl = slice(s * SLAB, (s + 1) * SLAB)
                    dma(tape, "sync", dram["scr_dt"].ap()[:, sl], drows[:, 5, :], sems)
                    dma(tape, "sync", dram["scr_acum"].ap()[:, sl], drows[:, 6, :], sems)
                    dma(tape, "sync", dram["scr_ahi"].ap()[:, sl], dbf[:, 0, :], sems)
                    dma(tape, "sync", dram["scr_alo"].ap()[:, sl], dbf[:, 1, :], sems)
                    t_dspill_prev = dma(tape, "sync", dram["scr_lnd"].ap()[:, sl],
                                        dbf[:, 2, :], sems)
            tape.wait("vector", sems.d_sync, t_c)
            tape.wait("gpsimd", sems.d_sync, t_c)
            t_sil = {}
            for fc in range(17):
                eng = "vector"
                cv = cva if fc % 2 == 0 else cvb
                cctr = sems.c_ve
                pp = 0
                tape.wait(eng, sems.c_act, t_xbc_ev[fc])
                if fc >= 2:
                    tape.wait(eng, sems.c_act, t_sil.get(fc - 2))
                tape.emit(eng, lambda e, fc=fc, cv=cv, pp=pp: e.tensor_scalar(
                    cv[:, pp, :], xbc[:, fc, 0:SLAB], cw[:, 4 * fc:4 * fc + 1], None,
                    ALU.mult), None)
                t_cv = None
                for tap in (1, 2, 3):
                    t_cv = tape.emit(eng, lambda e, fc=fc, cv=cv, pp=pp, tap=tap:
                                     e.scalar_tensor_tensor(
                                         cv[:, pp, :], xbc[:, fc, tap:SLAB + tap],
                                         cw[:, 4 * fc + tap:4 * fc + tap + 1], cv[:, pp, :],
                                         ALU.mult, ALU.add),
                                     cctr if tap == 3 else None)

                tape.wait("scalar", cctr, t_cv)
                pj = fc - 2
                if pj >= 0 and pj in t_sil_sp:
                    tape.wait("scalar", sems.d_sync, t_sil_sp[pj])
                t_sil[fc] = tape.emit("scalar", lambda e, fc=fc, cv=cv, pp=pp: e.activation(
                    xsil[:, fc % 2, :], cv[:, pp, :], AF.Silu, bias=cb[:, fc:fc + 1]),
                    sems.c_act)
                tape.wait("sync", sems.c_act, t_sil[fc])
                if fc < 16:
                    dst = dram["scr_xs"].ap()[fc * 128:(fc + 1) * 128, s * SLAB:(s + 1) * SLAB]
                else:
                    dst = dram["scr_bc"].ap()[:, s * SLAB:(s + 1) * SLAB]
                t_sil_sp[fc] = dma(tape, "sync", dst, xsil[:, fc % 2, :], sems)
        tape.wait("sync", sems.d_sync, sems.d_sync.n)
        tape.run_block("c1")




C2_LEVEL = 99


def phase_c2(nc, sems, dram):
    tape = Tape(nc)

    def sect(lv):
        tape.mute = lv > C2_LEVEL
    from contextlib import ExitStack
    with ExitStack() as ctx:
        Rs = ctx.enter_context(nc.sbuf_tensor("c2_R", [128, 8, SLAB], F32))
        z16 = ctx.enter_context(nc.sbuf_tensor("c2_z", [128, 16, SLAB], BF16))
        xf16 = ctx.enter_context(nc.sbuf_tensor("c2_xf", [128, 16, SLAB], BF16))
        bfm0 = ctx.enter_context(nc.sbuf_tensor("c2_bf0", [64, SLAB], BF16))
        cfm0 = ctx.enter_context(nc.sbuf_tensor("c2_cf0", [64, SLAB], BF16))
        xtc = [ctx.enter_context(nc.sbuf_tensor(f"c2_xt{i}", [64, 16, 128], BF16))
               for i in range(2)]
        dts = ctx.enter_context(nc.sbuf_tensor("c2_dt", [32, SLAB], F32))
        ac32 = ctx.enter_context(nc.sbuf_tensor("c2_ac", [32, SLAB], F32))
        ahi = ctx.enter_context(nc.sbuf_tensor("c2_ahi", [32, SLAB], BF16))
        alo = ctx.enter_context(nc.sbuf_tensor("c2_alo", [32, SLAB], BF16))
        lnd = ctx.enter_context(nc.sbuf_tensor("c2_lnd", [32, SLAB], BF16))
        afh = ctx.enter_context(nc.sbuf_tensor("c2_afh", [1, 2, 2048], BF16))
        afl = ctx.enter_context(nc.sbuf_tensor("c2_afl", [1, 2, 2048], BF16))
        wob = ctx.enter_context(nc.sbuf_tensor("c2_wob", [128, 16, 1024], BF16))
        S = ctx.enter_context(nc.sbuf_tensor("c2_S", [64, 32, 64], F32))
        Sbf = ctx.enter_context(nc.sbuf_tensor("c2_Sbf", [64, 32, 64], BF16))
        eE = ctx.enter_context(nc.sbuf_tensor("c2_eE", [64, 1024], BF16))
        sdo = ctx.enter_context(nc.sbuf_tensor("c2_sdo", [64, 1024], BF16))
        Pss = ctx.enter_context(nc.sbuf_tensor("c2_P", [64, 1024], BF16))
        xw = ctx.enter_context(nc.sbuf_tensor("c2_xw", [64, 1024], BF16))
        t2 = ctx.enter_context(nc.sbuf_tensor("c2_t2", [64, 1024], F32))
        t3 = t2
        ybf = ctx.enter_context(nc.sbuf_tensor("c2_ybf", [64, 2, 1024], BF16))
        btok = ctx.enter_context(nc.sbuf_tensor("c2_btok", [64, 64], BF16))
        wdt = ctx.enter_context(nc.sbuf_tensor("c2_wd", [32, 2, 64], F32))
        wdbf = ctx.enter_context(nc.sbuf_tensor("c2_wdbf", [32, 64], BF16))
        ebc64 = ctx.enter_context(nc.sbuf_tensor("c2_ebc", [32, 64], BF16))
        eb32 = ctx.enter_context(nc.sbuf_tensor("c2_eb32", [32, 1], F32))
        ysl = ctx.enter_context(nc.sbuf_tensor("c2_ysl", [128, 16, SLAB], F32))
        y3 = z16
        sqst = ctx.enter_context(nc.sbuf_tensor("c2_sqst", [128, 2, SLAB], BF16))
        szst = ctx.enter_context(nc.sbuf_tensor("c2_szst", [128, 2, SLAB], BF16))
        rrow = ctx.enter_context(nc.sbuf_tensor("c2_rrow", [1, 2, SLAB], F32))
        rbf = ctx.enter_context(nc.sbuf_tensor("c2_rbf", [1, SLAB], BF16))
        ident = ctx.enter_context(nc.sbuf_tensor("c2_id", [128, 128], BF16))
        onr = ctx.enter_context(nc.sbuf_tensor("c2_onr", [1, 128], BF16))
        onc = ctx.enter_context(nc.sbuf_tensor("c2_onc", [128, 1], BF16))
        Ind = ctx.enter_context(nc.sbuf_tensor("c2_ind", [32, 2048], BF16))
        IndN = ctx.enter_context(nc.sbuf_tensor("c2_indn", [32, 2048], BF16))
        maskc = ctx.enter_context(nc.sbuf_tensor("c2_mask", [64, 2048], BF16))
        Drep = ctx.enter_context(nc.sbuf_tensor("c2_drep", [128, 16], F32))
        nrmw = ctx.enter_context(nc.sbuf_tensor("c2_nw", [128, 16], F32))
        ostage = ctx.enter_context(nc.sbuf_tensor("c2_ostg", [128, 2, SLAB], F32))
        PS = [ctx.enter_context(nc.psum_tensor(f"c2_ps{i}", [128, 1024], F32))
              for i in range(4)]
        psYT = PS[2].bitcast(BF16)

        t_c = dma(tape, "sync", ident[:, :], dram["ident"].ap(), sems)
        t_c = dma(tape, "sync", onr[:, :], dram["ones_row"].ap(), sems)
        t_c = dma(tape, "sync", onc[:, :], dram["ones_col"].ap(), sems)
        t_c = dma(tape, "sync", Ind[:, :], dram["Ind"].ap(), sems)
        t_c = dma(tape, "sync", IndN[:, :], dram["IndN"].ap(), sems)
        t_c = dma(tape, "sync", maskc[:, :], dram["maskc"].ap(), sems)
        t_c = dma(tape, "sync", Drep[:, :], dram["Drep"].ap(), sems)
        t_c = dma(tape, "sync", nrmw[:, :], dram["normw"].ap(), sems)
        t_wo = dma(tape, "gpsimd", wob[:, :, :],
                   dram["w_out_b"].ap().rearrange("(c p) j -> p c j", p=128), sems)
        tape.emit("vector", lambda e: e.memset(S[:, :, :], 0.0), None)
        t_S0 = tape.emit("vector", lambda e: e.memset(Sbf[:, :, :], 0.0), sems.c_ve)

        src_r = dram["scr_R"].ap().rearrange("(c p) t -> p c t", p=128)
        z_r = dram["scr_z"].ap().rearrange("(c p) t -> p c t", p=128)
        xs_r = dram["scr_xs"].ap().rearrange("(c p) t -> p c t", p=128)

        t_sfree = {i: (None, None) for i in range(4)}

        def psum_wait(eng, idx):
            ctr, val = t_sfree[idx]
            if ctr is not None:
                tape.wait(eng, ctr, val)

        t_xtc_free = [(None, None), (None, None)]
        t_radd = {}
        t_ost = {}
        t_y3_mm = None
        t_sq15 = None
        t_Snew = t_S0
        for s in range(NSLAB):
            sect(0)
            sl = slice(s * SLAB, (s + 1) * SLAB)
            if s > 0:
                tape.wait("sync", sems.c_ve, t_radd.get((s - 1, 7)))
                tape.wait("sync", sems.c_pe, t_y3_mm)
                tape.wait("sync", sems.c_act, t_sq15)
            t_z = dma(tape, "sync", z16[:, :, :], z_r[:, :, sl], sems)
            t_xf = dma(tape, "sync", xf16[:, :, :], xs_r[:, :, sl], sems)
            t_b0 = dma(tape, "sync", bfm0[:, :], dram["scr_bc"].ap()[0:64, sl], sems)
            t_c0 = dma(tape, "sync", cfm0[:, :], dram["scr_bc"].ap()[64:128, sl], sems)
            t_dt = dma(tape, "sync", dts[:, :], dram["scr_dt"].ap()[:, sl], sems)
            t_ac = dma(tape, "sync", ac32[:, :], dram["scr_acum"].ap()[:, sl], sems)
            t_hi = dma(tape, "sync", ahi[:, :], dram["scr_ahi"].ap()[:, sl], sems)
            t_lo = dma(tape, "sync", alo[:, :], dram["scr_alo"].ap()[:, sl], sems)
            t_ln = dma(tape, "sync", lnd[:, :], dram["scr_lnd"].ap()[:, sl], sems)
            t_R = dma(tape, "sync", Rs[:, :, :], src_r[:, :, sl], sems)
            tape.wait("tensor", sems.d_sync, t_ln)
            tape.wait("tensor", sems.d_sync, t_xf)
            if s == 0:
                tape.wait("tensor", sems.d_gp, t_wo)

            t_af_mm = {0: None, 1: None}
            for c in range(8):
                sect(0)
                par = c % 2
                c64 = slice(c * 64, (c + 1) * 64)
                if t_af_mm[par] is not None:
                    tape.wait("sync", sems.c_pe, t_af_mm[par])
                csl = slice(s * SLAB + c * 64, s * SLAB + (c + 1) * 64)
                dma(tape, "sync", afh[:, par, :].rearrange("p (h l) -> p h l", h=32),
                    dram["scr_ahi"].ap()[:, csl].unsqueeze(0), sems)
                t_af = dma(tape, "sync", afl[:, par, :].rearrange("p (h l) -> p h l", h=32),
                           dram["scr_alo"].ap()[:, csl].unsqueeze(0), sems)
                sect(1)
                psum_wait("tensor", 2)
                g0, g1 = t_xtc_free[par]
                if g0 is not None:
                    tape.wait("tensor", g0, g1)
                t_xtr = {}
                for grp in range(2):
                    if grp == 1:
                        tape.wait("tensor", sems.c_ve, t_xtr[0])
                    for i in range(8):
                        fcc = grp * 8 + i
                        t_tr = tape.emit("tensor", lambda e, fcc=fcc, i=i, c64=c64: e.transpose(
                            psYT[0:64, i * 128:(i + 1) * 128], xf16[:, fcc, c64],
                            ident[:, :]), sems.c_pe if i == 7 else None)
                    tape.wait("vector", sems.c_pe, t_tr)
                    t_xtr[grp] = tape.emit("vector", lambda e, grp=grp, par=par: e.tensor_copy(
                        xtc[par][:, grp * 8:(grp + 1) * 8, :],
                        psYT[0:64, 0:1024].rearrange("p (a b) -> p a b", a=8)), sems.c_ve)
                    t_sfree[2] = (sems.c_ve, t_xtr[grp])
                t_xt_done = t_xtr[1]
                sect(2)
                psum_wait("tensor", 0)
                tape.wait("tensor", sems.d_sync, t_b0)
                psEb = PS[0].bitcast(BF16)
                t_bt = tape.emit("tensor", lambda e, c64=c64, psEb=psEb: e.transpose(
                    psEb[0:64, 0:64], bfm0[:, c64], ident[0:64, 0:64]), sems.c_pe)
                tape.wait("vector", sems.c_pe, t_bt)
                t_btc = tape.emit("vector", lambda e, psEb=psEb: e.tensor_copy(
                    btok[:, :], psEb[0:64, 0:64]), sems.c_ve)
                t_sfree[0] = (sems.c_ve, t_btc)
                sect(2)
                tape.wait("vector", sems.d_sync, t_ac)
                tape.wait("vector", sems.d_sync, t_dt)
                t_d1 = tape.emit("vector", lambda e, c=c: e.tensor_scalar(
                    wdt[:, 0, :], ac32[:, c * 64:(c + 1) * 64],
                    ac32[:, c * 64 + 63:c * 64 + 64], None, ALU.subtract), sems.c_ve)
                tape.wait("scalar", sems.c_ve, t_d1)
                t_ew = tape.emit("scalar", lambda e: e.activation(
                    wdt[:, 1, :], wdt[:, 0, :], AF.Exp, scale=-1.0), sems.c_act)
                tape.wait("vector", sems.c_act, t_ew)
                t_wd = tape.emit("vector", lambda e, c=c: e.tensor_tensor(
                    wdbf[:, :], wdt[:, 1, :], dts[:, c * 64:(c + 1) * 64], ALU.mult),
                    sems.c_ve)
                sect(2)
                tape.wait("scalar", sems.d_sync, t_ac)
                t_eb = tape.emit("scalar", lambda e, c=c: e.activation(
                    eb32[:, :], ac32[:, c * 64 + 63:c * 64 + 64], AF.Exp), sems.c_act)
                tape.wait("vector", sems.c_act, t_eb)
                t_ebc = tape.emit("vector", lambda e: e.tensor_copy(
                    ebc64[:, :], eb32[:, 0:1].broadcast_to([32, 64])), sems.c_ve)
                sect(3)
                psum_wait("tensor", 1)
                tape.wait("tensor", sems.d_sync, t_c0)
                t_G = None
                for half in range(2):
                    crep = cfm0[:, c64][:, None, :].broadcast_to([64, 8, 64])
                    t_G = tape.emit("tensor", lambda e, crep=crep, c64=c64, half=half: e.matmul(
                        PS[1][0:64, half * 512:(half + 1) * 512], bfm0[:, c64], crep,
                        start=True, stop=True), sems.c_pe)

                for hg in range(2):
                    hoff0 = hg * 1024
                    sect(3)
                    psum_wait("tensor", 0)
                    tape.wait("tensor", sems.d_sync, t_ln)
                    t_E = None
                    for half in range(2):
                        hs = slice(half * 512, (half + 1) * 512)
                        hoff = hoff0 + half * 512
                        for oi, (lt, rt) in enumerate(((ahi, IndN), (alo, IndN), (lnd, Ind))):
                            tape.emit("tensor", lambda e, lt=lt, rt=rt, c=c, hs=hs, hoff=hoff,
                                      oi=oi: e.matmul(
                                PS[0][0:64, hs], lt[:, c * 64:(c + 1) * 64],
                                rt[:, hoff:hoff + 512],
                                start=(oi == 0), stop=False), None)
                        tape.emit("tensor", lambda e, hs=hs, hoff=hoff: e.matmul(
                            PS[0][0:64, hs], ident[0:64, 0:64], maskc[:, hoff:hoff + 512],
                            start=False, stop=False), None)
                        if half == 0 and hoff0 == 0:
                            tape.wait("tensor", sems.d_sync, t_af)
                        tape.emit("tensor", lambda e, hs=hs, par=par, hoff=hoff: e.matmul(
                            PS[0][0:64, hs], onr[0:1, 0:64],
                            afh[0:1, par, hoff:hoff + 512],
                            start=False, stop=False), None)
                        t_E = tape.emit("tensor", lambda e, hs=hs, par=par, hoff=hoff: e.matmul(
                            PS[0][0:64, hs], onr[0:1, 0:64],
                            afl[0:1, par, hoff:hoff + 512],
                            start=False, stop=True), sems.c_pe)
                        t_af_mm[par] = t_E
                    tape.wait("scalar", sems.c_pe, t_E)
                    t_eE = tape.emit("scalar", lambda e: e.activation(
                        eE[:, :], PS[0][0:64, :], AF.Exp), sems.c_act)
                    t_sfree[0] = (sems.c_act, t_eE)
                    sect(3)
                    psum_wait("tensor", 0)
                    t_D = None
                    for half in range(2):
                        hs = slice(half * 512, (half + 1) * 512)
                        hoff = hoff0 + half * 512
                        tape.emit("tensor", lambda e, c=c, hs=hs, hoff=hoff: e.matmul(
                            PS[0][0:64, hs], ahi[:, c * 64:(c + 1) * 64],
                            Ind[:, hoff:hoff + 512], start=True, stop=False), None)
                        t_D = tape.emit("tensor", lambda e, c=c, hs=hs, hoff=hoff: e.matmul(
                            PS[0][0:64, hs], alo[:, c * 64:(c + 1) * 64],
                            Ind[:, hoff:hoff + 512], start=False, stop=True), sems.c_pe)
                    tape.wait("scalar", sems.c_pe, t_D)
                    t_sdo = tape.emit("scalar", lambda e: e.activation(
                        sdo[:, :], PS[0][0:64, :], AF.Exp), sems.c_act)
                    t_sfree[0] = (sems.c_act, t_sdo)
                    sect(3)
                    tape.wait("vector", sems.c_pe, t_G)
                    tape.wait("vector", sems.c_act, t_eE)
                    t_P = tape.emit("vector", lambda e: e.tensor_tensor(
                        Pss[:, :], PS[1][0:64, 0:1024], eE[:, :], ALU.mult), sems.c_ve)
                    if hg == 1:
                        t_sfree[1] = (sems.c_ve, t_P)
                    sect(4)
                    psum_wait("tensor", 2)
                    tape.wait("tensor", sems.c_ve, t_P)
                    tape.wait("tensor", sems.c_ve, t_xt_done)
                    t_yd = None
                    for i in range(16):
                        h = hg * 16 + i
                        hfc, hcol = h // 2, (h % 2) * 64
                        t_yd = tape.emit("tensor", lambda e, i=i, hfc=hfc, hcol=hcol,
                                         par=par: e.matmul(
                            PS[2][0:64, i * 64:(i + 1) * 64], Pss[:, i * 64:(i + 1) * 64],
                            xtc[par][:, hfc, hcol:hcol + 64],
                            start=(i in (0, 8)), stop=(i == 15)),
                            sems.c_pe if i == 15 else None)
                    sect(4)
                    psum_wait("tensor", 3)
                    t_yo = None
                    for i in range(16):
                        h = hg * 16 + i
                        t_yo = tape.emit("tensor", lambda e, i=i, h=h, c64=c64: e.matmul(
                            PS[3][0:64, i * 64:(i + 1) * 64], cfm0[:, c64],
                            Sbf[:, h, :], start=(i in (0, 8)), stop=(i == 15)),
                            sems.c_pe if i == 15 else None)
                    tape.wait("vector", sems.c_pe, t_yo)
                    tape.wait("vector", sems.c_act, t_sdo)
                    t_t2 = tape.emit("vector", lambda e: e.tensor_tensor(
                        t2[:, :], PS[3][0:64, :], sdo[:, :], ALU.mult), sems.c_ve)
                    t_sfree[3] = (sems.c_ve, t_t2)
                    tape.wait("vector", sems.c_pe, t_yd)
                    t_y = tape.emit("vector", lambda e, hg=hg: e.tensor_tensor(
                        ybf[:, hg, :], PS[2][0:64, :], t2[:, :], ALU.add), sems.c_ve)
                    t_sfree[2] = (sems.c_ve, t_y)
                    sect(5)
                    psum_wait("tensor", 0)
                    tape.wait("tensor", sems.c_ve, t_wd)
                    t_wb = None
                    for half in range(2):
                        hs = slice(half * 512, (half + 1) * 512)
                        hoff = hoff0 + half * 512
                        t_wb = tape.emit("tensor", lambda e, hs=hs, hoff=hoff: e.matmul(
                            PS[0][0:64, hs], wdbf[:, :], Ind[:, hoff:hoff + 512],
                            start=True, stop=True), sems.c_pe)
                    tape.wait("vector", sems.c_pe, t_wb)
                    xts = xtc[par][:, hg * 8:(hg + 1) * 8, :].rearrange("p a b -> p (a b)")
                    t_xw = tape.emit("vector", lambda e, xts=xts: e.tensor_tensor(
                        xw[:, :], PS[0][0:64, :], xts, ALU.mult), sems.c_ve)
                    t_sfree[0] = (sems.c_ve, t_xw)
                    sect(5)
                    psum_wait("tensor", 0)
                    tape.wait("tensor", sems.c_ve, t_xw)
                    tape.wait("tensor", sems.c_ve, t_btc)
                    t_st = None
                    for half in range(2):
                        hs = slice(half * 512, (half + 1) * 512)
                        t_st = tape.emit("tensor", lambda e, hs=hs: e.matmul(
                            PS[0][0:64, hs], btok[:, :], xw[:, hs],
                            start=True, stop=True), sems.c_pe)
                    sect(5)
                    psum_wait("tensor", 3)
                    tape.wait("tensor", sems.c_ve, t_ebc)
                    t_eB = None
                    for half in range(2):
                        hs = slice(half * 512, (half + 1) * 512)
                        hoff = hoff0 + half * 512
                        t_eB = tape.emit("tensor", lambda e, hs=hs, hoff=hoff: e.matmul(
                            PS[3][0:64, hs], ebc64[:, :], Ind[:, hoff:hoff + 512],
                            start=True, stop=True), sems.c_pe)
                    sect(5)
                    tape.wait("vector", sems.c_pe, t_eB)
                    Sv = S[:, hg * 16:(hg + 1) * 16, :].rearrange("p a b -> p (a b)")
                    t_t3 = tape.emit("vector", lambda e, Sv=Sv: e.tensor_tensor(
                        t3[:, :], Sv, PS[3][0:64, :], ALU.mult), sems.c_ve)
                    t_sfree[3] = (sems.c_ve, t_t3)
                    tape.wait("vector", sems.c_pe, t_st)
                    t_Snew = tape.emit("vector", lambda e, Sv=Sv: e.tensor_tensor(
                        Sv, t3[:, :], PS[0][0:64, :], ALU.add), sems.c_ve)
                    t_sfree[0] = (sems.c_ve, t_Snew)
                    Sbv = Sbf[:, hg * 16:(hg + 1) * 16, :].rearrange("p a b -> p (a b)")
                    t_Snew = tape.emit("vector", lambda e, Sv=Sv, Sbv=Sbv: e.tensor_copy(
                        Sbv, Sv), sems.c_ve)
                t_xtc_free[par] = (sems.c_ve, t_Snew)

                sect(6)
                for hg in range(2):
                    psum_wait("tensor", 2)
                    t_tr = None
                    for i in range(8):
                        t_tr = tape.emit("tensor", lambda e, hg=hg, i=i: e.transpose(
                            psYT[:, i * 64:(i + 1) * 64], ybf[0:64, hg, i * 128:(i + 1) * 128],
                            ident[0:64, 0:64]), sems.c_pe if i == 7 else None)
                    tape.wait("vector", sems.c_pe, t_tr)
                    t_yev = tape.emit("vector", lambda e, hg=hg, c=c: e.tensor_copy(
                        ysl[:, hg * 8:(hg + 1) * 8, c * 64:(c + 1) * 64],
                        psYT[:, 0:512].rearrange("p (a b) -> p a b", a=8)), sems.c_ve)
                    t_sfree[2] = (sems.c_ve, t_yev)

            sect(7)
            tape.wait("vector", sems.d_sync, t_xf)
            tape.wait("vector", sems.d_sync, t_c)
            for fcc in range(16):
                tape.emit("vector", lambda e, fcc=fcc: e.scalar_tensor_tensor(
                    ysl[:, fcc, :], xf16[:, fcc, :], Drep[:, fcc:fcc + 1], ysl[:, fcc, :],
                    ALU.mult, ALU.add), None)
            tape.wait("scalar", sems.d_sync, t_z)
            t_g2 = {}
            for fcc in range(16):
                if fcc >= 2:
                    tape.wait("scalar", sems.c_ve, t_g2[fcc - 2])
                t_sza = tape.emit("scalar", lambda e, fcc=fcc: e.activation(
                    szst[:, fcc % 2, :], z16[:, fcc, :], AF.Silu), sems.c_act)
                tape.wait("vector", sems.c_act, t_sza)
                t_g2[fcc] = tape.emit("vector", lambda e, fcc=fcc: e.tensor_tensor(
                    ysl[:, fcc, :], ysl[:, fcc, :], szst[:, fcc % 2, :], ALU.mult),
                    sems.c_ve)
            psum_wait("tensor", 1)
            t_rmm = None
            t_sq15 = None
            for fcc in range(16):
                tape.wait("scalar", sems.c_ve, t_g2[fcc])
                if fcc >= 2:
                    tape.wait("scalar", sems.c_pe, t_rmm)
                t_sq15 = tape.emit("scalar", lambda e, fcc=fcc: e.activation(
                    sqst[:, fcc % 2, :], ysl[:, fcc, :], AF.Square), sems.c_act)
                tape.wait("tensor", sems.c_act, t_sq15)
                t_rmm = tape.emit("tensor", lambda e, fcc=fcc: e.matmul(
                    PS[1][0:1, 0:512], onc[:, 0:1], sqst[:, fcc % 2, :],
                    start=(fcc == 0), stop=(fcc == 15)), sems.c_pe)
            tape.wait("vector", sems.c_pe, t_rmm)
            t_r1 = tape.emit("vector", lambda e: e.tensor_scalar(
                rrow[:, 0, :], PS[1][0:1, 0:512], 1.0 / DIN, EPS_RMS, ALU.mult, ALU.add),
                sems.c_ve)
            tape.wait("scalar", sems.c_ve, t_r1)
            tape.emit("scalar", lambda e: e.activation(
                rrow[:, 1, :], rrow[:, 0, :], AF.Ln), None)
            t_rst = tape.emit("scalar", lambda e: e.activation(
                rrow[:, 0, :], rrow[:, 1, :], AF.Exp, scale=-0.5), sems.c_act)
            tape.wait("vector", sems.c_act, t_rst)
            t_rbfc = tape.emit("vector", lambda e: e.tensor_copy(rbf[:, :], rrow[:, 0, :]),
                               sems.c_ve)
            tape.wait("tensor", sems.c_ve, t_rbfc)
            t_rb = tape.emit("tensor", lambda e: e.matmul(
                PS[1][:, 0:512], onr[0:1, :], rbf[:, :], start=True, stop=True), sems.c_pe)
            tape.wait("vector", sems.c_pe, t_rb)
            t_y3 = None
            for fcc in range(16):
                t_y3 = tape.emit("vector", lambda e, fcc=fcc: e.scalar_tensor_tensor(
                    y3[:, fcc, :], ysl[:, fcc, :], nrmw[:, fcc:fcc + 1], PS[1][:, 0:512],
                    ALU.mult, ALU.mult), sems.c_ve)
            t_sfree[1] = (sems.c_ve, t_y3)
            for dmc in range(8):
                psi = 2 if dmc % 2 == 0 else 3
                psum_wait("tensor", psi)
                tape.wait("tensor", sems.c_ve, t_y3)
                t_mm = None
                for fcc in range(16):
                    t_mm = tape.emit("tensor", lambda e, fcc=fcc, dmc=dmc, psi=psi: e.matmul(
                        PS[psi][:, 0:512], wob[:, fcc, dmc * 128:(dmc + 1) * 128],
                        y3[:, fcc, :], start=(fcc == 0), stop=(fcc == 15)),
                        sems.c_pe if fcc == 15 else None)
                t_y3_mm = t_mm
                tape.wait("vector", sems.c_pe, t_mm)
                tape.wait("vector", sems.d_sync, t_R)
                pj = (s, dmc - 2) if dmc >= 2 else ((s - 1, 6 + dmc) if s > 0 else None)
                if pj is not None and pj in t_ost:
                    tape.wait("vector", sems.d_sync, t_ost[pj])
                t_radd[(s, dmc)] = tape.emit("vector", lambda e, dmc=dmc, psi=psi: e.tensor_tensor(
                    ostage[:, dmc % 2, :], PS[psi][:, 0:512], Rs[:, dmc, :], ALU.add),
                    sems.c_ve)
                t_sfree[psi] = (sems.c_ve, t_radd[(s, dmc)])
                tape.wait("sync", sems.c_ve, t_radd[(s, dmc)])
                t_ost[(s, dmc)] = dma(
                    tape, "sync",
                    dram["scr_R"].ap()[dmc * 128:(dmc + 1) * 128, sl],
                    ostage[:, dmc % 2, :], sems)
        sect(0)
        tape.wait("sync", sems.d_sync, sems.d_sync.n)
        tape.run_block("c2")



def phase_finish(nc, sems, dram):
    tape = Tape(nc)
    t = dma(tape, "sync", dram["out"].ap(), dram["scr_R"].ap(), sems)
    tape.wait("sync", sems.d_sync, t)
    tape.run_block("fin")


def build_nc(n_sub=4):
    from contextlib import ExitStack
    nc = bass.Bass("TRN2", target_bir_lowering=False)
    dram = {}
    dram["xT"] = nc.dram_tensor("xT", [DM, SEQ], F32, kind="ExternalInput")
    dram["wA"] = nc.dram_tensor("wA", [DM, 5120], F32, kind="ExternalInput")
    dram["bA"] = nc.dram_tensor("bA", [128, 40], F32, kind="ExternalInput")
    dram["w_out_a"] = nc.dram_tensor("w_out_a", [DM, DM], F32, kind="ExternalInput")
    dram["b_out_a"] = nc.dram_tensor("b_out_a", [128, 8], F32, kind="ExternalInput")
    dram["gbias"] = nc.dram_tensor("gbias", [128, 32], F32, kind="ExternalInput")
    dram["wbias"] = nc.dram_tensor("wbias", [128, 1], F32, kind="ExternalInput")
    dram["wF"] = nc.dram_tensor("wF", [DM, 3072], F32, kind="ExternalInput")
    dram["muF"] = nc.dram_tensor("muF", [128, 16], F32, kind="ExternalInput")
    dram["w_in"] = nc.dram_tensor("w_in", [DM, DINP], F32, kind="ExternalInput")
    dram["w_out_b"] = nc.dram_tensor("w_out_b", [DIN, DM], F32, kind="ExternalInput")
    dram["convw"] = nc.dram_tensor("convw", [128, 68], F32, kind="ExternalInput")
    dram["convb"] = nc.dram_tensor("convb", [128, 17], F32, kind="ExternalInput")
    dram["dtb"] = nc.dram_tensor("dtb", [32, 1], F32, kind="ExternalInput")
    dram["alog"] = nc.dram_tensor("alog", [32, 1], F32, kind="ExternalInput")
    dram["Drep"] = nc.dram_tensor("Drep", [128, 16], F32, kind="ExternalInput")
    dram["normw"] = nc.dram_tensor("normw", [128, 16], F32, kind="ExternalInput")
    dram["ident"] = nc.dram_tensor("ident", [128, 128], BF16, kind="ExternalInput")
    dram["ones_col"] = nc.dram_tensor("ones_col", [128, 1], BF16, kind="ExternalInput")
    dram["ones_row"] = nc.dram_tensor("ones_row", [1, 128], BF16, kind="ExternalInput")
    dram["Ind"] = nc.dram_tensor("Ind", [32, 2048], BF16, kind="ExternalInput")
    dram["IndN"] = nc.dram_tensor("IndN", [32, 2048], BF16, kind="ExternalInput")
    dram["maskc"] = nc.dram_tensor("maskc", [64, 2048], BF16, kind="ExternalInput")
    dram["maskw"] = nc.dram_tensor("maskw", [128, 128], BF16, kind="ExternalInput")
    dram["out"] = nc.dram_tensor("out", [DM, SEQ], F32, kind="ExternalOutput")
    for nm, shp, dt in [
        ("scr_R", [DM, SEQ], F32), ("scr_vfm", [DM, SEQ], BF16),
        ("scr_wq", [DM, SEQ], BF16), ("scr_wk", [DM, SEQ], BF16),
        ("scr_gq", [DM, SEQ], BF16), ("scr_gk", [DM, SEQ], BF16),
        ("scr_z", [DIN, SEQ], BF16), ("scr_xs", [DIN, SEQ], BF16),
        ("scr_bc", [128, SEQ], BF16), ("scr_dt", [32, SEQ], F32),
        ("scr_acum", [32, SEQ], F32), ("scr_ahi", [32, SEQ], BF16),
        ("scr_alo", [32, SEQ], BF16), ("scr_lnd", [32, SEQ], BF16),
    ]:
        dram[nm] = nc.dram_tensor(nm, shp, dt, kind="Internal")

    with ExitStack() as ctx:
        sems = Sems(nc, ctx)
        phase_a1(nc, sems, dram)
        if n_sub >= 1:
            phase_a2(nc, sems, dram)
        if n_sub >= 2:
            phase_ffn(nc, sems, dram, dram["scr_R"], dram["scr_R"], "b")
        if n_sub >= 2.5:
            phase_c1(nc, sems, dram)
        if n_sub >= 3:
            phase_c2(nc, sems, dram)
        if n_sub >= 4:
            phase_ffn(nc, sems, dram, dram["scr_R"], dram["out"], "d")
        else:
            phase_finish(nc, sems, dram)
    return nc


def host_inputs(inputs, core):
    f32 = np.float32
    x = np.ascontiguousarray(inputs["L0_attn_a_x"][core].T).astype(f32, copy=False)
    d = {"xT": x}
    d["wA"] = np.ascontiguousarray(np.concatenate(
        [inputs["L0_attn_a_w_v"], inputs["L0_attn_a_w_wqk"], inputs["L0_attn_a_w_qk"]],
        axis=1)).astype(f32, copy=False)
    bcat = np.concatenate([inputs["L0_attn_a_b_v"], inputs["L0_attn_a_b_wqk"],
                           inputs["L0_attn_a_b_qk"]])
    d["bA"] = np.ascontiguousarray(bcat.reshape(40, 128).T).astype(f32, copy=False)
    d["w_out_a"] = np.ascontiguousarray(inputs["L0_attn_a_w_out"]).astype(f32, copy=False)
    d["b_out_a"] = np.ascontiguousarray(
        inputs["L0_attn_a_b_out"].reshape(8, 128).T).astype(f32, copy=False)
    pe = np.asarray(inputs["L0_attn_a_pos_embedding"])
    g = np.empty(SEQ, dtype=f32)
    g[0] = pe[4094]
    g[1:] = pe[::-1]
    d["gbias"] = np.ascontiguousarray(g.reshape(32, 128).T).astype(f32, copy=False)
    wpe = np.asarray(inputs["L0_attn_a_w_pos_embedding"])
    wb = wpe[63::-1]
    d["wbias"] = np.ascontiguousarray(np.tile(wb, 2).reshape(128, 1)).astype(f32, copy=False)
    d["wF"] = np.ascontiguousarray(np.concatenate(
        [inputs["L0_ffn_W_r"].T, inputs["L0_ffn_W_k"].T, inputs["L0_ffn_W_v"].T],
        axis=1)).astype(f32, copy=False)
    mu = np.concatenate([inputs["L0_ffn_mu_r"], inputs["L0_ffn_mu_k"]])
    d["muF"] = np.ascontiguousarray(mu.reshape(16, 128).T).astype(f32, copy=False)
    d["w_in"] = np.ascontiguousarray(inputs["L0_attn_b_W_in"].T).astype(f32, copy=False)
    d["w_out_b"] = np.ascontiguousarray(inputs["L0_attn_b_W_out"].T).astype(f32, copy=False)
    cwf = np.asarray(inputs["L0_attn_b_conv_w"]).reshape(CONVD, 4)
    d["convw"] = np.ascontiguousarray(
        cwf.reshape(17, 128, 4).transpose(1, 0, 2).reshape(128, 68)).astype(f32, copy=False)
    d["convb"] = np.ascontiguousarray(
        np.asarray(inputs["L0_attn_b_conv_b"]).reshape(17, 128).T).astype(f32, copy=False)
    d["dtb"] = np.ascontiguousarray(
        np.asarray(inputs["L0_attn_b_dt_bias"]).reshape(32, 1)).astype(f32, copy=False)
    d["alog"] = np.ascontiguousarray(
        np.asarray(inputs["L0_attn_b_A_log"]).reshape(32, 1)).astype(f32, copy=False)
    d["Drep"] = np.ascontiguousarray(np.repeat(
        np.asarray(inputs["L0_attn_b_D"]), 64).reshape(16, 128).T).astype(f32, copy=False)
    d["normw"] = np.ascontiguousarray(
        np.asarray(inputs["L0_attn_b_norm_w"]).reshape(16, 128).T).astype(f32, copy=False)
    d["ident"] = np.eye(128, dtype=BF16NP)
    d["ones_col"] = np.ones((128, 1), dtype=BF16NP)
    d["ones_row"] = np.ones((1, 128), dtype=BF16NP)
    ind = np.repeat(np.eye(32, dtype=np.float32), 64, axis=1)
    d["Ind"] = ind.astype(BF16NP)
    d["IndN"] = (-ind).astype(BF16NP)
    ll = np.arange(64)
    mask = np.where(ll[None, :] < np.arange(64)[:, None], np.float32(-1e30), np.float32(0.0))
    d["maskc"] = np.ascontiguousarray(np.tile(mask, (1, 32))).astype(BF16NP)
    mw = np.zeros((128, 128), dtype=np.float32)
    mw[0:64, 0:64] = 1.0
    mw[64:128, 64:128] = 1.0
    d["maskw"] = mw.astype(BF16NP)
    return d


_NC_CACHE = {}


def kernel(_n_sub=4, **inputs):
    if _n_sub not in _NC_CACHE:
        _NC_CACHE[_n_sub] = build_nc(_n_sub)
    nc = _NC_CACHE[_n_sub]
    in_maps = [host_inputs(inputs, core) for core in range(8)]
    res = run_bass_kernel_spmd(nc, in_maps, core_ids=list(range(8)))
    out = np.stack([np.asarray(r["out"]).T for r in res.results], axis=0)
    return np.ascontiguousarray(out.astype(np.float32))


# version 04 (final)
# speedup vs baseline: 1.2422x; 1.0355x over previous
import numpy as np
import ml_dtypes
import concourse.bass as bass
import concourse.mybir as mybir
from concourse.bass_utils import run_bass_kernel_spmd

F32 = mybir.dt.float32
BF16 = mybir.dt.bfloat16
U8 = mybir.dt.uint8
ALU = mybir.AluOpType
AF = mybir.ActivationFunctionType
BF16NP = ml_dtypes.bfloat16

SEQ = 4096
DM = 1024
SLAB = 512
NSLAB = SEQ // SLAB
NH_A = 16
HD = 64
DIN = 2048
NH_B = 32
DS = 64
CONVD = DIN + 2 * DS
DINP = 2 * DIN + 2 * DS + NH_B
EPS_LN = 1e-5
EPS_RMS = 1e-5

ENGINES = ("sync", "gpsimd", "scalar", "vector", "tensor")


class Ctr:
    def __init__(self, h):
        self.h = h
        self.n = 0

    def inc(self, k):
        self.n += k
        return self.n


class Tape:

    def __init__(self, nc):
        self.nc = nc
        self.streams = {e: [] for e in ENGINES}
        self.mute = False

    def emit(self, eng, fn, inc=None, k=1):
        if self.mute:
            return inc.n if inc is not None else None
        if inc is not None:
            t = inc.inc(k)
            self.streams[eng].append(lambda e, fn=fn, h=inc.h, k=k: fn(e).then_inc(h, k))
            return t
        self.streams[eng].append(fn)
        return None

    def wait(self, eng, ctr, val):
        if self.mute:
            return
        if val is None or val <= 0:
            return
        self.streams[eng].append(lambda e, h=ctr.h, v=val: e.wait_ge(h, v))

    def run_block(self, name=None):
        nc = self.nc
        with nc.Block(name) as blk:
            for ename in ENGINES:
                fns = self.streams[ename]
                if not fns:
                    continue
                dec = getattr(blk, ename)

                def body(eng, fns=fns):
                    for f in fns:
                        f(eng)

                dec(body)


class Sems:
    def __init__(self, nc, ctx):
        self.d_sync = Ctr(ctx.enter_context(nc.semaphore("d_sync")))
        self.d_gp = Ctr(ctx.enter_context(nc.semaphore("d_gp")))
        self.d_sca = Ctr(ctx.enter_context(nc.semaphore("d_sca")))
        self.c_pe = Ctr(ctx.enter_context(nc.semaphore("c_pe")))
        self.c_ve = Ctr(ctx.enter_context(nc.semaphore("c_ve")))
        self.c_act = Ctr(ctx.enter_context(nc.semaphore("c_act")))
        self.c_gp = Ctr(ctx.enter_context(nc.semaphore("c_gp")))

    def dma_ctr(self, eng):
        return {"sync": self.d_sync, "gpsimd": self.d_gp, "scalar": self.d_sca}[eng]


def dma(tape, eng, out_ap, in_ap, sems):
    ctr = sems.dma_ctr(eng)
    return tape.emit(eng, lambda e, o=out_ap, i=in_ap: e.dma_start(o, i), ctr, 16)


def _lp(nc, fn):
    with nc.allow_low_precision(reason="bf16 reciprocal feeds bf16 matmul broadcast"):
        return fn()


def carve(arena, off_bytes, shape, dtype):
    nel = int(np.prod(shape))
    nb = nel * mybir.dt.size(dtype)
    ap = arena[:, off_bytes:off_bytes + nb].bitcast(dtype)
    if len(shape) > 1:
        dims = " ".join(f"d{i}" for i in range(len(shape)))
        ap = ap.rearrange(f"p ({dims}) -> p {dims}",
                          **{f"d{i}": shape[i] for i in range(len(shape))})
    return ap



def ln_emit(tape, sems, src3, xn3, ln, nch=8, cols=SLAB, xn_off=0):
    sq, xc = ln["sq_bf"], ln["xc_bf"]
    t_sq = t_xc = None
    for c in range(nch):
        t_sq = tape.emit("scalar", lambda e, c=c: e.activation(
            sq[:, c, 0:cols], src3[:, c, 0:cols], AF.Square), sems.c_act)
        t_xc = tape.emit("vector", lambda e, c=c: e.tensor_copy(
            xc[:, c, 0:cols], src3[:, c, 0:cols]), sems.c_ve)
    tape.wait("tensor", sems.c_act, t_sq)
    tape.wait("tensor", sems.c_ve, t_xc)
    t_mm = None
    for c in range(nch):
        tape.emit("tensor", lambda e, c=c: e.matmul(
            ln["ps_s"][:, 0:cols], ln["ones_col"], xc[:, c, 0:cols],
            start=(c == 0), stop=(c == nch - 1)), None)
        t_mm = tape.emit("tensor", lambda e, c=c: e.matmul(
            ln["ps_q"][:, 0:cols], ln["ones_col"], sq[:, c, 0:cols],
            start=(c == 0), stop=(c == nch - 1)), sems.c_pe)
    inv = 1.0 / (nch * 128)
    tape.wait("vector", sems.c_pe, t_mm)
    tape.emit("vector", lambda e: e.tensor_scalar(
        ln["m_row"][:, 0:cols], ln["ps_s"][:, 0:cols], inv, None, ALU.mult), None)
    tape.emit("vector", lambda e: e.tensor_scalar(
        ln["a_row"][:, 0:cols], ln["ps_q"][:, 0:cols], inv, EPS_LN, ALU.mult, ALU.add), None)
    tape.emit("vector", lambda e: e.tensor_tensor(
        ln["msq"][:, 0:cols], ln["m_row"][:, 0:cols], ln["m_row"][:, 0:cols], ALU.mult), None)
    t_v = tape.emit("vector", lambda e: e.tensor_tensor(
        ln["v_row"][:, 0:cols], ln["a_row"][:, 0:cols], ln["msq"][:, 0:cols],
        ALU.subtract), sems.c_ve)
    tape.wait("scalar", sems.c_ve, t_v)
    tape.emit("scalar", lambda e: e.activation(
        ln["lnr"][:, 0:cols], ln["v_row"][:, 0:cols], AF.Ln), None)
    t_r = tape.emit("scalar", lambda e: e.activation(
        ln["rstd_row"][:, 0:cols], ln["lnr"][:, 0:cols], AF.Exp, scale=-0.5), sems.c_act)
    tape.wait("vector", sems.c_act, t_r)
    tape.emit("vector", lambda e: e.tensor_copy(
        ln["m_bf"][:, 0:cols], ln["m_row"][:, 0:cols]), None)
    t_cast = tape.emit("vector", lambda e: e.tensor_copy(
        ln["rstd_bf"][:, 0:cols], ln["rstd_row"][:, 0:cols]), sems.c_ve)
    tape.wait("tensor", sems.c_ve, t_cast)
    tape.emit("tensor", lambda e: e.matmul(
        ln["ps_mb"][:, 0:cols], ln["ones_row"], ln["m_bf"][:, 0:cols],
        start=True, stop=True), None)
    t_b = tape.emit("tensor", lambda e: e.matmul(
        ln["ps_rb"][:, 0:cols], ln["ones_row"], ln["rstd_bf"][:, 0:cols],
        start=True, stop=True), sems.c_pe)
    tape.wait("vector", sems.c_pe, t_b)
    t_ve = None
    for c in range(nch):
        tape.emit("vector", lambda e, c=c: e.tensor_tensor(
            ln["tmp32"][:, 0:cols], src3[:, c, 0:cols], ln["ps_mb"][:, 0:cols],
            ALU.subtract), None)
        t_ve = tape.emit("vector", lambda e, c=c: e.tensor_tensor(
            xn3[:, c, xn_off:xn_off + cols], ln["tmp32"][:, 0:cols],
            ln["ps_rb"][:, 0:cols], ALU.mult), sems.c_ve)
    return {"ve": t_ve, "sq": t_sq}


def make_ln_tiles(nc, ctx, pfx, nch=8, cols=SLAB):
    sq = ctx.enter_context(nc.sbuf_tensor(pfx + "_sq", [128, nch, cols], BF16))
    xc = ctx.enter_context(nc.sbuf_tensor(pfx + "_xc", [128, nch, cols], BF16))
    rows = ctx.enter_context(nc.sbuf_tensor(pfx + "_rows", [1, 6, cols], F32))
    rbf = ctx.enter_context(nc.sbuf_tensor(pfx + "_rbf", [1, 2, cols], BF16))
    tmp32 = ctx.enter_context(nc.sbuf_tensor(pfx + "_tmp32", [128, cols], F32))
    oc = ctx.enter_context(nc.sbuf_tensor(pfx + "_onc", [128, 1], BF16))
    orow = ctx.enter_context(nc.sbuf_tensor(pfx + "_onr", [1, 128], BF16))
    ps_s = ctx.enter_context(nc.psum_tensor(pfx + "_pss", [1, cols], F32))
    ps_q = ctx.enter_context(nc.psum_tensor(pfx + "_psq", [1, cols], F32))
    ps_mb = ctx.enter_context(nc.psum_tensor(pfx + "_psmb", [128, cols], F32))
    ps_rb = ctx.enter_context(nc.psum_tensor(pfx + "_psrb", [128, cols], F32))
    return dict(sq_bf=sq, xc_bf=xc,
                m_row=rows[:, 0, :], a_row=rows[:, 1, :], msq=rows[:, 2, :],
                v_row=rows[:, 3, :], lnr=rows[:, 4, :], rstd_row=rows[:, 5, :],
                m_bf=rbf[:, 0, :], rstd_bf=rbf[:, 1, :], tmp32=tmp32,
                ones_col=oc[:, 0:1], ones_row=orow[0:1, :],
                ps_s=ps_s, ps_q=ps_q, ps_mb=ps_mb, ps_rb=ps_rb,
                ones_col_t=oc, ones_row_t=orow)


def load_ln_consts(tape, sems, ln, dram):
    t = dma(tape, "sync", ln["ones_col_t"][:, :], dram["ones_col"].ap(), sems)
    t = dma(tape, "sync", ln["ones_row_t"][:, :], dram["ones_row"].ap(), sems)
    return t



def phase_a1(nc, sems, dram):
    tape = Tape(nc)
    from contextlib import ExitStack
    with ExitStack() as ctx:
        xslab = ctx.enter_context(nc.sbuf_tensor("a1_xslab", [128, 8, SLAB], F32))
        xn = ctx.enter_context(nc.sbuf_tensor("a1_xn", [128, 8, SLAB], BF16))
        wbuf = ctx.enter_context(nc.sbuf_tensor("a1_wbuf", [128, 2, 8, 2560], BF16))
        bA = ctx.enter_context(nc.sbuf_tensor("a1_bA", [128, 40], F32))
        stage = ctx.enter_context(nc.sbuf_tensor("a1_stage", [128, 2, SLAB], BF16))
        ln = make_ln_tiles(nc, ctx, "a1")
        ps_p = [ctx.enter_context(nc.psum_tensor("a1_pp0", [128, SLAB], F32)),
                ctx.enter_context(nc.psum_tensor("a1_pp1", [128, SLAB], F32))]

        t_c = load_ln_consts(tape, sems, ln, dram)
        t_c = dma(tape, "sync", bA[:, :], dram["bA"].ap(), sems)

        spill_dst = []
        for j in range(40):
            if j < 8:
                spill_dst.append((dram["scr_vfm"], j * 128))
            elif j < 16:
                spill_dst.append((dram["scr_wq"], (j - 8) * 128))
            elif j < 24:
                spill_dst.append((dram["scr_wk"], (j - 16) * 128))
            elif j < 32:
                spill_dst.append((dram["scr_gq"], (j - 24) * 128))
            else:
                spill_dst.append((dram["scr_gk"], (j - 32) * 128))

        xT_r = dram["xT"].ap().rearrange("(c p) t -> p c t", p=128)
        wA_r = dram["wA"].ap().rearrange("(c p) j -> p c j", p=128)

        t_w = [None, None]
        for hlf in range(2):
            t_w[hlf] = dma(tape, "gpsimd", wbuf[:, hlf, :, :],
                           wA_r[:, :, hlf * 2560:(hlf + 1) * 2560], sems)
        t_spill = {}
        t_evac = {}
        prev_ln = None
        t_half_mm = [None, None]
        for s in range(NSLAB):
            if prev_ln is not None:
                tape.wait("sync", sems.c_ve, prev_ln["ve"])
                tape.wait("sync", sems.c_act, prev_ln["sq"])
            t_x = dma(tape, "sync", xslab[:, :, :], xT_r[:, :, s * SLAB:(s + 1) * SLAB], sems)
            tape.wait("scalar", sems.d_sync, t_x)
            tape.wait("vector", sems.d_sync, t_x)
            prev_ln = ln_emit(tape, sems, xslab, xn, ln)
            for j in range(40):
                hlf, jq = j // 20, j % 20
                ps = ps_p[j % 2]
                tape.wait("tensor", sems.d_gp, t_w[hlf])
                tape.wait("tensor", sems.c_ve, prev_ln["ve"])
                if (s, j) >= (0, 2):
                    pj = (s, j - 2) if j >= 2 else (s - 1, 38 + j)
                    tape.wait("tensor", sems.c_act, t_evac[pj])
                t_mm = None
                for c in range(8):
                    t_mm = tape.emit("tensor", lambda e, c=c, jq=jq, hlf=hlf, ps=ps: e.matmul(
                        ps[:, :], wbuf[:, hlf, c, jq * 128:(jq + 1) * 128],
                        xn[:, c, :], start=(c == 0), stop=(c == 7)),
                        sems.c_pe if c == 7 else None)
                t_half_mm[hlf] = t_mm
                tape.wait("scalar", sems.c_pe, t_mm)
                tape.wait("scalar", sems.d_sync, t_c)
                pj = None
                if j >= 2:
                    pj = (s, j - 2)
                elif s > 0:
                    pj = (s - 1, 38 + j)
                if pj is not None and pj in t_spill:
                    tape.wait("scalar", sems.d_sync, t_spill[pj])
                t_evac[(s, j)] = tape.emit("scalar", lambda e, j=j, ps=ps: e.activation(
                    stage[:, j % 2, :], ps[:, :], AF.Identity, bias=bA[:, j:j + 1]), sems.c_act)
                scr, row = spill_dst[j]
                tape.wait("sync", sems.c_act, t_evac[(s, j)])
                t_spill[(s, j)] = dma(
                    tape, "sync", scr.ap()[row:row + 128, s * SLAB:(s + 1) * SLAB],
                    stage[:, j % 2, :], sems)
        tape.wait("sync", sems.d_sync, sems.d_sync.n)
        tape.run_block("a1")



A2_PARTS = "all"


def phase_a2(nc, sems, dram):
    parts = A2_PARTS
    tape = Tape(nc)
    from contextlib import ExitStack
    with ExitStack() as ctx:
        vt = ctx.enter_context(nc.sbuf_tensor("a2_vt", [128, 32, NH_A * 65], BF16))
        att = ctx.enter_context(nc.sbuf_tensor("a2_att", [128, 8, SEQ], BF16))
        arena = ctx.enter_context(nc.sbuf_tensor("a2_arena", [128, 36864], U8))
        vch = carve(arena, 0, [8, 128], BF16)
        qp = carve(arena, 0, [SEQ], BF16)
        kp = carve(arena, 8192, [SEQ], BF16)
        wqp = carve(arena, 16384, [SEQ], BF16)
        wkp = carve(arena, 24576, [SEQ], BF16)
        woa = carve(arena, 0, [8, 1024], BF16)
        xres = carve(arena, 16384, [8, SLAB], F32)
        ostage = carve(arena, 32768, [2, SLAB], F32)

        ptile = ctx.enter_context(nc.sbuf_tensor("a2_p", [128, 2, 2, 1024], BF16))
        wout = ctx.enter_context(nc.sbuf_tensor("a2_wout", [64, 2, SEQ], BF16))
        gb = ctx.enter_context(nc.sbuf_tensor("a2_gb", [128, 32], F32))
        wb = ctx.enter_context(nc.sbuf_tensor("a2_wb", [128, 1], F32))
        rden = ctx.enter_context(nc.sbuf_tensor("a2_rden", [65, 2048], BF16))
        rbc = ctx.enter_context(nc.sbuf_tensor("a2_rbc", [64, 2, 1024], BF16))
        pv32 = ctx.enter_context(nc.sbuf_tensor("a2_pv32", [64, 1024], F32))
        obf = ctx.enter_context(nc.sbuf_tensor("a2_obf", [64, 1024], BF16))
        ident = ctx.enter_context(nc.sbuf_tensor("a2_id", [128, 128], BF16))
        onesb = ctx.enter_context(nc.sbuf_tensor("a2_onesb", [128, 64], BF16))
        onr = ctx.enter_context(nc.sbuf_tensor("a2_onr", [1, 128], BF16))
        boa = ctx.enter_context(nc.sbuf_tensor("a2_boa", [128, 8], F32))
        psA = ctx.enter_context(nc.psum_tensor("a2_psA", [128, 1024], F32))
        psB = ctx.enter_context(nc.psum_tensor("a2_psB", [128, 1024], F32))
        ps_v = ctx.enter_context(nc.psum_tensor("a2_ps_v", [65, 1024], F32))
        psW2 = ctx.enter_context(nc.psum_tensor("a2_psW2", [128, 1024], F32))
        ps_w = psW2[:, 0:512]
        ps_vw = psW2[0:65, 512:1024]

        maskw = ctx.enter_context(nc.sbuf_tensor("a2_mw", [128, 128], BF16))
        t_c = dma(tape, "sync", gb[:, :], dram["gbias"].ap(), sems)
        t_c = dma(tape, "sync", wb[:, :], dram["wbias"].ap(), sems)
        t_c = dma(tape, "sync", maskw[:, :], dram["maskw"].ap(), sems)
        t_c = dma(tape, "sync", ident[:, :], dram["ident"].ap(), sems)
        t_fill1 = tape.emit("gpsimd", lambda e: e.memset(onesb[:, :], 1.0), sems.c_gp)
        t_c = dma(tape, "sync", onr[:, :], dram["ones_row"].ap(), sems)

        t_fill = tape.emit("gpsimd", lambda e: e.memset(vt[:, :, :], 1.0), sems.c_gp)
        vfm_r = dram["scr_vfm"].ap().rearrange("(c p) t -> p c t", p=128)
        t_copy_bank = {0: None, 1: None}
        t_tr_last = None
        for tc in range(32):
            if t_tr_last is not None:
                tape.wait("sync", sems.c_pe, t_tr_last)
            t_vl = dma(tape, "sync", vch[:, :, :], vfm_r[:, :, tc * 128:(tc + 1) * 128], sems)
            tape.wait("tensor", sems.d_sync, t_vl)
            if tc == 0:
                tape.wait("tensor", sems.d_sync, t_c)
                tape.wait("tensor", sems.c_gp, t_fill)
            for fc in range(8):
                b = fc % 2
                psb = (psA if b == 0 else psB).bitcast(BF16)
                tape.wait("tensor", sems.c_ve, t_copy_bank[b])
                t_tr_last = tape.emit("tensor", lambda e, fc=fc, psb=psb: e.transpose(
                    psb[:, 0:128], vch[:, fc, :], ident[:, :]), sems.c_pe)
                tape.wait("vector", sems.c_pe, t_tr_last)
                h0, h1 = 2 * fc, 2 * fc + 1
                tape.emit("vector", lambda e, psb=psb, tc=tc, h0=h0: e.tensor_copy(
                    vt[:, tc, h0 * 65:h0 * 65 + 64], psb[:, 0:64]), None)
                t_copy_bank[b] = tape.emit("vector", lambda e, psb=psb, tc=tc, h1=h1: e.tensor_copy(
                    vt[:, tc, h1 * 65:h1 * 65 + 64], psb[:, 64:128]), sems.c_ve)
        t_vt_done = t_copy_bank[1]

        gq, gk = dram["scr_gq"].ap(), dram["scr_gk"].ap()
        wq_, wk_ = dram["scr_wq"].ap(), dram["scr_wk"].ap()
        t_att_dma = None
        t_obf_dma = None
        t_exp = {}
        t_pv32_rd = None
        t_accfree = {0: (None, None, None, None), 1: (None, None, None, None)}
        t_parity_pe = [t_tr_last, t_tr_last]
        t_ld = [None, None]
        for h in range(NH_A):
            fc = h // 2
            r0 = (h % 2) * 64
            rs = slice(r0, r0 + 64)
            tape.wait("sync", sems.c_pe, t_parity_pe[h % 2])
            dma(tape, "sync", qp[rs, :], gq[h * 64:h * 64 + 64, :], sems)
            dma(tape, "sync", kp[rs, :], gk[h * 64:h * 64 + 64, :], sems)
            dma(tape, "sync", wqp[rs, :], wq_[h * 64:h * 64 + 64, :], sems)
            t_ld[h % 2] = dma(tape, "sync", wkp[rs, :], wk_[h * 64:h * 64 + 64, :], sems)
            if h == 0:
                tape.wait("tensor", sems.c_gp, t_fill1)
                tape.wait("tensor", sems.c_ve, t_vt_done)

            tape.wait("tensor", sems.d_sync, t_ld[h % 2])
            t_wexp = None
            t_wpv = None
            t_wmask = None
            t_wout = None
            t_wpvc = None
            for wg in range(8):
                qw0 = wg * 512
                if t_wexp is not None:
                    tape.wait("tensor", sems.c_act, t_wexp)
                elif t_accfree[1][0] is not None:
                    tape.wait("tensor", t_accfree[1][0], t_accfree[1][1])
                    tape.wait("tensor", t_accfree[1][2], t_accfree[1][3])
                t_qkw = None
                for i in range(4):
                    d = wg * 4 + i
                    t_qkw = tape.emit("tensor", lambda e, d=d, i=i, rs=rs: e.matmul(
                        ps_w[:, i * 128:(i + 1) * 128],
                        wkp[rs, d * 128:(d + 1) * 128],
                        wqp[rs, d * 128:(d + 1) * 128],
                        start=(i == 0), stop=(i == 3)),
                        sems.c_pe if i == 3 else None)
                tape.wait("scalar", sems.c_pe, t_qkw)
                if t_wmask is not None:
                    tape.wait("scalar", sems.c_ve, t_wmask)
                t_wexp = tape.emit("scalar", lambda e: e.activation(
                    ptile[:, 1, 1, 0:512], ps_w[:, 0:512], AF.Exp,
                    bias=wb[:, 0:1], scale=0.125), sems.c_act)
                tape.wait("vector", sems.c_act, t_wexp)
                if t_wpv is not None:
                    tape.wait("vector", sems.c_pe, t_wpv)
                mwb = maskw[:, None, :].broadcast_to([128, 4, 128])
                t_wmask = tape.emit("vector", lambda e, mwb=mwb: e.tensor_tensor(
                    ptile[:, 1, 1, 0:512].rearrange("p (a b) -> p a b", a=4),
                    ptile[:, 1, 1, 0:512].rearrange("p (a b) -> p a b", a=4),
                    mwb, ALU.mult), sems.c_ve)
                tape.wait("tensor", sems.c_ve, t_wmask)
                if t_wpvc is not None:
                    tape.wait("tensor", sems.c_act, t_wpvc)
                    tape.wait("tensor", sems.c_ve, t_wrr)
                elif t_accfree[1][0] is not None:
                    tape.wait("tensor", t_accfree[1][0], t_accfree[1][1])
                    tape.wait("tensor", t_accfree[1][2], t_accfree[1][3])
                for i in range(4):
                    d = wg * 4 + i
                    t_wpv = tape.emit(
                        "tensor",
                        lambda e, d=d, h=h, i=i: e.matmul(
                            ps_vw[:, i * 128:i * 128 + 128],
                            vt[:, d, h * 65:h * 65 + 65],
                            ptile[:, 1, 1, i * 128:(i + 1) * 128],
                            start=(i == 0), stop=(i == 3)),
                        sems.c_pe if i == 3 else None)
                tape.wait("vector", sems.c_pe, t_wpv)
                t_wrr = tape.emit("vector", lambda e: _lp(nc, lambda: e.reciprocal(
                    rden[64:65, 0:512], ps_vw[64:65, 0:512])), sems.c_ve)
                tape.wait("sync", sems.c_ve, t_wrr)
                t_r1 = dma(tape, "sync", dram["scr_rr"].ap()[0:1, 0:512],
                           rden[64:65, 0:512], sems)
                tape.wait("sync", sems.d_sync, t_r1)
                t_rdma = dma(tape, "sync", rbc[:, 0, 0:512],
                             dram["scr_rr"].ap()[0:1, 0:512].broadcast_to([64, 512]), sems)
                tape.wait("scalar", sems.c_pe, t_wpv)
                tape.wait("scalar", sems.c_ve, t_pv32_rd)
                t_wpvc = tape.emit("scalar", lambda e: e.activation(
                    pv32[:, 0:512], ps_vw[0:64, 0:512], AF.Copy), sems.c_act)
                tape.wait("vector", sems.c_act, t_wpvc)
                tape.wait("vector", sems.d_sync, t_rdma)
                t_wout = tape.emit("vector", lambda e, qw0=qw0: e.tensor_tensor(
                    wout[:, 0, qw0:qw0 + 512], pv32[:, 0:512], rbc[:, 0, 0:512],
                    ALU.mult), sems.c_ve)
                t_pv32_rd = t_wout
            t_wout_done = t_wout

            for qb in range(4):
                q0 = qb * 1024
                acc = ps_v[:, :] if qb % 2 == 0 else psW2[0:65, :]
                ai = qb % 2
                t_qkd = {}
                for kc in range(33):
                    if kc < 32:
                        par = kc % 2
                        psS = psA if par == 0 else psB
                        if kc >= 2:
                            tape.wait("tensor", sems.c_act, t_exp.get(kc - 2))
                        t_qk = None
                        for nn in range(2):
                            t_qk = tape.emit("tensor", lambda e, kc=kc, q0=q0, nn=nn, rs=rs,
                                             psS=psS: e.matmul(
                                psS[:, nn * 512:(nn + 1) * 512],
                                kp[rs, kc * 128:(kc + 1) * 128],
                                qp[rs, q0 + nn * 512:q0 + (nn + 1) * 512],
                                start=True, stop=True),
                                sems.c_pe if nn == 1 else None)
                        t_qkd[kc] = t_qk
                    if kc >= 1:
                        kp_ = kc - 1
                        parp = kp_ % 2
                        tape.wait("tensor", sems.c_act, t_exp[kp_])
                        if kp_ == 0:
                            if ai == 1:
                                tape.wait("tensor", sems.c_act, t_wpvc)
                                tape.wait("tensor", sems.c_ve, t_wrr)
                            g = t_accfree[ai]
                            if g[0] is not None:
                                tape.wait("tensor", g[0], g[1])
                                tape.wait("tensor", g[2], g[3])
                        for nn in range(2):
                            t_pvk = tape.emit("tensor", lambda e, kp_=kp_, parp=parp, nn=nn,
                                              h=h, acc=acc: e.matmul(
                                acc[:, nn * 512:(nn + 1) * 512],
                                vt[:, kp_, h * 65:h * 65 + 65],
                                ptile[:, 0, parp, nn * 512:(nn + 1) * 512],
                                start=(kp_ == 0), stop=(kp_ == 31)),
                                sems.c_pe if nn == 1 else None)
                    if kc < 32:
                        tape.wait("scalar", sems.c_pe, t_qkd[kc])
                        t_exp[kc] = tape.emit("scalar", lambda e, kc=kc, par=kc % 2,
                                              psS=psA if kc % 2 == 0 else psB:
                                              e.activation(
                            ptile[:, 0, par, :], psS[:, :], AF.Exp, bias=gb[:, kc:kc + 1]),
                            sems.c_act)
                t_lpv = t_pvk
                tape.wait("vector", sems.c_pe, t_lpv)
                t_rr = tape.emit("vector", lambda e, acc=acc: _lp(nc, lambda: e.reciprocal(
                    rden[64:65, 1024:2048], acc[64:65, :])), sems.c_ve)
                tape.wait("sync", sems.c_ve, t_rr)
                t_r1 = dma(tape, "sync", dram["scr_rr"].ap()[1:2, :],
                           rden[64:65, 1024:2048], sems)
                tape.wait("sync", sems.d_sync, t_r1)
                t_rdma = dma(tape, "sync", rbc[:, 1, :],
                             dram["scr_rr"].ap()[1:2, :].broadcast_to([64, 1024]), sems)
                tape.wait("scalar", sems.c_pe, t_lpv)
                tape.wait("scalar", sems.c_ve, t_pv32_rd)
                if t_obf_dma is not None:
                    tape.wait("scalar", sems.d_sync, t_obf_dma)
                t_pvc = tape.emit("scalar", lambda e, acc=acc: e.activation(
                    pv32[:, :], acc[0:64, :], AF.Copy), sems.c_act)
                t_accfree[ai] = (sems.c_act, t_pvc, sems.c_ve, t_rr)
                tape.wait("vector", sems.c_act, t_pvc)
                tape.wait("vector", sems.d_sync, t_rdma)
                if t_obf_dma is not None:
                    tape.wait("vector", sems.d_sync, t_obf_dma)
                t_gout = tape.emit("vector", lambda e: e.tensor_tensor(
                    obf[:, :], pv32[:, :], rbc[:, 1, :], ALU.mult), sems.c_ve)
                t_pv32_rd = t_gout
                if h % 2 == 0:
                    t_aw = tape.emit("vector", lambda e, fc=fc, q0=q0: e.tensor_tensor(
                        att[0:64, fc, q0:q0 + 1024], obf[:, :],
                        wout[:, 0, q0:q0 + 1024], ALU.add), sems.c_ve)
                    t_obf_last_ve = t_aw
                else:
                    t_aw = tape.emit("vector", lambda e, q0=q0: e.tensor_tensor(
                        obf[:, :], obf[:, :], wout[:, 0, q0:q0 + 1024], ALU.add),
                        sems.c_ve)
                    tape.wait("sync", sems.c_ve, t_aw)
                    t_att_dma = dma(
                        tape, "sync", att[64:128, fc, q0:q0 + 1024], obf[:, :], sems)
                    t_obf_dma = t_att_dma
                    t_obf_last_ve = t_aw

            t_parity_pe[h % 2] = t_pvk
            t_sfree_ve = t_obf_last_ve
        t_last_pair_pe = t_pvk

        if parts != "all":
            tape.wait("sync", sems.d_sync, sems.d_sync.n)
            tape.run_block("a2")
            return
        t_att_done = t_sfree_ve
        tape.wait("gpsimd", sems.c_pe, t_last_pair_pe)
        t_woa = dma(tape, "gpsimd", woa[:, :, :],
                    dram["w_out_a"].ap().rearrange("(c p) j -> p c j", p=128), sems)
        t_boa = dma(tape, "sync", boa[:, :], dram["b_out_a"].ap(), sems)
        xT_r = dram["xT"].ap().rearrange("(c p) t -> p c t", p=128)
        t_oev = {}
        t_ost = {}
        t_radd = {}
        for s in range(NSLAB):
            if s > 0:
                tape.wait("sync", sems.c_ve, t_radd[(s - 1, 7)])
            else:
                tape.wait("sync", sems.c_pe, t_last_pair_pe)
                tape.wait("sync", sems.c_ve, t_att_done)
            t_xr = dma(tape, "sync", xres[:, :, :], xT_r[:, :, s * SLAB:(s + 1) * SLAB], sems)
            for dmc in range(8):
                ps = psA if dmc % 2 == 0 else psB
                pj = (s, dmc - 2) if dmc >= 2 else ((s - 1, 6 + dmc) if s > 0 else None)
                tape.wait("tensor", sems.d_gp, t_woa)
                if s == 0 and dmc == 0:
                    tape.wait("tensor", sems.c_ve, t_att_done)
                    for g in t_accfree.values():
                        if g[0] is not None:
                            tape.wait("tensor", g[0], g[1])
                            tape.wait("tensor", g[2], g[3])
                    tape.wait("tensor", sems.d_sync, t_att_dma)
                if pj is not None:
                    tape.wait("tensor", sems.c_ve, t_oev[pj])
                t_mm = None
                for c in range(8):
                    t_mm = tape.emit("tensor", lambda e, c=c, dmc=dmc, s=s, ps=ps: e.matmul(
                        ps[:, 0:512], woa[:, c, dmc * 128:(dmc + 1) * 128],
                        att[:, c, s * SLAB:(s + 1) * SLAB], start=(c == 0), stop=(c == 7)),
                        sems.c_pe if c == 7 else None)
                tape.wait("vector", sems.c_pe, t_mm)
                tape.wait("vector", sems.d_sync, t_xr)
                tape.wait("vector", sems.d_sync, t_boa)
                if pj is not None and pj in t_ost:
                    tape.wait("vector", sems.d_sync, t_ost[pj])
                t_radd[(s, dmc)] = tape.emit(
                    "vector", lambda e, ps=ps, dmc=dmc: e.scalar_tensor_tensor(
                        ostage[:, dmc % 2, :], ps[:, 0:512], boa[:, dmc:dmc + 1],
                        xres[:, dmc, :], ALU.add, ALU.add), sems.c_ve)
                t_oev[(s, dmc)] = t_radd[(s, dmc)]
                tape.wait("sync", sems.c_ve, t_radd[(s, dmc)])
                t_ost[(s, dmc)] = dma(
                    tape, "sync",
                    dram["scr_R"].ap()[dmc * 128:(dmc + 1) * 128, s * SLAB:(s + 1) * SLAB],
                    ostage[:, dmc % 2, :], sems)
        tape.wait("sync", sems.d_sync, sems.d_sync.n)
        tape.run_block("a2")



def phase_ffn(nc, sems, dram, src, dst, pfx):
    tape = Tape(nc)
    from contextlib import ExitStack
    with ExitStack() as ctx:
        Rs2 = ctx.enter_context(nc.sbuf_tensor(pfx + "_R", [128, 2, 8, SLAB], F32))
        xn = ctx.enter_context(nc.sbuf_tensor(pfx + "_xn", [128, 8, SLAB + 1], BF16))
        xr2 = ctx.enter_context(nc.sbuf_tensor(pfx + "_xr", [128, 2, 8, SLAB], BF16))
        xk2 = ctx.enter_context(nc.sbuf_tensor(pfx + "_xk", [128, 2, 8, SLAB], BF16))
        mixt = ctx.enter_context(nc.sbuf_tensor(pfx + "_mt", [128, SLAB], BF16))
        wF = ctx.enter_context(nc.sbuf_tensor(pfx + "_wF", [128, 8, 3072], BF16))
        muF = ctx.enter_context(nc.sbuf_tensor(pfx + "_mu", [128, 16], F32))
        omF = ctx.enter_context(nc.sbuf_tensor(pfx + "_om", [128, 16], F32))
        r32 = ctx.enter_context(nc.sbuf_tensor(pfx + "_r32", [128, 8, SLAB], F32))
        ka = ctx.enter_context(nc.sbuf_tensor(pfx + "_ka", [128, 8, SLAB], BF16))
        km = ctx.enter_context(nc.sbuf_tensor(pfx + "_km", [128, 2, SLAB], F32))
        vst = ctx.enter_context(nc.sbuf_tensor(pfx + "_vst", [128, 2, SLAB], F32))
        ost = ctx.enter_context(nc.sbuf_tensor(pfx + "_ost", [128, 2, SLAB], F32))
        ln = make_ln_tiles(nc, ctx, pfx)
        ps_p = [ctx.enter_context(nc.psum_tensor(pfx + "_pp0", [128, SLAB], F32)),
                ctx.enter_context(nc.psum_tensor(pfx + "_pp1", [128, SLAB], F32))]

        t_c = load_ln_consts(tape, sems, ln, dram)
        t_mu = dma(tape, "sync", muF[:, :], dram["muF"].ap(), sems)
        t_wf = dma(tape, "gpsimd", wF[:, :, :],
                   dram["wF"].ap().rearrange("(c p) j -> p c j", p=128), sems)
        tape.wait("vector", sems.d_sync, t_mu)
        t_om = tape.emit("vector", lambda e: e.tensor_scalar(
            omF[:, :], muF[:, :], -1.0, 1.0, ALU.mult, ALU.add), sems.c_ve)
        tape.emit("vector", lambda e: e.memset(xn[:, :, 0:1], 0.0), None)

        src_r = src.ap().rearrange("(c p) t -> p c t", p=128)
        dst_r = dst.ap().rearrange("(c p) t -> p c t", p=128)
        prev_ln = None
        t_rk_mm = [None, None]
        t_rkv_mm = None
        t_vadd = {}
        t_store = {}
        psg = [(None, None), (None, None)]
        for s in range(NSLAB):
            sp = s % 2
            Rs = Rs2[:, sp]
            xr = xr2[:, sp]
            xk = xk2[:, sp]
            if prev_ln is not None:
                tape.wait("sync", sems.c_ve, prev_ln["ve"])
                tape.wait("sync", sems.c_act, prev_ln["sq"])
            if s > 1:
                tape.wait("sync", sems.c_ve, t_vadd[(s - 2, 7)])
            t_R = dma(tape, "sync", Rs[:, :, :], src_r[:, :, s * SLAB:(s + 1) * SLAB], sems)
            tape.wait("scalar", sems.d_sync, t_R)
            tape.wait("vector", sems.d_sync, t_R)
            if s > 0:
                tape.emit("vector", lambda e: e.tensor_copy(
                    xn[:, :, 0:1], xn[:, :, SLAB:SLAB + 1]), None)
            prev_ln = ln_emit(tape, sems, Rs, xn, ln, xn_off=1)
            tape.wait("vector", sems.c_pe, t_rk_mm[sp])
            t_mix = None
            for c in range(8):
                tape.emit("vector", lambda e, c=c: e.tensor_scalar(
                    mixt[:, :], xn[:, c, 1:SLAB + 1], muF[:, c:c + 1], None, ALU.mult), None)
                tape.emit("vector", lambda e, c=c, xr=xr: e.scalar_tensor_tensor(
                    xr[:, c, :], xn[:, c, 0:SLAB], omF[:, c:c + 1], mixt[:, :],
                    ALU.mult, ALU.add), None)
                tape.emit("vector", lambda e, c=c: e.tensor_scalar(
                    mixt[:, :], xn[:, c, 1:SLAB + 1], muF[:, 8 + c:9 + c], None, ALU.mult), None)
                t_mix = tape.emit("vector", lambda e, c=c, xk=xk: e.scalar_tensor_tensor(
                    xk[:, c, :], xn[:, c, 0:SLAB], omF[:, 8 + c:9 + c], mixt[:, :],
                    ALU.mult, ALU.add), sems.c_ve)
            tape.wait("tensor", sems.d_gp, t_wf)
            tape.wait("tensor", sems.c_ve, t_mix)
            for dj in range(8):
                ps = ps_p[dj % 2]
                g = psg[dj % 2]
                tape.wait("tensor", g[0], g[1]) if g[0] else None
                t_mm = None
                for c in range(8):
                    t_mm = tape.emit("tensor", lambda e, c=c, dj=dj, ps=ps, xr=xr: e.matmul(
                        ps[:, :], wF[:, c, dj * 128:(dj + 1) * 128], xr[:, c, :],
                        start=(c == 0), stop=(c == 7)), sems.c_pe if c == 7 else None)
                tape.wait("scalar", sems.c_pe, t_mm)
                t_ev = tape.emit("scalar", lambda e, dj=dj, ps=ps: e.activation(
                    r32[:, dj, :], ps[:, :], AF.Sigmoid), sems.c_act)
                psg[dj % 2] = (sems.c_act, t_ev)
            for dj in range(8):
                ps = ps_p[dj % 2]
                g = psg[dj % 2]
                tape.wait("tensor", g[0], g[1]) if g[0] else None
                t_mm = None
                for c in range(8):
                    t_mm = tape.emit("tensor", lambda e, c=c, dj=dj, ps=ps, xk=xk: e.matmul(
                        ps[:, :], wF[:, c, 1024 + dj * 128:1024 + (dj + 1) * 128], xk[:, c, :],
                        start=(c == 0), stop=(c == 7)), sems.c_pe if c == 7 else None)
                tape.wait("vector", sems.c_pe, t_mm)
                if dj >= 2:
                    tape.wait("vector", sems.c_act, t_kmr)
                t_km = tape.emit("vector", lambda e, dj=dj, ps=ps: e.tensor_scalar(
                    km[:, dj % 2, :], ps[:, :], 0.0, None, ALU.max), sems.c_ve)
                if dj == 7:
                    t_rk_mm[sp] = t_mm
                psg[dj % 2] = (sems.c_ve, t_km)
                tape.wait("scalar", sems.c_ve, t_km)
                t_kmr = tape.emit("scalar", lambda e, dj=dj: e.activation(
                    ka[:, dj, :], km[:, dj % 2, :], AF.Square), sems.c_act)
            t_ka_done = t_kmr
            tape.wait("tensor", sems.c_act, t_ka_done)
            for dj in range(8):
                ps = ps_p[dj % 2]
                g = psg[dj % 2]
                tape.wait("tensor", g[0], g[1]) if g[0] else None
                t_mm = None
                for c in range(8):
                    t_mm = tape.emit("tensor", lambda e, c=c, dj=dj, ps=ps: e.matmul(
                        ps[:, :], wF[:, c, 2048 + dj * 128:2048 + (dj + 1) * 128], ka[:, c, :],
                        start=(c == 0), stop=(c == 7)), sems.c_pe if c == 7 else None)
                t_rkv_mm = t_mm
                tape.wait("vector", sems.c_pe, t_mm)
                pj = (s, dj - 2) if dj >= 2 else ((s - 1, 6 + dj) if s > 0 else None)
                if pj is not None and pj in t_store:
                    tape.wait("vector", sems.d_sync, t_store[pj])
                tape.emit("vector", lambda e, dj=dj, ps=ps: e.tensor_tensor(
                    vst[:, dj % 2, :], r32[:, dj, :], ps[:, :], ALU.mult), None)
                psg[dj % 2] = (sems.c_ve, None)
                t_vadd[(s, dj)] = tape.emit("vector", lambda e, dj=dj, Rs=Rs: e.tensor_tensor(
                    ost[:, dj % 2, :], vst[:, dj % 2, :], Rs[:, dj, :], ALU.add), sems.c_ve)
                psg[dj % 2] = (sems.c_ve, t_vadd[(s, dj)])
                tape.wait("sync", sems.c_ve, t_vadd[(s, dj)])
                t_store[(s, dj)] = dma(
                    tape, "sync", dst_r[:, dj, s * SLAB:(s + 1) * SLAB],
                    ost[:, dj % 2, :], sems)
        tape.wait("sync", sems.d_sync, sems.d_sync.n)
        tape.run_block(pfx)



def phase_c1(nc, sems, dram):
    tape = Tape(nc)
    from contextlib import ExitStack
    with ExitStack() as ctx:
        Rs = ctx.enter_context(nc.sbuf_tensor("c1_R", [128, 8, SLAB], F32))
        u = ctx.enter_context(nc.sbuf_tensor("c1_u", [128, 8, SLAB], BF16))
        winT = ctx.enter_context(nc.sbuf_tensor("c1_w", [128, 8, DINP], BF16))
        xbc = ctx.enter_context(nc.sbuf_tensor("c1_xbc", [128, 17, SLAB + 3], F32))
        zst = ctx.enter_context(nc.sbuf_tensor("c1_zst", [128, 2, SLAB], BF16))
        cva = ctx.enter_context(nc.sbuf_tensor("c1_cva", [128, 2, SLAB], F32))
        cvb = ctx.enter_context(nc.sbuf_tensor("c1_cvb", [128, 2, SLAB], F32))
        xsil = ctx.enter_context(nc.sbuf_tensor("c1_xsil", [128, 2, SLAB], BF16))
        cw = ctx.enter_context(nc.sbuf_tensor("c1_cw", [128, 68], F32))
        cb = ctx.enter_context(nc.sbuf_tensor("c1_cb", [128, 17], F32))
        dtb = ctx.enter_context(nc.sbuf_tensor("c1_dtb", [32, 1], F32))
        alog = ctx.enter_context(nc.sbuf_tensor("c1_alog", [32, 1], F32))
        acol = ctx.enter_context(nc.sbuf_tensor("c1_acol", [32, 1], F32))
        drows = ctx.enter_context(nc.sbuf_tensor("c1_dr", [32, 8, SLAB], F32))
        dbf = ctx.enter_context(nc.sbuf_tensor("c1_dbf", [32, 3, SLAB], BF16))
        ahi32 = ctx.enter_context(nc.sbuf_tensor("c1_ahi32", [32, SLAB], F32))
        ln = make_ln_tiles(nc, ctx, "c1")
        ps_p = [ctx.enter_context(nc.psum_tensor("c1_pp0", [128, SLAB], F32)),
                ctx.enter_context(nc.psum_tensor("c1_pp1", [128, SLAB], F32))]

        t_c = load_ln_consts(tape, sems, ln, dram)
        t_c = dma(tape, "sync", cw[:, :], dram["convw"].ap(), sems)
        t_c = dma(tape, "sync", cb[:, :], dram["convb"].ap(), sems)
        t_c = dma(tape, "sync", dtb[:, :], dram["dtb"].ap(), sems)
        t_al = dma(tape, "sync", alog[:, :], dram["alog"].ap(), sems)
        t_wi = dma(tape, "gpsimd", winT[:, :, :],
                   dram["w_in"].ap().rearrange("(c p) j -> p c j", p=128), sems)
        tape.wait("scalar", sems.d_sync, t_al)
        t_ea = tape.emit("scalar", lambda e: e.activation(
            acol[:, :], alog[:, :], AF.Exp), sems.c_act)

        t_tail = tape.emit("vector", lambda e: e.memset(xbc[:, :, 0:3], 0.0), sems.c_ve)
        t_tap_gp = None

        src_r = dram["scr_R"].ap().rearrange("(c p) t -> p c t", p=128)
        prev_ln = None
        psg = [(None, None), (None, None)]
        t_zsp = {}
        t_conv_ve = None
        t_conv_gp = None
        t_xbc_ev = {}
        t_sil_sp = {}
        t_dspill_prev = None
        for s in range(NSLAB):
            if prev_ln is not None:
                tape.wait("sync", sems.c_ve, prev_ln["ve"])
                tape.wait("sync", sems.c_act, prev_ln["sq"])
            t_R = dma(tape, "sync", Rs[:, :, :], src_r[:, :, s * SLAB:(s + 1) * SLAB], sems)
            tape.wait("scalar", sems.d_sync, t_R)
            tape.wait("vector", sems.d_sync, t_R)
            prev_ln = ln_emit(tape, sems, Rs, u, ln)
            tape.wait("vector", sems.c_gp, t_tap_gp)
            if s > 0:
                t_tail = tape.emit("vector", lambda e: e.tensor_copy(
                    xbc[:, :, 0:3], xbc[:, :, SLAB:SLAB + 3]), sems.c_ve)
            tape.wait("tensor", sems.d_gp, t_wi)
            tape.wait("tensor", sems.c_ve, prev_ln["ve"])
            for j in range(34):
                ps = ps_p[j % 2]
                g = psg[j % 2]
                tape.wait("tensor", g[0], g[1]) if g[0] else None
                mwid = 128 if j < 33 else 32
                t_mm = None
                for c in range(8):
                    t_mm = tape.emit("tensor", lambda e, c=c, j=j, ps=ps, mwid=mwid: e.matmul(
                        ps[0:mwid, :], winT[:, c, j * 128:j * 128 + mwid], u[:, c, :],
                        start=(c == 0), stop=(c == 7)), sems.c_pe if c == 7 else None)
                if j < 16:
                    tape.wait("scalar", sems.c_pe, t_mm)
                    pj = (s, j - 2) if j >= 2 else ((s - 1, 14 + j) if s > 0 else None)
                    if pj is not None and pj in t_zsp:
                        tape.wait("scalar", sems.d_sync, t_zsp[pj])
                    t_ev = tape.emit("scalar", lambda e, j=j, ps=ps: e.activation(
                        zst[:, j % 2, :], ps[:, :], AF.Copy), sems.c_act)
                    psg[j % 2] = (sems.c_act, t_ev)
                    tape.wait("sync", sems.c_act, t_ev)
                    t_zsp[(s, j)] = dma(
                        tape, "sync",
                        dram["scr_z"].ap()[j * 128:(j + 1) * 128, s * SLAB:(s + 1) * SLAB],
                        zst[:, j % 2, :], sems)
                elif j < 33:
                    fc = j - 16
                    tape.wait("scalar", sems.c_pe, t_mm)
                    if fc == 0:
                        tape.wait("scalar", sems.c_ve, t_tail)
                    t_ev = tape.emit("scalar", lambda e, fc=fc, ps=ps: e.activation(
                        xbc[:, fc, 3:SLAB + 3], ps[:, :], AF.Copy), sems.c_act)
                    psg[j % 2] = (sems.c_act, t_ev)
                    t_xbc_ev[fc] = t_ev
                else:
                    tape.wait("scalar", sems.c_pe, t_mm)
                    tape.wait("scalar", sems.d_sync, t_c)
                    t_dtp = tape.emit("scalar", lambda e, ps=ps: e.activation(
                        drows[:, 0, :], ps[0:32, :], AF.Identity, bias=dtb[:, 0:1]), sems.c_act)
                    psg[j % 2] = (sems.c_act, t_dtp)
                    tape.wait("vector", sems.c_act, t_dtp)
                    t_ve1 = tape.emit("vector", lambda e: e.tensor_scalar(
                        drows[:, 4, :], drows[:, 0, :], 0.0, None, ALU.max), sems.c_ve)
                    tape.emit("scalar", lambda e: e.activation(
                        drows[:, 1, :], drows[:, 0, :], AF.Abs), None)
                    tape.wait("scalar", sems.c_ve, t_ve1)
                    tape.emit("scalar", lambda e: e.activation(
                        drows[:, 2, :], drows[:, 1, :], AF.Exp, scale=-1.0), None)
                    t_a1 = tape.emit("scalar", lambda e: e.activation(
                        drows[:, 3, :], drows[:, 2, :], AF.Ln, bias=1.0), sems.c_act)
                    tape.wait("vector", sems.c_act, t_a1)
                    t_dt = tape.emit("vector", lambda e: e.tensor_tensor(
                        drows[:, 5, :], drows[:, 4, :], drows[:, 3, :], ALU.add), sems.c_ve)
                    tape.wait("vector", sems.c_act, t_ea)
                    t_ad = tape.emit("vector", lambda e: e.tensor_scalar(
                        drows[:, 7, :], drows[:, 5, :], acol[:, 0:1], -1.0,
                        ALU.mult, ALU.mult), sems.c_ve)
                    t_sc = None
                    for cc in range(8):
                        t_sc = tape.emit("vector", lambda e, cc=cc: e.tensor_tensor_scan(
                            drows[:, 6, cc * 64:(cc + 1) * 64],
                            drows[:, 7, cc * 64:(cc + 1) * 64],
                            drows[:, 7, cc * 64:(cc + 1) * 64], 0.0,
                            ALU.add, ALU.bypass), sems.c_ve)
                    tape.emit("vector", lambda e: e.tensor_scalar(
                        drows[:, 1, :], drows[:, 5, :], 1e-30, None, ALU.max), None)
                    tape.emit("vector", lambda e: e.tensor_copy(
                        dbf[:, 0, :], drows[:, 6, :]), None)
                    t_hi = tape.emit("vector", lambda e: e.tensor_copy(
                        ahi32[:, :], dbf[:, 0, :]), sems.c_ve)
                    t_lo = tape.emit("vector", lambda e: e.tensor_tensor(
                        dbf[:, 1, :], drows[:, 6, :], ahi32[:, :], ALU.subtract), sems.c_ve)
                    tape.wait("scalar", sems.c_ve, t_lo)
                    t_lnd = tape.emit("scalar", lambda e: e.activation(
                        dbf[:, 2, :], drows[:, 1, :], AF.Ln), sems.c_act)
                    if t_dspill_prev is not None:
                        tape.wait("sync", sems.d_sync, t_dspill_prev)
                    tape.wait("sync", sems.c_ve, t_lo)
                    tape.wait("sync", sems.c_act, t_lnd)
                    sl = slice(s * SLAB, (s + 1) * SLAB)
                    dma(tape, "sync", dram["scr_dt"].ap()[:, sl], drows[:, 5, :], sems)
                    dma(tape, "sync", dram["scr_acum"].ap()[:, sl], drows[:, 6, :], sems)
                    dma(tape, "sync", dram["scr_ahi"].ap()[:, sl], dbf[:, 0, :], sems)
                    dma(tape, "sync", dram["scr_alo"].ap()[:, sl], dbf[:, 1, :], sems)
                    t_dspill_prev = dma(tape, "sync", dram["scr_lnd"].ap()[:, sl],
                                        dbf[:, 2, :], sems)
            tape.wait("vector", sems.d_sync, t_c)
            tape.wait("gpsimd", sems.d_sync, t_c)
            t_sil = {}
            for fc in range(17):
                eng = "vector"
                cv = cva if fc % 2 == 0 else cvb
                cctr = sems.c_ve
                pp = 0
                tape.wait(eng, sems.c_act, t_xbc_ev[fc])
                if fc >= 2:
                    tape.wait(eng, sems.c_act, t_sil.get(fc - 2))
                tape.emit(eng, lambda e, fc=fc, cv=cv, pp=pp: e.tensor_scalar(
                    cv[:, pp, :], xbc[:, fc, 0:SLAB], cw[:, 4 * fc:4 * fc + 1], None,
                    ALU.mult), None)
                t_cv = None
                for tap in (1, 2, 3):
                    t_cv = tape.emit(eng, lambda e, fc=fc, cv=cv, pp=pp, tap=tap:
                                     e.scalar_tensor_tensor(
                                         cv[:, pp, :], xbc[:, fc, tap:SLAB + tap],
                                         cw[:, 4 * fc + tap:4 * fc + tap + 1], cv[:, pp, :],
                                         ALU.mult, ALU.add),
                                     cctr if tap == 3 else None)

                tape.wait("scalar", cctr, t_cv)
                pj = fc - 2
                if pj >= 0 and pj in t_sil_sp:
                    tape.wait("scalar", sems.d_sync, t_sil_sp[pj])
                t_sil[fc] = tape.emit("scalar", lambda e, fc=fc, cv=cv, pp=pp: e.activation(
                    xsil[:, fc % 2, :], cv[:, pp, :], AF.Silu, bias=cb[:, fc:fc + 1]),
                    sems.c_act)
                tape.wait("sync", sems.c_act, t_sil[fc])
                if fc < 16:
                    dst = dram["scr_xs"].ap()[fc * 128:(fc + 1) * 128, s * SLAB:(s + 1) * SLAB]
                else:
                    dst = dram["scr_bc"].ap()[:, s * SLAB:(s + 1) * SLAB]
                t_sil_sp[fc] = dma(tape, "sync", dst, xsil[:, fc % 2, :], sems)
        tape.wait("sync", sems.d_sync, sems.d_sync.n)
        tape.run_block("c1")




C2_LEVEL = 99


def phase_c2(nc, sems, dram):
    tape = Tape(nc)

    def sect(lv):
        tape.mute = lv > C2_LEVEL
    from contextlib import ExitStack
    with ExitStack() as ctx:
        Rs = ctx.enter_context(nc.sbuf_tensor("c2_R", [128, 8, SLAB], F32))
        z16 = ctx.enter_context(nc.sbuf_tensor("c2_z", [128, 16, SLAB], BF16))
        xf16 = ctx.enter_context(nc.sbuf_tensor("c2_xf", [128, 16, SLAB], BF16))
        bfm0 = ctx.enter_context(nc.sbuf_tensor("c2_bf0", [64, SLAB], BF16))
        cfm0 = ctx.enter_context(nc.sbuf_tensor("c2_cf0", [64, SLAB], BF16))
        xtc = [ctx.enter_context(nc.sbuf_tensor(f"c2_xt{i}", [64, 16, 128], BF16))
               for i in range(2)]
        dts = ctx.enter_context(nc.sbuf_tensor("c2_dt", [32, SLAB], F32))
        ac32 = ctx.enter_context(nc.sbuf_tensor("c2_ac", [32, SLAB], F32))
        ahi = ctx.enter_context(nc.sbuf_tensor("c2_ahi", [32, SLAB], BF16))
        alo = ctx.enter_context(nc.sbuf_tensor("c2_alo", [32, SLAB], BF16))
        lnd = ctx.enter_context(nc.sbuf_tensor("c2_lnd", [32, SLAB], BF16))
        afh = ctx.enter_context(nc.sbuf_tensor("c2_afh", [1, 2, 2048], BF16))
        afl = ctx.enter_context(nc.sbuf_tensor("c2_afl", [1, 2, 2048], BF16))
        wob = ctx.enter_context(nc.sbuf_tensor("c2_wob", [128, 16, 1024], BF16))
        S = ctx.enter_context(nc.sbuf_tensor("c2_S", [64, 32, 64], F32))
        Sbf = ctx.enter_context(nc.sbuf_tensor("c2_Sbf", [64, 32, 64], BF16))
        eE = ctx.enter_context(nc.sbuf_tensor("c2_eE", [64, 1024], BF16))
        sdo = ctx.enter_context(nc.sbuf_tensor("c2_sdo", [64, 1024], BF16))
        Pss = ctx.enter_context(nc.sbuf_tensor("c2_P", [64, 1024], BF16))
        xw = ctx.enter_context(nc.sbuf_tensor("c2_xw", [64, 1024], BF16))
        t2 = ctx.enter_context(nc.sbuf_tensor("c2_t2", [64, 1024], F32))
        t3 = t2
        ybf = ctx.enter_context(nc.sbuf_tensor("c2_ybf", [64, 2, 1024], BF16))
        btok = ctx.enter_context(nc.sbuf_tensor("c2_btok", [64, 64], BF16))
        wdt = ctx.enter_context(nc.sbuf_tensor("c2_wd", [32, 2, 64], F32))
        wdbf = ctx.enter_context(nc.sbuf_tensor("c2_wdbf", [32, 64], BF16))
        ebc64 = ctx.enter_context(nc.sbuf_tensor("c2_ebc", [32, 64], BF16))
        eb32 = ctx.enter_context(nc.sbuf_tensor("c2_eb32", [32, 1], F32))
        ysl = ctx.enter_context(nc.sbuf_tensor("c2_ysl", [128, 16, SLAB], F32))
        y3 = z16
        sqst = ctx.enter_context(nc.sbuf_tensor("c2_sqst", [128, 2, SLAB], BF16))
        szst = ctx.enter_context(nc.sbuf_tensor("c2_szst", [128, 2, SLAB], BF16))
        rrow = ctx.enter_context(nc.sbuf_tensor("c2_rrow", [1, 2, SLAB], F32))
        rbf = ctx.enter_context(nc.sbuf_tensor("c2_rbf", [1, SLAB], BF16))
        ident = ctx.enter_context(nc.sbuf_tensor("c2_id", [128, 128], BF16))
        onr = ctx.enter_context(nc.sbuf_tensor("c2_onr", [1, 128], BF16))
        onc = ctx.enter_context(nc.sbuf_tensor("c2_onc", [128, 1], BF16))
        Ind = ctx.enter_context(nc.sbuf_tensor("c2_ind", [32, 2048], BF16))
        IndN = ctx.enter_context(nc.sbuf_tensor("c2_indn", [32, 2048], BF16))
        maskc = ctx.enter_context(nc.sbuf_tensor("c2_mask", [64, 2048], BF16))
        Drep = ctx.enter_context(nc.sbuf_tensor("c2_drep", [128, 16], F32))
        nrmw = ctx.enter_context(nc.sbuf_tensor("c2_nw", [128, 16], F32))
        ostage = ctx.enter_context(nc.sbuf_tensor("c2_ostg", [128, 2, SLAB], F32))
        PS = [ctx.enter_context(nc.psum_tensor(f"c2_ps{i}", [128, 1024], F32))
              for i in range(4)]
        psYT = PS[2].bitcast(BF16)

        t_c = dma(tape, "sync", ident[:, :], dram["ident"].ap(), sems)
        t_c = dma(tape, "sync", onr[:, :], dram["ones_row"].ap(), sems)
        t_c = dma(tape, "sync", onc[:, :], dram["ones_col"].ap(), sems)
        t_c = dma(tape, "sync", Ind[:, :], dram["Ind"].ap(), sems)
        t_c = dma(tape, "sync", IndN[:, :], dram["IndN"].ap(), sems)
        t_c = dma(tape, "sync", maskc[:, :], dram["maskc"].ap(), sems)
        t_c = dma(tape, "sync", Drep[:, :], dram["Drep"].ap(), sems)
        t_c = dma(tape, "sync", nrmw[:, :], dram["normw"].ap(), sems)
        t_wo = dma(tape, "gpsimd", wob[:, :, :],
                   dram["w_out_b"].ap().rearrange("(c p) j -> p c j", p=128), sems)
        tape.emit("vector", lambda e: e.memset(S[:, :, :], 0.0), None)
        t_S0 = tape.emit("vector", lambda e: e.memset(Sbf[:, :, :], 0.0), sems.c_ve)

        src_r = dram["scr_R"].ap().rearrange("(c p) t -> p c t", p=128)
        z_r = dram["scr_z"].ap().rearrange("(c p) t -> p c t", p=128)
        xs_r = dram["scr_xs"].ap().rearrange("(c p) t -> p c t", p=128)

        t_sfree = {i: (None, None) for i in range(4)}

        def psum_wait(eng, idx):
            ctr, val = t_sfree[idx]
            if ctr is not None:
                tape.wait(eng, ctr, val)

        t_xtc_free = [(None, None), (None, None)]
        t_radd = {}
        t_ost = {}
        t_y3_mm = None
        t_sq15 = None
        t_Snew = t_S0
        for s in range(NSLAB):
            sect(0)
            sl = slice(s * SLAB, (s + 1) * SLAB)
            if s > 0:
                tape.wait("sync", sems.c_ve, t_radd.get((s - 1, 7)))
                tape.wait("sync", sems.c_pe, t_y3_mm)
                tape.wait("sync", sems.c_act, t_sq15)
            t_z = dma(tape, "sync", z16[:, :, :], z_r[:, :, sl], sems)
            t_xf = dma(tape, "sync", xf16[:, :, :], xs_r[:, :, sl], sems)
            t_b0 = dma(tape, "sync", bfm0[:, :], dram["scr_bc"].ap()[0:64, sl], sems)
            t_c0 = dma(tape, "sync", cfm0[:, :], dram["scr_bc"].ap()[64:128, sl], sems)
            t_dt = dma(tape, "sync", dts[:, :], dram["scr_dt"].ap()[:, sl], sems)
            t_ac = dma(tape, "sync", ac32[:, :], dram["scr_acum"].ap()[:, sl], sems)
            t_hi = dma(tape, "sync", ahi[:, :], dram["scr_ahi"].ap()[:, sl], sems)
            t_lo = dma(tape, "sync", alo[:, :], dram["scr_alo"].ap()[:, sl], sems)
            t_ln = dma(tape, "sync", lnd[:, :], dram["scr_lnd"].ap()[:, sl], sems)
            t_R = dma(tape, "sync", Rs[:, :, :], src_r[:, :, sl], sems)
            tape.wait("tensor", sems.d_sync, t_ln)
            tape.wait("tensor", sems.d_sync, t_xf)
            if s == 0:
                tape.wait("tensor", sems.d_gp, t_wo)

            t_af_mm = {0: None, 1: None}
            for c in range(8):
                sect(0)
                par = c % 2
                c64 = slice(c * 64, (c + 1) * 64)
                if t_af_mm[par] is not None:
                    tape.wait("sync", sems.c_pe, t_af_mm[par])
                csl = slice(s * SLAB + c * 64, s * SLAB + (c + 1) * 64)
                dma(tape, "sync", afh[:, par, :].rearrange("p (h l) -> p h l", h=32),
                    dram["scr_ahi"].ap()[:, csl].unsqueeze(0), sems)
                t_af = dma(tape, "sync", afl[:, par, :].rearrange("p (h l) -> p h l", h=32),
                           dram["scr_alo"].ap()[:, csl].unsqueeze(0), sems)
                sect(1)
                psum_wait("tensor", 2)
                g0, g1 = t_xtc_free[par]
                if g0 is not None:
                    tape.wait("tensor", g0, g1)
                t_xtr = {}
                for grp in range(2):
                    if grp == 1:
                        tape.wait("tensor", sems.c_ve, t_xtr[0])
                    for i in range(8):
                        fcc = grp * 8 + i
                        t_tr = tape.emit("tensor", lambda e, fcc=fcc, i=i, c64=c64: e.transpose(
                            psYT[0:64, i * 128:(i + 1) * 128], xf16[:, fcc, c64],
                            ident[:, :]), sems.c_pe if i == 7 else None)
                    tape.wait("vector", sems.c_pe, t_tr)
                    t_xtr[grp] = tape.emit("vector", lambda e, grp=grp, par=par: e.tensor_copy(
                        xtc[par][:, grp * 8:(grp + 1) * 8, :],
                        psYT[0:64, 0:1024].rearrange("p (a b) -> p a b", a=8)), sems.c_ve)
                    t_sfree[2] = (sems.c_ve, t_xtr[grp])
                t_xt_done = t_xtr[1]
                sect(2)
                psum_wait("tensor", 0)
                tape.wait("tensor", sems.d_sync, t_b0)
                psEb = PS[0].bitcast(BF16)
                t_bt = tape.emit("tensor", lambda e, c64=c64, psEb=psEb: e.transpose(
                    psEb[0:64, 0:64], bfm0[:, c64], ident[0:64, 0:64]), sems.c_pe)
                tape.wait("vector", sems.c_pe, t_bt)
                t_btc = tape.emit("vector", lambda e, psEb=psEb: e.tensor_copy(
                    btok[:, :], psEb[0:64, 0:64]), sems.c_ve)
                t_sfree[0] = (sems.c_ve, t_btc)
                sect(2)
                tape.wait("vector", sems.d_sync, t_ac)
                tape.wait("vector", sems.d_sync, t_dt)
                t_d1 = tape.emit("vector", lambda e, c=c: e.tensor_scalar(
                    wdt[:, 0, :], ac32[:, c * 64:(c + 1) * 64],
                    ac32[:, c * 64 + 63:c * 64 + 64], None, ALU.subtract), sems.c_ve)
                tape.wait("scalar", sems.c_ve, t_d1)
                t_ew = tape.emit("scalar", lambda e: e.activation(
                    wdt[:, 1, :], wdt[:, 0, :], AF.Exp, scale=-1.0), sems.c_act)
                tape.wait("vector", sems.c_act, t_ew)
                t_wd = tape.emit("vector", lambda e, c=c: e.tensor_tensor(
                    wdbf[:, :], wdt[:, 1, :], dts[:, c * 64:(c + 1) * 64], ALU.mult),
                    sems.c_ve)
                sect(2)
                tape.wait("scalar", sems.d_sync, t_ac)
                t_eb = tape.emit("scalar", lambda e, c=c: e.activation(
                    eb32[:, :], ac32[:, c * 64 + 63:c * 64 + 64], AF.Exp), sems.c_act)
                tape.wait("vector", sems.c_act, t_eb)
                t_ebc = tape.emit("vector", lambda e: e.tensor_copy(
                    ebc64[:, :], eb32[:, 0:1].broadcast_to([32, 64])), sems.c_ve)
                sect(3)
                psum_wait("tensor", 1)
                tape.wait("tensor", sems.d_sync, t_c0)
                t_G = None
                for half in range(2):
                    crep = cfm0[:, c64][:, None, :].broadcast_to([64, 8, 64])
                    t_G = tape.emit("tensor", lambda e, crep=crep, c64=c64, half=half: e.matmul(
                        PS[1][0:64, half * 512:(half + 1) * 512], bfm0[:, c64], crep,
                        start=True, stop=True), sems.c_pe)

                for hg in range(2):
                    hoff0 = hg * 1024
                    sect(3)
                    psum_wait("tensor", 0)
                    tape.wait("tensor", sems.d_sync, t_ln)
                    t_E = None
                    for half in range(2):
                        hs = slice(half * 512, (half + 1) * 512)
                        hoff = hoff0 + half * 512
                        for oi, (lt, rt) in enumerate(((ahi, IndN), (alo, IndN), (lnd, Ind))):
                            tape.emit("tensor", lambda e, lt=lt, rt=rt, c=c, hs=hs, hoff=hoff,
                                      oi=oi: e.matmul(
                                PS[0][0:64, hs], lt[:, c * 64:(c + 1) * 64],
                                rt[:, hoff:hoff + 512],
                                start=(oi == 0), stop=False), None)
                        tape.emit("tensor", lambda e, hs=hs, hoff=hoff: e.matmul(
                            PS[0][0:64, hs], ident[0:64, 0:64], maskc[:, hoff:hoff + 512],
                            start=False, stop=False), None)
                        if half == 0 and hoff0 == 0:
                            tape.wait("tensor", sems.d_sync, t_af)
                        tape.emit("tensor", lambda e, hs=hs, par=par, hoff=hoff: e.matmul(
                            PS[0][0:64, hs], onr[0:1, 0:64],
                            afh[0:1, par, hoff:hoff + 512],
                            start=False, stop=False), None)
                        t_E = tape.emit("tensor", lambda e, hs=hs, par=par, hoff=hoff: e.matmul(
                            PS[0][0:64, hs], onr[0:1, 0:64],
                            afl[0:1, par, hoff:hoff + 512],
                            start=False, stop=True), sems.c_pe)
                        t_af_mm[par] = t_E
                    tape.wait("scalar", sems.c_pe, t_E)
                    t_eE = tape.emit("scalar", lambda e: e.activation(
                        eE[:, :], PS[0][0:64, :], AF.Exp), sems.c_act)
                    t_sfree[0] = (sems.c_act, t_eE)
                    sect(3)
                    psum_wait("tensor", 0)
                    t_D = None
                    for half in range(2):
                        hs = slice(half * 512, (half + 1) * 512)
                        hoff = hoff0 + half * 512
                        tape.emit("tensor", lambda e, c=c, hs=hs, hoff=hoff: e.matmul(
                            PS[0][0:64, hs], ahi[:, c * 64:(c + 1) * 64],
                            Ind[:, hoff:hoff + 512], start=True, stop=False), None)
                        t_D = tape.emit("tensor", lambda e, c=c, hs=hs, hoff=hoff: e.matmul(
                            PS[0][0:64, hs], alo[:, c * 64:(c + 1) * 64],
                            Ind[:, hoff:hoff + 512], start=False, stop=True), sems.c_pe)
                    tape.wait("scalar", sems.c_pe, t_D)
                    t_sdo = tape.emit("scalar", lambda e: e.activation(
                        sdo[:, :], PS[0][0:64, :], AF.Exp), sems.c_act)
                    t_sfree[0] = (sems.c_act, t_sdo)
                    sect(3)
                    tape.wait("vector", sems.c_pe, t_G)
                    tape.wait("vector", sems.c_act, t_eE)
                    t_P = tape.emit("vector", lambda e: e.tensor_tensor(
                        Pss[:, :], PS[1][0:64, 0:1024], eE[:, :], ALU.mult), sems.c_ve)
                    if hg == 1:
                        t_sfree[1] = (sems.c_ve, t_P)
                    sect(4)
                    psum_wait("tensor", 2)
                    tape.wait("tensor", sems.c_ve, t_P)
                    tape.wait("tensor", sems.c_ve, t_xt_done)
                    t_yd = None
                    for i in range(16):
                        h = hg * 16 + i
                        hfc, hcol = h // 2, (h % 2) * 64
                        t_yd = tape.emit("tensor", lambda e, i=i, hfc=hfc, hcol=hcol,
                                         par=par: e.matmul(
                            PS[2][0:64, i * 64:(i + 1) * 64], Pss[:, i * 64:(i + 1) * 64],
                            xtc[par][:, hfc, hcol:hcol + 64],
                            start=(i in (0, 8)), stop=(i == 15)),
                            sems.c_pe if i == 15 else None)
                    sect(4)
                    psum_wait("tensor", 3)
                    t_yo = None
                    for i in range(16):
                        h = hg * 16 + i
                        t_yo = tape.emit("tensor", lambda e, i=i, h=h, c64=c64: e.matmul(
                            PS[3][0:64, i * 64:(i + 1) * 64], cfm0[:, c64],
                            Sbf[:, h, :], start=(i in (0, 8)), stop=(i == 15)),
                            sems.c_pe if i == 15 else None)
                    tape.wait("vector", sems.c_pe, t_yo)
                    tape.wait("vector", sems.c_act, t_sdo)
                    t_t2 = tape.emit("vector", lambda e: e.tensor_tensor(
                        t2[:, :], PS[3][0:64, :], sdo[:, :], ALU.mult), sems.c_ve)
                    t_sfree[3] = (sems.c_ve, t_t2)
                    tape.wait("vector", sems.c_pe, t_yd)
                    t_y = tape.emit("vector", lambda e, hg=hg: e.tensor_tensor(
                        ybf[:, hg, :], PS[2][0:64, :], t2[:, :], ALU.add), sems.c_ve)
                    t_sfree[2] = (sems.c_ve, t_y)
                    sect(5)
                    psum_wait("tensor", 0)
                    tape.wait("tensor", sems.c_ve, t_wd)
                    t_wb = None
                    for half in range(2):
                        hs = slice(half * 512, (half + 1) * 512)
                        hoff = hoff0 + half * 512
                        t_wb = tape.emit("tensor", lambda e, hs=hs, hoff=hoff: e.matmul(
                            PS[0][0:64, hs], wdbf[:, :], Ind[:, hoff:hoff + 512],
                            start=True, stop=True), sems.c_pe)
                    tape.wait("vector", sems.c_pe, t_wb)
                    xts = xtc[par][:, hg * 8:(hg + 1) * 8, :].rearrange("p a b -> p (a b)")
                    t_xw = tape.emit("vector", lambda e, xts=xts: e.tensor_tensor(
                        xw[:, :], PS[0][0:64, :], xts, ALU.mult), sems.c_ve)
                    t_sfree[0] = (sems.c_ve, t_xw)
                    sect(5)
                    psum_wait("tensor", 0)
                    tape.wait("tensor", sems.c_ve, t_xw)
                    tape.wait("tensor", sems.c_ve, t_btc)
                    t_st = None
                    for half in range(2):
                        hs = slice(half * 512, (half + 1) * 512)
                        t_st = tape.emit("tensor", lambda e, hs=hs: e.matmul(
                            PS[0][0:64, hs], btok[:, :], xw[:, hs],
                            start=True, stop=True), sems.c_pe)
                    sect(5)
                    psum_wait("tensor", 3)
                    tape.wait("tensor", sems.c_ve, t_ebc)
                    t_eB = None
                    for half in range(2):
                        hs = slice(half * 512, (half + 1) * 512)
                        hoff = hoff0 + half * 512
                        t_eB = tape.emit("tensor", lambda e, hs=hs, hoff=hoff: e.matmul(
                            PS[3][0:64, hs], ebc64[:, :], Ind[:, hoff:hoff + 512],
                            start=True, stop=True), sems.c_pe)
                    sect(5)
                    tape.wait("vector", sems.c_pe, t_eB)
                    Sv = S[:, hg * 16:(hg + 1) * 16, :].rearrange("p a b -> p (a b)")
                    t_t3 = tape.emit("vector", lambda e, Sv=Sv: e.tensor_tensor(
                        t3[:, :], Sv, PS[3][0:64, :], ALU.mult), sems.c_ve)
                    t_sfree[3] = (sems.c_ve, t_t3)
                    tape.wait("vector", sems.c_pe, t_st)
                    t_Snew = tape.emit("vector", lambda e, Sv=Sv: e.tensor_tensor(
                        Sv, t3[:, :], PS[0][0:64, :], ALU.add), sems.c_ve)
                    t_sfree[0] = (sems.c_ve, t_Snew)
                    Sbv = Sbf[:, hg * 16:(hg + 1) * 16, :].rearrange("p a b -> p (a b)")
                    t_Snew = tape.emit("vector", lambda e, Sv=Sv, Sbv=Sbv: e.tensor_copy(
                        Sbv, Sv), sems.c_ve)
                t_xtc_free[par] = (sems.c_ve, t_Snew)

                sect(6)
                for hg in range(2):
                    psum_wait("tensor", 2)
                    t_tr = None
                    for i in range(8):
                        t_tr = tape.emit("tensor", lambda e, hg=hg, i=i: e.transpose(
                            psYT[:, i * 64:(i + 1) * 64], ybf[0:64, hg, i * 128:(i + 1) * 128],
                            ident[0:64, 0:64]), sems.c_pe if i == 7 else None)
                    tape.wait("vector", sems.c_pe, t_tr)
                    t_yev = tape.emit("vector", lambda e, hg=hg, c=c: e.tensor_copy(
                        ysl[:, hg * 8:(hg + 1) * 8, c * 64:(c + 1) * 64],
                        psYT[:, 0:512].rearrange("p (a b) -> p a b", a=8)), sems.c_ve)
                    t_sfree[2] = (sems.c_ve, t_yev)

            sect(7)
            tape.wait("vector", sems.d_sync, t_xf)
            tape.wait("vector", sems.d_sync, t_c)
            for fcc in range(16):
                tape.emit("vector", lambda e, fcc=fcc: e.scalar_tensor_tensor(
                    ysl[:, fcc, :], xf16[:, fcc, :], Drep[:, fcc:fcc + 1], ysl[:, fcc, :],
                    ALU.mult, ALU.add), None)
            tape.wait("scalar", sems.d_sync, t_z)
            t_g2 = {}
            for fcc in range(16):
                if fcc >= 2:
                    tape.wait("scalar", sems.c_ve, t_g2[fcc - 2])
                t_sza = tape.emit("scalar", lambda e, fcc=fcc: e.activation(
                    szst[:, fcc % 2, :], z16[:, fcc, :], AF.Silu), sems.c_act)
                tape.wait("vector", sems.c_act, t_sza)
                t_g2[fcc] = tape.emit("vector", lambda e, fcc=fcc: e.tensor_tensor(
                    ysl[:, fcc, :], ysl[:, fcc, :], szst[:, fcc % 2, :], ALU.mult),
                    sems.c_ve)
            psum_wait("tensor", 1)
            t_rmm = None
            t_sq15 = None
            for fcc in range(16):
                tape.wait("scalar", sems.c_ve, t_g2[fcc])
                if fcc >= 2:
                    tape.wait("scalar", sems.c_pe, t_rmm)
                t_sq15 = tape.emit("scalar", lambda e, fcc=fcc: e.activation(
                    sqst[:, fcc % 2, :], ysl[:, fcc, :], AF.Square), sems.c_act)
                tape.wait("tensor", sems.c_act, t_sq15)
                t_rmm = tape.emit("tensor", lambda e, fcc=fcc: e.matmul(
                    PS[1][0:1, 0:512], onc[:, 0:1], sqst[:, fcc % 2, :],
                    start=(fcc == 0), stop=(fcc == 15)), sems.c_pe)
            tape.wait("vector", sems.c_pe, t_rmm)
            t_r1 = tape.emit("vector", lambda e: e.tensor_scalar(
                rrow[:, 0, :], PS[1][0:1, 0:512], 1.0 / DIN, EPS_RMS, ALU.mult, ALU.add),
                sems.c_ve)
            tape.wait("scalar", sems.c_ve, t_r1)
            tape.emit("scalar", lambda e: e.activation(
                rrow[:, 1, :], rrow[:, 0, :], AF.Ln), None)
            t_rst = tape.emit("scalar", lambda e: e.activation(
                rrow[:, 0, :], rrow[:, 1, :], AF.Exp, scale=-0.5), sems.c_act)
            tape.wait("vector", sems.c_act, t_rst)
            t_rbfc = tape.emit("vector", lambda e: e.tensor_copy(rbf[:, :], rrow[:, 0, :]),
                               sems.c_ve)
            tape.wait("tensor", sems.c_ve, t_rbfc)
            t_rb = tape.emit("tensor", lambda e: e.matmul(
                PS[1][:, 0:512], onr[0:1, :], rbf[:, :], start=True, stop=True), sems.c_pe)
            tape.wait("vector", sems.c_pe, t_rb)
            t_y3 = None
            for fcc in range(16):
                t_y3 = tape.emit("vector", lambda e, fcc=fcc: e.scalar_tensor_tensor(
                    y3[:, fcc, :], ysl[:, fcc, :], nrmw[:, fcc:fcc + 1], PS[1][:, 0:512],
                    ALU.mult, ALU.mult), sems.c_ve)
            t_sfree[1] = (sems.c_ve, t_y3)
            for dmc in range(8):
                psi = 2 if dmc % 2 == 0 else 3
                psum_wait("tensor", psi)
                tape.wait("tensor", sems.c_ve, t_y3)
                t_mm = None
                for fcc in range(16):
                    t_mm = tape.emit("tensor", lambda e, fcc=fcc, dmc=dmc, psi=psi: e.matmul(
                        PS[psi][:, 0:512], wob[:, fcc, dmc * 128:(dmc + 1) * 128],
                        y3[:, fcc, :], start=(fcc == 0), stop=(fcc == 15)),
                        sems.c_pe if fcc == 15 else None)
                t_y3_mm = t_mm
                tape.wait("vector", sems.c_pe, t_mm)
                tape.wait("vector", sems.d_sync, t_R)
                pj = (s, dmc - 2) if dmc >= 2 else ((s - 1, 6 + dmc) if s > 0 else None)
                if pj is not None and pj in t_ost:
                    tape.wait("vector", sems.d_sync, t_ost[pj])
                t_radd[(s, dmc)] = tape.emit("vector", lambda e, dmc=dmc, psi=psi: e.tensor_tensor(
                    ostage[:, dmc % 2, :], PS[psi][:, 0:512], Rs[:, dmc, :], ALU.add),
                    sems.c_ve)
                t_sfree[psi] = (sems.c_ve, t_radd[(s, dmc)])
                tape.wait("sync", sems.c_ve, t_radd[(s, dmc)])
                t_ost[(s, dmc)] = dma(
                    tape, "sync",
                    dram["scr_R"].ap()[dmc * 128:(dmc + 1) * 128, sl],
                    ostage[:, dmc % 2, :], sems)
        sect(0)
        tape.wait("sync", sems.d_sync, sems.d_sync.n)
        tape.run_block("c2")



def phase_finish(nc, sems, dram):
    tape = Tape(nc)
    t = dma(tape, "sync", dram["out"].ap(), dram["scr_R"].ap(), sems)
    tape.wait("sync", sems.d_sync, t)
    tape.run_block("fin")


def build_nc(n_sub=4):
    from contextlib import ExitStack
    nc = bass.Bass("TRN2", target_bir_lowering=False)
    dram = {}
    dram["xT"] = nc.dram_tensor("xT", [DM, SEQ], F32, kind="ExternalInput")
    dram["wA"] = nc.dram_tensor("wA", [DM, 5120], F32, kind="ExternalInput")
    dram["bA"] = nc.dram_tensor("bA", [128, 40], F32, kind="ExternalInput")
    dram["w_out_a"] = nc.dram_tensor("w_out_a", [DM, DM], F32, kind="ExternalInput")
    dram["b_out_a"] = nc.dram_tensor("b_out_a", [128, 8], F32, kind="ExternalInput")
    dram["gbias"] = nc.dram_tensor("gbias", [128, 32], F32, kind="ExternalInput")
    dram["wbias"] = nc.dram_tensor("wbias", [128, 1], F32, kind="ExternalInput")
    dram["wF"] = nc.dram_tensor("wF", [DM, 3072], F32, kind="ExternalInput")
    dram["muF"] = nc.dram_tensor("muF", [128, 16], F32, kind="ExternalInput")
    dram["w_in"] = nc.dram_tensor("w_in", [DM, DINP], F32, kind="ExternalInput")
    dram["w_out_b"] = nc.dram_tensor("w_out_b", [DIN, DM], F32, kind="ExternalInput")
    dram["convw"] = nc.dram_tensor("convw", [128, 68], F32, kind="ExternalInput")
    dram["convb"] = nc.dram_tensor("convb", [128, 17], F32, kind="ExternalInput")
    dram["dtb"] = nc.dram_tensor("dtb", [32, 1], F32, kind="ExternalInput")
    dram["alog"] = nc.dram_tensor("alog", [32, 1], F32, kind="ExternalInput")
    dram["Drep"] = nc.dram_tensor("Drep", [128, 16], F32, kind="ExternalInput")
    dram["normw"] = nc.dram_tensor("normw", [128, 16], F32, kind="ExternalInput")
    dram["ident"] = nc.dram_tensor("ident", [128, 128], BF16, kind="ExternalInput")
    dram["ones_col"] = nc.dram_tensor("ones_col", [128, 1], BF16, kind="ExternalInput")
    dram["ones_row"] = nc.dram_tensor("ones_row", [1, 128], BF16, kind="ExternalInput")
    dram["Ind"] = nc.dram_tensor("Ind", [32, 2048], BF16, kind="ExternalInput")
    dram["IndN"] = nc.dram_tensor("IndN", [32, 2048], BF16, kind="ExternalInput")
    dram["maskc"] = nc.dram_tensor("maskc", [64, 2048], BF16, kind="ExternalInput")
    dram["maskw"] = nc.dram_tensor("maskw", [128, 128], BF16, kind="ExternalInput")
    dram["out"] = nc.dram_tensor("out", [DM, SEQ], F32, kind="ExternalOutput")
    for nm, shp, dt in [
        ("scr_R", [DM, SEQ], F32), ("scr_vfm", [DM, SEQ], BF16),
        ("scr_wq", [DM, SEQ], BF16), ("scr_wk", [DM, SEQ], BF16),
        ("scr_gq", [DM, SEQ], BF16), ("scr_gk", [DM, SEQ], BF16),
        ("scr_z", [DIN, SEQ], BF16), ("scr_xs", [DIN, SEQ], BF16),
        ("scr_bc", [128, SEQ], BF16), ("scr_dt", [32, SEQ], F32),
        ("scr_acum", [32, SEQ], F32), ("scr_ahi", [32, SEQ], BF16),
        ("scr_alo", [32, SEQ], BF16), ("scr_lnd", [32, SEQ], BF16),
        ("scr_rr", [2, 1024], BF16),
    ]:
        dram[nm] = nc.dram_tensor(nm, shp, dt, kind="Internal")

    with ExitStack() as ctx:
        sems = Sems(nc, ctx)
        phase_a1(nc, sems, dram)
        if n_sub >= 1:
            phase_a2(nc, sems, dram)
        if n_sub >= 2:
            phase_ffn(nc, sems, dram, dram["scr_R"], dram["scr_R"], "b")
        if n_sub >= 2.5:
            phase_c1(nc, sems, dram)
        if n_sub >= 3:
            phase_c2(nc, sems, dram)
        if n_sub >= 4:
            phase_ffn(nc, sems, dram, dram["scr_R"], dram["out"], "d")
        else:
            phase_finish(nc, sems, dram)
    return nc


def host_inputs(inputs, core):
    f32 = np.float32
    x = np.ascontiguousarray(inputs["L0_attn_a_x"][core].T).astype(f32, copy=False)
    d = {"xT": x}
    d["wA"] = np.ascontiguousarray(np.concatenate(
        [inputs["L0_attn_a_w_v"], inputs["L0_attn_a_w_wqk"], inputs["L0_attn_a_w_qk"]],
        axis=1)).astype(f32, copy=False)
    bcat = np.concatenate([inputs["L0_attn_a_b_v"], inputs["L0_attn_a_b_wqk"],
                           inputs["L0_attn_a_b_qk"]])
    d["bA"] = np.ascontiguousarray(bcat.reshape(40, 128).T).astype(f32, copy=False)
    d["w_out_a"] = np.ascontiguousarray(inputs["L0_attn_a_w_out"]).astype(f32, copy=False)
    d["b_out_a"] = np.ascontiguousarray(
        inputs["L0_attn_a_b_out"].reshape(8, 128).T).astype(f32, copy=False)
    pe = np.asarray(inputs["L0_attn_a_pos_embedding"])
    g = np.empty(SEQ, dtype=f32)
    g[0] = pe[4094]
    g[1:] = pe[::-1]
    d["gbias"] = np.ascontiguousarray(g.reshape(32, 128).T).astype(f32, copy=False)
    wpe = np.asarray(inputs["L0_attn_a_w_pos_embedding"])
    wb = wpe[63::-1]
    d["wbias"] = np.ascontiguousarray(np.tile(wb, 2).reshape(128, 1)).astype(f32, copy=False)
    d["wF"] = np.ascontiguousarray(np.concatenate(
        [inputs["L0_ffn_W_r"].T, inputs["L0_ffn_W_k"].T, inputs["L0_ffn_W_v"].T],
        axis=1)).astype(f32, copy=False)
    mu = np.concatenate([inputs["L0_ffn_mu_r"], inputs["L0_ffn_mu_k"]])
    d["muF"] = np.ascontiguousarray(mu.reshape(16, 128).T).astype(f32, copy=False)
    d["w_in"] = np.ascontiguousarray(inputs["L0_attn_b_W_in"].T).astype(f32, copy=False)
    d["w_out_b"] = np.ascontiguousarray(inputs["L0_attn_b_W_out"].T).astype(f32, copy=False)
    cwf = np.asarray(inputs["L0_attn_b_conv_w"]).reshape(CONVD, 4)
    d["convw"] = np.ascontiguousarray(
        cwf.reshape(17, 128, 4).transpose(1, 0, 2).reshape(128, 68)).astype(f32, copy=False)
    d["convb"] = np.ascontiguousarray(
        np.asarray(inputs["L0_attn_b_conv_b"]).reshape(17, 128).T).astype(f32, copy=False)
    d["dtb"] = np.ascontiguousarray(
        np.asarray(inputs["L0_attn_b_dt_bias"]).reshape(32, 1)).astype(f32, copy=False)
    d["alog"] = np.ascontiguousarray(
        np.asarray(inputs["L0_attn_b_A_log"]).reshape(32, 1)).astype(f32, copy=False)
    d["Drep"] = np.ascontiguousarray(np.repeat(
        np.asarray(inputs["L0_attn_b_D"]), 64).reshape(16, 128).T).astype(f32, copy=False)
    d["normw"] = np.ascontiguousarray(
        np.asarray(inputs["L0_attn_b_norm_w"]).reshape(16, 128).T).astype(f32, copy=False)
    d["ident"] = np.eye(128, dtype=BF16NP)
    d["ones_col"] = np.ones((128, 1), dtype=BF16NP)
    d["ones_row"] = np.ones((1, 128), dtype=BF16NP)
    ind = np.repeat(np.eye(32, dtype=np.float32), 64, axis=1)
    d["Ind"] = ind.astype(BF16NP)
    d["IndN"] = (-ind).astype(BF16NP)
    ll = np.arange(64)
    mask = np.where(ll[None, :] < np.arange(64)[:, None], np.float32(-1e30), np.float32(0.0))
    d["maskc"] = np.ascontiguousarray(np.tile(mask, (1, 32))).astype(BF16NP)
    mw = np.zeros((128, 128), dtype=np.float32)
    mw[0:64, 0:64] = 1.0
    mw[64:128, 64:128] = 1.0
    d["maskw"] = mw.astype(BF16NP)
    return d


_NC_CACHE = {}


def kernel(_n_sub=4, **inputs):
    if _n_sub not in _NC_CACHE:
        _NC_CACHE[_n_sub] = build_nc(_n_sub)
    nc = _NC_CACHE[_n_sub]
    in_maps = [host_inputs(inputs, core) for core in range(8)]
    res = run_bass_kernel_spmd(nc, in_maps, core_ids=list(range(8)))
    out = np.stack([np.asarray(r["out"]).T for r in res.results], axis=0)
    return np.ascontiguousarray(out.astype(np.float32))
